# version 07
# baseline (speedup 1.0000x reference)
import numpy as np
import ml_dtypes
import concourse.bass as bass
import concourse.mybir as mybir
from concourse.bass_utils import run_bass_kernel_spmd
from concourse.bass_core_rust import IndirectOffsetOnAxis

F32 = mybir.dt.float32
BF16 = mybir.dt.bfloat16
I32 = mybir.dt.int32
U32 = mybir.dt.uint32
AF = mybir.ActivationFunctionType
ALU = mybir.AluOpType
AX = mybir.AxisListType

ENGINES = ("sync", "vector", "scalar", "tensor", "gpsimd")
LN_EPS = 1e-5
CLIP = 8.0
NH = 32
HD = 128
HSC = HD ** -0.5
PBASE = 10240
PSIZE = 196 * 1024


class Cfg:
    def __init__(self, S=4096, D=4096):
        self.S = S
        self.D = D
        self.TOWN = S // 2
        self.QB = max(128, S // 8)
        self.NBLK = S // self.QB
        self.QPOS = ([0, 1, self.NBLK - 2, self.NBLK - 1],
                     list(range(2, self.NBLK - 2)))
        self.DT = D // 128
        self.ST = S // 128
        self.OT = self.TOWN // 128
        self.KP = (D // 2 // 128 + 1) * 128
        self.KT = self.KP // 128


CFG = Cfg()


_CONST_CACHE = {}


def host_constants(cfg):
    key = (cfg.S, cfg.D)
    if key in _CONST_CACHE:
        return _CONST_CACHE[key]
    D = cfg.D
    n = np.arange(D, dtype=np.float64)[:, None]
    k = np.arange(cfg.KP, dtype=np.float64)[None, :]
    ang = 2.0 * np.pi * (n * k) / D
    cosF = np.cos(ang); cosF[:, D // 2 + 1:] = 0.0
    sinF = np.sin(ang); sinF[:, D // 2 + 1:] = 0.0
    kk = np.arange(cfg.KP, dtype=np.float64)[:, None]
    dd = np.arange(D // 2 + 1, dtype=np.float64)[None, :]
    w = np.full((cfg.KP, 1), 2.0); w[0] = 1.0; w[D // 2] = 1.0; w[D // 2 + 1:] = 0.0
    invH = (w * np.cos(2.0 * np.pi * (kk * dd) / D)) / D
    invT = np.zeros((cfg.KP, D))
    invT[:, :D // 2 + 1] = invH
    invT[:, D // 2 + 1:] = invH[:, 1:D // 2][:, ::-1]
    m = np.arange(1, NH + 1, dtype=np.float64) * (8.0 / NH)
    slopes = (1.0 / np.power(2.0, m)).astype(np.float64)
    iota_kt = (np.arange(128)[:, None] + 128.0 * np.arange(cfg.ST)[None, :]).astype(np.float32)
    nmask = cfg.QB // 128
    masks = np.stack([
        np.where(np.arange(128)[:, None] + o * 128 <= np.arange(cfg.QB)[None, :],
                 np.float32(0.0), np.float32(1e30))
        for o in range(nmask)]).astype(np.float32).astype(ml_dtypes.bfloat16)
    rowbase = ((np.arange(128)[:, None] + 128 * np.arange(cfg.OT)[None, :]).astype(np.int64)
               * (2 * D)).astype(np.float32)
    _CONST_CACHE[key] = dict(
        cosF=cosF.astype(ml_dtypes.bfloat16), sinF=sinF.astype(ml_dtypes.bfloat16),
        invT=invT.astype(ml_dtypes.bfloat16), slopes=slopes, iota_kt=iota_kt,
        masks=masks, rowbase=rowbase)
    return _CONST_CACHE[key]


def host_core_inputs(cfg, inputs, core):
    s, role = core // 2, core % 2
    x = np.asarray(inputs["L0_attn_a_hidden_states"], dtype=np.float32)
    xs = np.ascontiguousarray(x[s][:cfg.S, :cfg.D])
    qpos = cfg.QPOS[role]
    xo = np.ascontiguousarray(
        np.concatenate([xs[p * cfg.QB:(p + 1) * cfg.QB] for p in qpos], axis=0))
    posrow = np.concatenate(
        [np.arange(p * cfg.QB, (p + 1) * cfg.QB) for p in qpos]
    ).astype(np.float32)[None, :]
    cst = host_constants(cfg)
    D = cfg.D

    def f32c(a):
        return np.ascontiguousarray(np.asarray(a, dtype=np.float32))

    m = {
        "x_seq": xs,
        "x_own": xo,
        "role": np.array([[role]], dtype=np.int32),
        "posrow": posrow,
        "wqkvT": f32c(np.asarray(inputs["L0_attn_a_Wqkv"]).T),
        "woutT": f32c(np.asarray(inputs["L0_attn_a_Wout"]).T),
        "wbT": f32c(np.asarray(inputs["L0_ffn_W_base"]).T),
        "wrT": f32c(np.asarray(inputs["L0_ffn_W_router"]).T),
        "laT": f32c(np.asarray(inputs["L0_ffn_lora_A"]).reshape(128, D).T),
        "lbC": f32c(np.transpose(np.asarray(inputs["L0_ffn_lora_B"]), (0, 2, 1)).reshape(128, D)),
        "bbase": f32c(np.asarray(inputs["L0_ffn_b_base"])[None, :]),
        "qlnw": f32c(np.asarray(inputs["L0_attn_a_q_ln_w"])[None, :]),
        "qlnb": f32c(np.asarray(inputs["L0_attn_a_q_ln_b"])[None, :]),
        "klnw": f32c(np.asarray(inputs["L0_attn_a_k_ln_w"])[None, :]),
        "klnb": f32c(np.asarray(inputs["L0_attn_a_k_ln_b"])[None, :]),
        "cosFb": cst["cosF"], "sinFb": cst["sinF"], "invTb": cst["invT"],
        "iota_kt": cst["iota_kt"], "rowbase": cst["rowbase"],
    }
    for o in range(cst["masks"].shape[0]):
        m[f"mask{o}"] = np.ascontiguousarray(cst["masks"][o])
    return m


class SemT:
    def __init__(self, handle, name):
        self.h = handle; self.name = name; self.n = 0

    def inc(self, by):
        self.n += by
        return self.n


class InstRec:
    __slots__ = ("meth", "a", "kw", "incs")

    def __init__(self, meth, a, kw):
        self.meth = meth; self.a = a; self.kw = kw; self.incs = []

    def then_inc(self, sem, n=1):
        self.incs.append((sem, n))
        return self


class EngRec:
    def __init__(self, q):
        object.__setattr__(self, "_q", q)

    def __getattr__(self, meth):
        q = self._q

        def f(*a, **kw):
            r = InstRec(meth, a, kw)
            q.append(r)
            return r
        return f

    def wait(self, sem, val=None):
        self.wait_ge(sem.h, sem.n if val is None else val)

    def dma(self, out, in_, sem, **kw):
        self.dma_start(out=out, in_=in_, **kw).then_inc(sem, 16)
        return sem.inc(16)


class Rec:
    def __init__(self):
        self.q = {e: [] for e in ENGINES}
        for e in ENGINES:
            setattr(self, e, EngRec(self.q[e]))


def replay(rec, engine_name, eng, role_reg):
    cur = None
    for r in rec.q[engine_name]:
        if r.meth == "IF_ROLE":
            if callable(role_reg):
                role_reg = role_reg()
            cur = eng.If_eq(role_reg, r.a[0])
            cur.__enter__()
            continue
        if r.meth == "ELSE_ROLE":
            cur.__exit__(None, None, None)
            cur = eng.Else()
            cur.__enter__()
            continue
        if r.meth == "ENDIF_ROLE":
            cur.__exit__(None, None, None)
            cur = None
            eng.end_ifs()
            continue
        inst = getattr(eng, r.meth)(*r.a, **r.kw)
        for (sem, n) in r.incs:
            inst = inst.then_inc(sem.h, n)


class Ctx:
    DMA_SEMS = ("ld", "ld0", "ld1", "ld2", "st0", "st1", "st2", "st3",
                "wl0", "wl1", "rl0", "rl1", "rl2", "rl3",
                "vl0", "vl1", "cs0", "cs1", "gld", "gst")

    def __init__(self, nc, cfg):
        self.nc = nc; self.cfg = cfg
        self.dram = {}; self.sb = {}; self.ps = {}; self.sems = {}
        self.rec = Rec()
        self.marks = {}
        self._guards = []

    def din(self, name, shape, dt=F32):
        self.dram[name] = self.nc.dram_tensor(name, shape, dt, kind="ExternalInput").ap()
        return self.dram[name]

    def dint(self, name, shape, dt=F32):
        self.dram[name] = self.nc.dram_tensor(name, shape, dt, kind="Internal").ap()
        return self.dram[name]

    def dout(self, name, shape, dt=F32):
        self.dram[name] = self.nc.dram_tensor(name, shape, dt, kind="ExternalOutput").ap()
        return self.dram[name]

    def reserve_sbuf(self, nbytes):
        h = self.nc.alloc_sbuf_tensor("manual_region", [128, nbytes], mybir.dt.uint8)
        self._mb = self.nc.lookup_mloc(h).addr
        self._msz = nbytes

    def asb(self, name, shape, dt, offset):
        end = offset + int(np.prod(shape[1:])) * mybir.dt.size(dt)
        assert end <= self._msz, f"sbuf region overflow: {name} end {end}"
        t = self.nc.alloc_sbuf_tensor_at(name, shape, dt, offset=self._mb + offset)
        self.sb[name] = t
        return t

    def layout(self, base, specs):
        out = {}
        off = base
        for (name, shape, dt) in specs:
            out[name] = self.asb(name, shape, dt, off)
            off += (int(np.prod(shape[1:])) * mybir.dt.size(dt) + 63) & ~63
        assert off <= self._msz, f"layout overflow at {off}"
        out["_end"] = off
        return out

    def apsum(self, name, shape, dt=F32):
        self._psn = getattr(self, "_psn", 0) + 1
        g = self.nc.psum_tensor(f"{name}_u{self._psn}", shape, dt)
        t = g.__enter__()
        self._psum_guards = getattr(self, "_psum_guards", [])
        self._psum_guards.append((name, g))
        self.ps[name] = t
        return t

    def psum_free_all(self):
        for name, g in reversed(getattr(self, "_psum_guards", [])):
            g.__exit__(None, None, None)
            self.ps.pop(name, None)
        self._psum_guards = []

    def psum_get(self, name, shape, dt=F32):
        if name not in self.ps:
            self.apsum(name, shape, dt)
        return self.ps[name]

    def sem_snapshot(self):
        return {k: s.n for k, s in self.sems.items()}

    def sem_restore(self, snap):
        for k, v in snap.items():
            self.sems[k].n = v

    def sem(self, name):
        g = self.nc.semaphore(name)
        h = g.__enter__()
        self._guards.append(g)
        self.sems[name] = SemT(h, name)
        return self.sems[name]

    def barrier(self, engines=ENGINES):
        r = self.rec
        for e in engines:
            for s in self.DMA_SEMS:
                if s in self.sems and self.sems[s].n:
                    getattr(r, e).wait_ge(self.sems[s].h, self.sems[s].n)
        for e in engines:
            sT = self.sems[f"b_{e}"]
            eng = getattr(r, e)
            if e in ("tensor", "sync"):
                eng.nop().then_inc(sT, 1)
            else:
                eng.drain().then_inc(sT, 1)
            sT.inc(1)
        for e in engines:
            for o in engines:
                so = self.sems[f"b_{o}"]
                getattr(r, e).wait_ge(so.h, so.n)


def emit_prelude(ctx):
    r = ctx.rec
    ident = ctx.sb["ident"]
    ones = ctx.sb["onescol"]
    r.gpsimd.memset(ident[:], 1.0)
    r.gpsimd.drain()
    r.gpsimd.affine_select(ident[:], ident[:], pattern=[[1, 128]],
                           compare_op=ALU.is_equal, fill=0.0,
                           base=0, channel_multiplier=-1)
    r.gpsimd.drain().then_inc(ctx.sems["gp"], 1)
    ctx.sems["gp"].inc(1)
    r.vector.memset(ones[:], 1.0)
    r.vector.memset(ctx.sb["onesrow"][:], 1.0)
    r.vector.memset(ctx.sb["onesrowf"][:], 1.0)
    r.vector.drain().then_inc(ctx.sems["ve"], 1)
    ctx.sems["ve"].inc(1)
    ctx.barrier()


def emit_cast_weights(ctx, jobs):
    r = ctx.rec
    wc = ctx.sb["wcf"]; wb = ctx.sb["wcb"]
    ld = [ctx.sems["wl0"], ctx.sems["wl1"]]
    st = [ctx.sems["cs0"], ctx.sems["cs1"]]
    gp = ctx.sems["gp"]
    i = 0
    for (src, dst, rows, cols) in jobs:
        for r0 in range(0, rows, 128):
            for c0 in range(0, cols, 2048):
                cw = min(2048, cols - c0)
                b = i % 2
                if i >= 2:
                    r.scalar.wait(*ctx.marks[("wc", i - 2)])
                r.scalar.dma(wc[b][:, 0:cw],
                             ctx.dram[src][r0:r0 + 128, c0:c0 + cw], ld[b])
                if i % 2 == 0:
                    r.gpsimd.wait(ld[b])
                    if i >= 2:
                        r.gpsimd.wait(*ctx.marks[("wcb", i - 2)])
                    r.gpsimd.tensor_copy(wb[b][:, 0:cw], wc[b][:, 0:cw]) \
                        .then_inc(gp, 1)
                    gp.inc(1)
                else:
                    r.scalar.wait(ld[b])
                    if i >= 2:
                        r.scalar.wait(*ctx.marks[("wcb", i - 2)])
                    r.scalar.activation(wb[b][:, 0:cw], wc[b][:, 0:cw],
                                        AF.Copy, bias=0.0, scale=1.0) \
                        .then_inc(gp, 1)
                    gp.inc(1)
                ctx.marks[("wc", i)] = (gp, gp.n)
                r.scalar.wait(gp)
                r.scalar.dma(ctx.dram[dst][r0:r0 + 128, c0:c0 + cw],
                             wb[b][:, 0:cw], st[b])
                ctx.marks[("wcb", i)] = (st[b], st[b].n)
                i += 1


def emit_row_broadcast(ctx, jobs):
    r = ctx.rec
    rowf = ctx.sb["rbf"]; rowb = ctx.sb["rbb"]
    pbc = ctx.ps["rbp"]
    ld = ctx.sems["ld"]
    ve, mm = ctx.sems["ve"], ctx.sems["mm"]
    ones = ctx.sb["onesrow"]
    for (src, dstn, width, scale) in jobs:
        dst = ctx.sb[dstn]
        r.sync.wait(ve, ve.n)
        r.sync.dma(rowf[0:1, 0:width], ctx.dram[src], ld)
        r.vector.wait(ld)
        if scale != 1.0:
            r.vector.tensor_scalar_mul(rowf[0:1, 0:width], rowf[0:1, 0:width], scale)
            r.vector.drain()
        r.vector.tensor_copy(rowb[0:1, 0:width], rowf[0:1, 0:width])
        r.vector.drain().then_inc(ve, 1); ve.inc(1)
        r.tensor.wait(ve)
        for c0 in range(0, width, 512):
            cw = min(512, width - c0)
            pb = (c0 // 512) % 2
            if c0 >= 1024:
                r.tensor.wait(*ctx.marks[("rb", dstn, c0 - 1024)])
            r.tensor.matmul(pbc[pb][:, 0:cw], ones[0:1, :],
                            rowb[0:1, c0:c0 + cw], start=True, stop=True) \
                .then_inc(mm, 1)
            mm.inc(1)
            r.vector.wait(mm)
            r.vector.tensor_copy(dst[:, c0:c0 + cw], pbc[pb][:, 0:cw]) \
                .then_inc(ve, 1)
            ve.inc(1)
            ctx.marks[("rb", dstn, c0)] = (ve, ve.n)


def emit_ln_transpose(ctx, src, dst, ntiles, tag, src_dt=F32, gb=None,
                      extra_out=None, eps=LN_EPS, store_q="scalar"):
    cfg, r = ctx.cfg, ctx.rec
    D, DT = cfg.D, cfg.DT
    if src_dt == F32:
        xt = [ctx.sb[f"{tag}xt0"], ctx.sb[f"{tag}xt1"]]
    else:
        xt = [ctx.sb[f"{tag}xtb0"], ctx.sb[f"{tag}xtb1"]]
    tmp = ctx.sb[f"{tag}tmp"]
    bn = ctx.sb[f"{tag}bn"]; mv = ctx.sb[f"{tag}mv"]; rstd = ctx.sb[f"{tag}rs"]
    xn = [ctx.sb[f"{tag}xn0"], ctx.sb[f"{tag}xn1"]]
    fm = [ctx.sb[f"{tag}fm0"], ctx.sb[f"{tag}fm1"]]
    ptr = [ctx.psum_get("ptr0", [128, 128], BF16),
           ctx.psum_get("ptr1", [128, 128], BF16)]
    ident = ctx.sb["ident"]
    ld = [ctx.sems["ld0"], ctx.sems["ld1"]]
    st = [ctx.sems["st0"], ctx.sems["st1"]]
    ve, sc, mm = (ctx.sems[k] for k in ("ve", "sc", "mm"))
    M = ctx.marks

    for i in range(ntiles):
        b = i % 2
        if i >= 2:
            r.sync.wait(*M[(tag, i - 2, "xtfree")])
        r.sync.dma(xt[b][:], ctx.dram[src][i * 128:(i + 1) * 128, :], ld[b])
        ldmark = ld[b].n
        r.vector.wait(ld[b], ldmark)
        for c in range(D // 512):
            r.vector.bn_stats(bn[:, 6 * c:6 * c + 6], xt[b][:, c * 512:(c + 1) * 512])
        r.vector.drain()
        r.vector.bn_aggr(mv[:], bn[:])
        r.vector.drain()
        r.vector.tensor_scalar_add(mv[:, 1:2], mv[:, 1:2], eps)
        r.vector.drain().then_inc(ve, 1); ve.inc(1)
        r.scalar.wait(ve)
        r.scalar.activation(rstd[:], mv[:, 1:2], AF.Ln, bias=0.0, scale=1.0)
        r.scalar.drain()
        r.scalar.activation(rstd[:], rstd[:], AF.Exp, bias=0.0, scale=-0.5)
        r.scalar.drain().then_inc(sc, 1); sc.inc(1)
        r.vector.wait(sc)
        if i >= 2:
            r.vector.wait(*M[(tag, i - 2, "xnfree")])
        if gb is None:
            r.vector.tensor_scalar(xn[b][:], xt[b][:], mv[:, 0:1], rstd[:, 0:1],
                                   op0=ALU.subtract, op1=ALU.mult)
        else:
            gbc, bbc = ctx.sb[gb[0]], ctx.sb[gb[1]]
            r.vector.tensor_scalar(tmp[:], xt[b][:], mv[:, 0:1], rstd[:, 0:1],
                                   op0=ALU.subtract, op1=ALU.mult) \
                .then_inc(ve, 1)
            ve.inc(1)
            r.gpsimd.wait(ve)
            r.gpsimd.tensor_mul(tmp[:], tmp[:], gbc[:])
            r.gpsimd.drain()
            r.gpsimd.tensor_add(xn[b][:], tmp[:], bbc[:])
            r.gpsimd.drain().then_inc(ctx.sems["gp"], 1)
            ctx.sems["gp"].inc(1)
            r.vector.wait(ctx.sems["gp"])
        r.vector.drain().then_inc(ve, 1); ve.inc(1)
        M[(tag, i, "xtfree")] = (ve, ve.n)
        if extra_out is not None:
            r.sync.wait(ve)
            r.sync.dma(ctx.dram[extra_out][i * 128:(i + 1) * 128, 0:D], xn[b][:], ld[b])
            r.scalar.wait(ve)
            vls = ctx.sems[f"vl{b}"]
            r.scalar.dma(ctx.dram[extra_out][i * 128:(i + 1) * 128, D:2 * D], xn[b][:],
                         vls)
            M[(tag, i, "xnextra")] = (vls, vls.n)
        if store_q != "scalar":
            pass
        r.tensor.wait(ve)
        for j in range(DT):
            pb = j % 2
            if j >= 2:
                r.tensor.wait(*M[(tag, i, "evac", j - 2)])

            r.tensor.transpose(ptr[pb][:], xn[b][:, j * 128:(j + 1) * 128],
                               ident[:]).then_inc(mm, 1)
            mm.inc(1)
            r.scalar.wait(mm)
            if j == 0 and i >= 2:
                r.scalar.wait(*M[(tag, i - 2, "stored")])
            r.scalar.activation(fm[b][:, j * 128:(j + 1) * 128], ptr[pb][:],
                                AF.Copy, bias=0.0, scale=1.0).then_inc(sc, 1)
            sc.inc(1)
            M[(tag, i, "evac", j)] = (sc, sc.n)
        if extra_out is not None:
            M[(tag, i, "xnfree")] = M[(tag, i, "xnextra")]
        else:
            M[(tag, i, "xnfree")] = (mm, mm.n)
        getattr(r, store_q).wait(sc)
        ng = max(1, DT // 2)
        for g in range(ng):
            nj = min(2, DT - g * 2)
            sbap = fm[b][:, g * 2 * 128:(g * 2 + nj) * 128] \
                .rearrange("p (j c) -> p j c", c=128)
            dap = ctx.dram[dst][g * 2 * 128:(g * 2 + nj) * 128,
                                i * 128:(i + 1) * 128] \
                .rearrange("(j p) c -> p j c", p=128)
            getattr(r, store_q).dma(dap, sbap, st[b])
        M[(tag, i, "stored")] = (st[b], st[b].n)


def emit_proj_tm(ctx, tag, src_fm, T, wname, col0, ncols, grp, post,
                 tcw=512, extra_mm=None):
    cfg, r = ctx.cfg, ctx.rec
    D, DT = cfg.D, cfg.DT
    w_sb = [ctx.sb["pjw0"], ctx.sb["pjw1"]]
    xnb = [ctx.sb["pjx0"], ctx.sb["pjx1"]]
    ld = [ctx.sems["ld0"], ctx.sems["ld1"]]
    wl = [ctx.sems["wl0"], ctx.sems["wl1"]]
    mm = ctx.sems["mm"]
    M = ctx.marks
    grp = 512
    nGRP = (ncols + grp - 1) // grp
    nTC = (T + tcw - 1) // tcw
    psums = [ctx.ps["pj0"], ctx.ps["pj1"], ctx.ps["pj2"], ctx.ps["pj3"]]
    pcnt = 0

    for g in range(nGRP):
        gw = min(grp, ncols - g * grp)
        wb = g % 2
        if ("pjwfree", wb) in M:
            r.sync.wait(*M[("pjwfree", wb)])
        for jg in range(0, DT, 4):
            nj = min(4, DT - jg)
            dap = ctx.dram[wname][jg * 128:(jg + nj) * 128,
                                  col0 + g * grp: col0 + g * grp + gw] \
                .rearrange("(j p) c -> p j c", p=128)
            sap = w_sb[wb][:, jg * gw:(jg + nj) * gw] \
                .rearrange("p (j c) -> p j c", c=gw)
            r.sync.dma(sap, dap, wl[wb])
        wmark = wl[wb].n
        for tc in range(nTC):
            b = tc % 2
            if ("pjxfree", b) in M:
                r.sync.wait(*M[("pjxfree", b)])
                r.vector.wait(*M[("pjxfree", b)])
            for jg in range(0, DT, 2):
                nj = min(2, DT - jg)
                dap = ctx.dram[src_fm][jg * 128:(jg + nj) * 128,
                                       tc * tcw:tc * tcw + tcw] \
                    .rearrange("(j p) c -> p j c", p=128)
                sap = xnb[b][:, jg * tcw:(jg + nj) * tcw] \
                    .rearrange("p (j c) -> p j c", c=tcw)
                r.sync.dma(sap, dap, ld[b])
            xmark = ld[b].n
            ntt = tcw // 128
            for tti in range(ntt):
                tt = tc * ntt + tti
                r.tensor.wait(ld[b], xmark)
                if tc == 0 and tti == 0:
                    r.tensor.wait(wl[wb], wmark)
                pb = pcnt % 4
                pcnt += 1
                if ("pjpfree", pb) in M:
                    r.tensor.wait(*M[("pjpfree", pb)])
                last = DT - 1 if extra_mm is None else DT
                for j in range(DT):
                    lhs = xnb[b][:, j * tcw + tti * 128:
                                 j * tcw + (tti + 1) * 128]
                    rhs = w_sb[wb][:, j * gw: j * gw + gw]
                    mmi = r.tensor.matmul(psums[pb][:, 0:gw], lhs, rhs,
                                          start=(j == 0), stop=(j == last))
                if extra_mm is not None:
                    mmi = extra_mm(tt, g * grp, psums[pb][:, 0:gw], gw)
                mmi.then_inc(mm, 1)
                mm.inc(1)
                M[("pjpfree", pb)] = post(tt, g * grp, psums[pb][:, 0:gw], gw)
            M[("pjxfree", b)] = (mm, mm.n)
        M[("pjwfree", wb)] = (mm, mm.n)


def build_program(cfg, phases=("all",)):
    nc = bass.Bass("TRN2", target_bir_lowering=False, num_devices=8)
    D, S, TOWN = cfg.D, cfg.S, cfg.TOWN
    ctx = Ctx(nc, cfg)

    ctx.din("x_seq", [S, D]); ctx.din("x_own", [TOWN, D])
    ctx.din("role", [1, 1], I32); ctx.din("posrow", [1, TOWN])
    ctx.din("wqkvT", [D, 3 * D]); ctx.din("woutT", [D, D]); ctx.din("wbT", [D, D])
    ctx.din("wrT", [D, 8]); ctx.din("laT", [D, 128]); ctx.din("lbC", [128, D])
    ctx.din("bbase", [1, D])
    for nm in ("qlnw", "qlnb", "klnw", "klnb"):
        ctx.din(nm, [1, D])
    ctx.din("cosFb", [D, cfg.KP], BF16); ctx.din("sinFb", [D, cfg.KP], BF16)
    ctx.din("invTb", [cfg.KP, D], BF16)
    ctx.din("iota_kt", [128, cfg.ST])
    ctx.din("rowbase", [128, cfg.OT], F32)
    for o in range(cfg.QB // 128):
        ctx.din(f"mask{o}", [128, cfg.QB], BF16)

    ctx.dint("xnfm", [D, S], BF16)
    ctx.dint("xnofm", [D, TOWN], BF16)
    ctx.dint("ktm", [S, D], BF16)
    ctx.dint("vtm", [S, D], BF16)
    ctx.dint("qtm", [TOWN, D], BF16)
    ctx.dint("kfm", [D, S], BF16)
    ctx.dint("qfm", [D, TOWN], BF16)
    ctx.dint("ofm", [D, TOWN], BF16)
    ctx.dint("xr1", [TOWN, D]); ctx.dint("xr2", [TOWN, D]); ctx.dint("xr3", [TOWN, D])
    ctx.dint("vdbl", [TOWN, 2 * D], BF16)
    ctx.dint("vfm", [D, TOWN], BF16)
    ctx.dint("corrh", [TOWN, D], BF16)
    ctx.dint("Ph", [cfg.KP, TOWN], BF16)
    ctx.dint("wqkvTb", [D, 3 * D], BF16)
    ctx.dint("woutTb", [D, D], BF16)
    ctx.dint("wbTb", [D, D], BF16)
    ctx.dint("laTb", [D, 128], BF16)
    ctx.dint("wrTb", [D, 8], BF16)
    ctx.dint("lbCb", [128, D], BF16)

    ctx.dout("y", [TOWN, D])

    for nm in ("ld", "ld0", "ld1", "ld2", "st0", "st1", "wl0", "wl1",
               "rl0", "rl1", "rl2", "rl3", "st2", "st3",
               "vl0", "vl1", "cs0", "cs1", "ve", "sc", "mm", "gp",
               "gld", "gst",
               "b_sync", "b_vector", "b_scalar", "b_tensor", "b_gpsimd"):
        ctx.sem(nm)

    ctx.reserve_sbuf(PSIZE)
    ctx.asb("ident", [128, 128], BF16, 0)
    ctx.asb("onescol", [128, 1], BF16, 256)
    ctx.asb("onesrow", [1, 128], BF16, 288)
    ctx.asb("onesrowf", [1, 128], F32, 8832)
    ctx.asb("iotasb", [128, max(cfg.ST, 2)], F32, 9344)
    ctx.asb("rolesb", [1, 1], I32, 576)
    ctx.asb("posrsb", [1, max(D, 2048)], F32, 640)
    ctx.asb("rbf", [1, D], F32, PBASE)
    ctx.asb("rbb", [1, D], BF16, PBASE + D * 4)

    GBB = PSIZE - 4 * (D * 2 + 64) - 512
    ctx.asb("qgbc", [128, D], BF16, GBB)
    ctx.asb("qbbc", [128, D], BF16, GBB + (D * 2 + 64))
    ctx.asb("kgbc", [128, D], BF16, GBB + 2 * (D * 2 + 64))
    ctx.asb("kbbc", [128, D], BF16, GBB + 3 * (D * 2 + 64))

    L = ctx.layout(PBASE, [
        ("lnxt0", [128, D], F32), ("lnxt1", [128, D], F32),
        ("lntmp", [128, D], F32),
        ("lnxn0", [128, D], BF16), ("lnxn1", [128, D], BF16),
        ("lnfm0", [128, D], BF16), ("lnfm1", [128, D], BF16),
        ("lnxtb0", [128, D], BF16), ("lnxtb1", [128, D], BF16),
        ("lnbn", [128, 6 * (D // 512)], F32),
        ("lnmv", [128, 2], F32), ("lnrs", [128, 1], F32),
    ])
    LN_END = L["_end"]
    C = ctx.layout(LN_END, [
        ("wcf0", [128, 2048], F32), ("wcf1", [128, 2048], F32),
        ("wcb0", [128, 2048], BF16), ("wcb1", [128, 2048], BF16),
    ])
    ctx.sb["wcf"] = [ctx.sb["wcf0"], ctx.sb["wcf1"]]
    ctx.sb["wcb"] = [ctx.sb["wcb0"], ctx.sb["wcb1"]]
    assert C["_end"] <= GBB

    GRP = 512
    P = ctx.layout(PBASE, [
        ("pjw0", [128, (D // 128) * GRP], BF16),
        ("pjw1", [128, (D // 128) * GRP], BF16),
        ("pjx0", [128, (D // 128) * 512], BF16),
        ("pjx1", [128, (D // 128) * 512], BF16),
        ("pjo0", [128, 512], BF16), ("pjo1", [128, 512], BF16),
        ("pjo2", [128, 512], BF16), ("pjo3", [128, 512], BF16),
        ("pjr0", [128, 512], F32), ("pjr1", [128, 512], F32),
        ("pjq0", [128, 512], F32), ("pjq1", [128, 512], F32),
        ("pjr2", [128, 512], F32), ("pjr3", [128, 512], F32),
        ("pjq2", [128, 512], F32), ("pjq3", [128, 512], F32),
    ])
    assert P["_end"] <= GBB and L["_end"] <= GBB
    FB = ctx.layout(P["_end"], [
        ("bbb", [128, D], F32),
        ("flb", [128, D], BF16),
        ("fla", [128, D], BF16),
        ("fwr", [128, cfg.DT * 8], BF16),
        ("frts", [128, 16], F32),
        ("fe8", [128, 8], F32),
        ("fp8", [128, 16], F32),
        ("fw8", [128, 8], F32),
        ("fhs", [128, 128], F32),
        ("fgb", [128, 128], BF16),
        ("fgfm", [128, TOWN], BF16),
    ])
    assert FB["_end"] <= PSIZE
    ABF = ctx.layout(PBASE, [
        ("abc0", [128, 256], BF16), ("abc1", [128, 256], BF16),
        ("absq1", [128, TOWN], F32), ("absq2", [128, TOWN], F32),
        ("abps0", [128, TOWN], BF16), ("abps1", [128, TOWN], BF16),
        ("abvr", [128, cfg.DT * TOWN], BF16),
    ])
    assert ABF["_end"] <= PSIZE
    ABI = ctx.layout(PBASE, [
        ("abpt0", [128, 128], BF16), ("abpt1", [128, 128], BF16),
        ("abcs", [128, 2 * min(2048, D)], BF16),
        ("abinv", [128, cfg.KT * min(2048, D)], BF16),
    ])
    assert ABI["_end"] <= PSIZE
    AG = ctx.layout(PBASE, [
        ("abct0", [128, D], BF16), ("abct1", [128, D], BF16),
        ("abt8", [128, 24], F32),
        ("abi8", [128, 8], U32),
        ("abw8", [128, 8], F32),
        ("aboff", [128, 8], U32),
        ("abrb", [128, cfg.OT], F32),
        ("abof", [128, 8], F32),
        ("abacc", [128, D], F32),
        ("abg0", [128, D], BF16), ("abg1", [128, D], BF16),
    ])
    assert AG["_end"] <= PSIZE
    A = ctx.layout(PBASE, [
        ("atK0", [128, S], BF16), ("atK1", [128, S], BF16),
        ("atK2", [128, S], BF16),
        ("atV0", [128, S], BF16), ("atV1", [128, S], BF16),
        ("atV2", [128, S], BF16),
        ("atQ0", [128, TOWN], BF16), ("atQ1", [128, TOWN], BF16),
        ("atQ2", [128, TOWN], BF16),
        ("atsq0", [128, S + TOWN], BF16), ("atsq1", [128, S + TOWN], BF16),
        ("atp0", [128, cfg.QB], BF16), ("atp1", [128, cfg.QB], BF16),
        ("atp2", [128, cfg.QB], BF16),
        ("atncb0", [1, TOWN], BF16), ("atncb1", [1, TOWN], BF16),
        ("atncb2", [1, TOWN], BF16),
        ("atbt0", [128, 4 * cfg.ST], F32), ("atbt1", [128, 4 * cfg.ST], F32),
        ("atbt2", [128, 4 * cfg.ST], F32),
        ("atnrm", [1, S + TOWN + 4], F32),
        ("atbnd", [128, 3], F32),
        ("atsd0", [128, cfg.QB], F32), ("atsd1", [128, cfg.QB], F32),
        ("atsd2", [128, cfg.QB], F32),
        ("atlsb0", [1, cfg.QB], F32), ("atlsb1", [1, cfg.QB], F32),
        ("atlbc", [128, cfg.QB], F32),
        ("atob0", [128, cfg.QB], BF16), ("atob1", [128, cfg.QB], BF16),
    ] + [(f"msk{o}", [128, cfg.QB], BF16) for o in range(cfg.QB // 128)])
    assert A["_end"] <= GBB

    ctx.apsum("ptr0", [128, 128], BF16)
    ctx.apsum("ptr1", [128, 128], BF16)
    ctx.apsum("rbp0", [128, 512], F32)
    ctx.apsum("rbp1", [128, 512], F32)
    ctx.ps["rbp"] = [ctx.ps["rbp0"], ctx.ps["rbp1"]]
    for k in range(4):
        ctx.apsum(f"pj{k}", [128, 512], F32)

    r = ctx.rec
    r.sync.dma(ctx.sb["rolesb"][:], ctx.dram["role"], ctx.sems["ld"])
    emit_prelude(ctx)

    allp = "all" in phases

    if allp or "cast" in phases:
        emit_row_broadcast(ctx, [
            ("qlnw", "qgbc", D, HSC), ("qlnb", "qbbc", D, HSC),
            ("klnw", "kgbc", D, 1.0), ("klnb", "kbbc", D, 1.0),
        ])
        ctx.barrier()
        emit_cast_weights(ctx, [
            ("wqkvT", "wqkvTb", D, 3 * D),
            ("woutT", "woutTb", D, D),
            ("wbT", "wbTb", D, D),
            ("laT", "laTb", D, 128),
            ("wrT", "wrTb", D, 8),
            ("lbC", "lbCb", 128, D),
        ])

    if allp or "ln1" in phases:
        emit_ln_transpose(ctx, "x_seq", "xnfm", cfg.ST, "ln", store_q="sync")
        ctx.barrier()
        emit_ln_transpose(ctx, "x_own", "xnofm", cfg.OT, "ln")
        ctx.barrier()

    if allp or "qkv" in phases:
        emit_qkv(ctx)
        ctx.barrier()
        emit_ln_transpose(ctx, "ktm", "kfm", cfg.ST, "ln", src_dt=BF16,
                          gb=("kgbc", "kbbc"))
        ctx.barrier()
        emit_ln_transpose(ctx, "qtm", "qfm", cfg.OT, "ln", src_dt=BF16,
                          gb=("qgbc", "qbbc"))
        ctx.barrier()

    if allp or "attn" in phases:
        emit_attention(ctx)
        ctx.barrier()

    if allp or "wout" in phases:
        ctx.psum_free_all()
        for k in range(4):
            ctx.apsum(f"pj{k}", [128, 512], F32)
        emit_wout(ctx)
        ctx.barrier()

    if allp or "ffn1" in phases:
        emit_ffn(ctx, "xr1", "xr2", "f1")
        ctx.barrier()

    if allp or "attnb" in phases:
        emit_attnb(ctx)
        ctx.barrier()

    if allp or "ffn2" in phases:
        emit_ffn(ctx, "xr3", "y", "f2")
        ctx.barrier()

    with nc.Block() as block:
        @block.sync
        def _(eng):
            _replay_role(ctx, "sync", eng)

        @block.vector
        def _(eng):
            _replay_role(ctx, "vector", eng)

        @block.scalar
        def _(eng):
            _replay_role(ctx, "scalar", eng)

        @block.tensor
        def _(eng):
            _replay_role(ctx, "tensor", eng)

        @block.gpsimd
        def _(eng):
            _replay_role(ctx, "gpsimd", eng)
    return nc, ctx


def emit_qkv(ctx):
    cfg, r = ctx.cfg, ctx.rec
    D, S, TOWN = cfg.D, cfg.S, cfg.TOWN
    ve = ctx.sems["ve"]
    st = [ctx.sems["st0"], ctx.sems["st1"], ctx.sems["st2"], ctx.sems["st3"]]
    ob = [ctx.sb["pjo0"], ctx.sb["pjo1"], ctx.sb["pjo2"], ctx.sb["pjo3"]]
    M = ctx.marks

    def mk_post(dst_name):
        cnt = [0]

        def post(tt, oc_abs, pap, ow):
            b = cnt[0] % 4
            cnt[0] += 1
            r.vector.wait(ctx.sems["mm"])
            if (dst_name, "of", b) in M:
                r.vector.wait(*M[(dst_name, "of", b)])
            r.vector.tensor_scalar(ob[b][:, 0:ow], pap, -CLIP, CLIP,
                                   op0=ALU.max, op1=ALU.min) \
                .then_inc(ve, 1)
            ve.inc(1)
            pfree = (ve, ve.n)
            r.scalar.wait(ve)
            dst = ctx.dram[dst_name]
            r.scalar.dma(dst[tt * 128:(tt + 1) * 128, oc_abs:oc_abs + ow],
                         ob[b][:, 0:ow], st[b])
            M[(dst_name, "of", b)] = (st[b], st[b].n)
            return pfree
        return post

    emit_proj_tm(ctx, "k", "xnfm", S, "wqkvTb", D, D, 1024, mk_post("ktm"))
    ctx.barrier()
    emit_proj_tm(ctx, "v", "xnfm", S, "wqkvTb", 2 * D, D, 1024, mk_post("vtm"))
    ctx.barrier()
    emit_proj_tm(ctx, "q", "xnofm", TOWN, "wqkvTb", 0, D, 1024, mk_post("qtm"))


ROLE_ENGINES = ("sync", "vector", "scalar", "tensor")


def emit_attention(ctx):
    cfg, r = ctx.cfg, ctx.rec
    ctx.psum_free_all()
    for nm2 in ("atS0", "atS1", "atS2", "atO0", "atO1", "atLB"):
        ctx.apsum(nm2, [128, cfg.QB], F32)
    ctx.apsum("atL0", [1, cfg.QB], F32)
    ctx.apsum("atL1", [1, cfg.QB], F32)

    ld = ctx.sems["ld"]
    for o in range(cfg.QB // 128):
        r.sync.dma(ctx.sb[f"msk{o}"][:], ctx.dram[f"mask{o}"], ld)
    r.sync.dma(ctx.sb["iotasb"][:], ctx.dram["iota_kt"], ld)
    r.sync.dma(ctx.sb["posrsb"][0:1, 0:cfg.TOWN], ctx.dram["posrow"], ld)
    ldv = ld.n
    for e in ROLE_ENGINES:
        getattr(r, e).wait_ge(ld.h, ldv)

    cst = host_constants(cfg)
    for e in ROLE_ENGINES:
        getattr(r, e).IF_ROLE(0)
    snap = ctx.sem_snapshot()
    msnap = dict(ctx.marks)
    _emit_attn_role(ctx, 0, cst)
    end0 = ctx.sem_snapshot()
    ctx.sem_restore(snap)
    ctx.marks = msnap
    for e in ROLE_ENGINES:
        getattr(r, e).ELSE_ROLE()
    _emit_attn_role(ctx, 1, cst)
    end1 = ctx.sem_snapshot()
    assert end0 == end1, f"role sem mismatch: { {k: (end0[k], end1[k]) for k in end0 if end0[k] != end1[k]} }"
    for e in ROLE_ENGINES:
        getattr(r, e).ENDIF_ROLE()


def _emit_attn_role(ctx, role, cst):
    cfg, r = ctx.cfg, ctx.rec
    D, S, TOWN, QB = cfg.D, cfg.S, cfg.TOWN, cfg.QB
    QBT = QB // 128
    qpos = cfg.QPOS[role]
    M = ctx.marks
    ve, sc, mm = ctx.sems["ve"], ctx.sems["sc"], ctx.sems["mm"]
    ld = [ctx.sems["ld0"], ctx.sems["ld1"], ctx.sems["ld2"]]
    st = [ctx.sems["st0"], ctx.sems["st1"]]
    sp = [ctx.ps["atS0"], ctx.ps["atS1"], ctx.ps["atS2"]]
    op = [ctx.ps["atO0"], ctx.ps["atO1"]]
    lp = [ctx.ps["atL0"], ctx.ps["atL1"]]
    lb = ctx.ps["atLB"]
    onesrow = ctx.sb["onesrow"]; onescol = ctx.sb["onescol"]
    onesrowf = ctx.sb["onesrowf"]
    Ksb = [ctx.sb["atK0"], ctx.sb["atK1"], ctx.sb["atK2"]]
    Vsb = [ctx.sb["atV0"], ctx.sb["atV1"], ctx.sb["atV2"]]
    Qsb = [ctx.sb["atQ0"], ctx.sb["atQ1"], ctx.sb["atQ2"]]
    sq = [ctx.sb["atsq0"], ctx.sb["atsq1"]]
    pT = [ctx.sb["atp0"], ctx.sb["atp1"], ctx.sb["atp2"]]
    sd = [ctx.sb["atsd0"], ctx.sb["atsd1"], ctx.sb["atsd2"]]
    negcb = [ctx.sb["atncb0"], ctx.sb["atncb1"], ctx.sb["atncb2"]]
    biast = [ctx.sb["atbt0"], ctx.sb["atbt1"], ctx.sb["atbt2"]]
    nrm = ctx.sb["atnrm"]
    lsb = [ctx.sb["atlsb0"], ctx.sb["atlsb1"]]
    lbc = ctx.sb["atlbc"]
    osb = [ctx.sb["atob0"], ctx.sb["atob1"]]
    skipk1 = [float(cst["slopes"][h]) * (QB - 1) <= 20.0 for h in range(NH)]

    def np_norms(h, hb, hld):
        sqb = h % 2
        slope = float(cst["slopes"][h])
        r.scalar.wait(ld[hb], hld)
        if (("at", sqb, "sqfree") in M):
            r.scalar.wait(*M[("at", sqb, "sqfree")])
        r.scalar.activation(sq[sqb][:, 0:S], Ksb[hb][:, 0:S], AF.Square,
                            bias=0.0, scale=1.0)
        r.scalar.drain()
        r.scalar.activation(sq[sqb][:, S:S + TOWN], Qsb[hb][:, 0:TOWN],
                            AF.Square, bias=0.0, scale=1.0)
        r.scalar.drain().then_inc(sc, 1); sc.inc(1)
        r.tensor.wait(sc)
        np_ = ctx.ps["atLB"]
        CW = min(512, QB)
        nch = (S + TOWN) // CW
        for c in range(nch):
            if ("at", "lbfree") in M:
                r.tensor.wait(*M[("at", "lbfree")])
            r.tensor.matmul(np_[0:1, 0:CW], onescol[:, 0:1],
                            sq[sqb][:, c * CW:(c + 1) * CW],
                            start=True, stop=True).then_inc(mm, 1)
            mm.inc(1)
            r.vector.wait(mm)
            r.vector.tensor_copy(nrm[0:1, c * CW:(c + 1) * CW],
                                 np_[0:1, 0:CW]).then_inc(ve, 1)
            ve.inc(1)
            M[("at", "lbfree")] = (ve, ve.n)
        M[("at", sqb, "sqfree")] = (mm, mm.n)
        r.vector.drain()
        r.vector.reduce_max(nrm[0:1, S + TOWN:S + TOWN + 1], nrm[0:1, 0:S],
                            axis=AX.X)
        r.vector.drain()
        r.vector.reduce_max(nrm[0:1, S + TOWN + 1:S + TOWN + 2],
                            nrm[0:1, S:S + TOWN], axis=AX.X)
        r.vector.drain().then_inc(ve, 1); ve.inc(1)
        r.scalar.wait(ve)
        r.scalar.activation(nrm[0:1, S + TOWN:S + TOWN + 2],
                            nrm[0:1, S + TOWN:S + TOWN + 2], AF.Ln,
                            bias=0.0, scale=1.0)
        r.scalar.drain().then_inc(sc, 1); sc.inc(1)
        r.vector.wait(sc)
        r.vector.reduce_sum(nrm[0:1, S + TOWN + 2:S + TOWN + 3],
                            nrm[0:1, S + TOWN:S + TOWN + 2], axis=AX.X)
        r.vector.drain().then_inc(ve, 1); ve.inc(1)
        r.scalar.wait(ve)
        r.scalar.activation(nrm[0:1, S + TOWN + 3:S + TOWN + 4],
                            nrm[0:1, S + TOWN + 2:S + TOWN + 3], AF.Exp,
                            bias=0.0, scale=0.5)
        r.scalar.drain().then_inc(sc, 1); sc.inc(1)
        r.vector.wait(sc)
        if (("at", hb, "ncfree") in M):
            r.vector.wait(*M[("at", hb, "ncfree")])
        r.vector.tensor_scalar(negcb[hb][0:1, 0:TOWN], ctx.sb["posrsb"][0:1, 0:TOWN],
                               -slope, nrm[0:1, S + TOWN + 3:S + TOWN + 4],
                               op0=ALU.mult, op1=ALU.subtract)
        r.vector.drain()
        if skipk1[h]:
            np_ = ctx.ps["atLB"]
            if ("at", "lbfree") in M:
                r.tensor.wait(*M[("at", "lbfree")])
            r.tensor.wait(sc)
            r.tensor.matmul(np_[:, 0:1], onesrowf[0:1, :],
                            nrm[0:1, S + TOWN + 3:S + TOWN + 4],
                            start=True, stop=True).then_inc(mm, 1)
            mm.inc(1)
            r.vector.wait(mm)
            bnd = ctx.sb["atbnd"]
            r.vector.tensor_copy(bnd[:, hb:hb + 1], np_[:, 0:1]).then_inc(ve, 1)
            ve.inc(1)
            M[("at", "lbfree")] = (ve, ve.n)
            r.vector.drain()
            for bi in range(4):
                qbe = (qpos[bi] + 1) * QB - 1
                r.vector.tensor_scalar(
                    biast[hb][:, bi * cfg.ST:(bi + 1) * cfg.ST],
                    ctx.sb["iotasb"][:, 0:cfg.ST], slope,
                    bnd[:, hb:hb + 1],
                    op0=ALU.mult, op1=ALU.subtract)
                r.vector.drain()
                r.vector.tensor_scalar_add(
                    biast[hb][:, bi * cfg.ST:(bi + 1) * cfg.ST],
                    biast[hb][:, bi * cfg.ST:(bi + 1) * cfg.ST],
                    -slope * qbe)
                r.vector.drain()
        else:
            r.vector.tensor_scalar_mul(biast[hb][:, 0:cfg.ST],
                                       ctx.sb["iotasb"][:, 0:cfg.ST], slope)
            r.vector.drain()
        ii = r.vector.nop()
        ii.then_inc(ve, 1); ve.inc(1)
        return ve.n

    def head_loads(h, hb):
        if (("at", hb, "kfree") in M):
            r.sync.wait(*M[("at", hb, "kfree")])
        r.sync.dma(Ksb[hb][:, 0:S], ctx.dram["kfm"][h * 128:(h + 1) * 128, :],
                   ld[hb])
        for vg in range(8):
            vs0, vs1 = vg * (S // 8), (vg + 1) * (S // 8)
            r.sync.dma(Vsb[hb][:, vs0:vs1].rearrange("p (kt c) -> p kt c", c=128),
                       ctx.dram["vtm"][vs0:vs1, h * 128:(h + 1) * 128]
                       .rearrange("(kt p) c -> p kt c", p=128), ld[hb])
        r.sync.dma(Qsb[hb][:, 0:TOWN], ctx.dram["qfm"][h * 128:(h + 1) * 128, :],
                   ld[hb])
        return ld[hb].n

    hld = {0: head_loads(0, 0), 1: head_loads(1, 1)}
    prep_ve = {0: np_norms(0, 0, hld[0])}

    for h in range(NH):
        hb = h % 3
        if h + 2 < NH:
            hld[h + 2] = head_loads(h + 2, (h + 2) % 3)
        if h + 1 < NH:
            prep_ve[h + 1] = np_norms(h + 1, (h + 1) % 3, hld[h + 1])
        r.tensor.wait(ve, prep_ve[h])
        r.tensor.wait(ld[hb], hld[h])
        kiter = 0
        pend = None
        for bi in range(4):
            p = qpos[bi]
            nkt = (p + 1) * QBT
            ob2 = bi % 2
            for kt in range(nkt):
                s2 = kiter % 3
                if (("at", s2, "spfree") in M):
                    r.tensor.wait(*M[("at", s2, "spfree")])
                if (("at", s2, "ptfree") in M):
                    r.tensor.wait(*M[("at", s2, "ptfree")])
                if kt == 0 and (("at", ob2, "opfree") in M):
                    r.tensor.wait(*M[("at", ob2, "opfree")])
                if skipk1[h]:
                    r.tensor.matmul(sp[s2][:, 0:QB],
                                    Ksb[hb][:, kt * 128:(kt + 1) * 128],
                                    Qsb[hb][:, bi * QB:(bi + 1) * QB],
                                    start=True, stop=True).then_inc(mm, 1)
                else:
                    r.tensor.matmul(sp[s2][:, 0:QB], onesrow[0:1, :],
                                    negcb[hb][0:1, bi * QB:(bi + 1) * QB],
                                    start=True, stop=False)
                    r.tensor.matmul(sp[s2][:, 0:QB],
                                    Ksb[hb][:, kt * 128:(kt + 1) * 128],
                                    Qsb[hb][:, bi * QB:(bi + 1) * QB],
                                    start=False, stop=True).then_inc(mm, 1)
                mm.inc(1)
                diag = kt - (nkt - QBT)
                bcol = biast[hb][:, (bi * cfg.ST if skipk1[h] else 0) + kt:
                                 (bi * cfg.ST if skipk1[h] else 0) + kt + 1]
                if diag >= 0:
                    r.vector.wait(mm)
                    if ("at", s2, "sdfree") in M:
                        r.vector.wait(*M[("at", s2, "sdfree")])
                    r.vector.tensor_sub(sd[s2][:, 0:QB], sp[s2][:, 0:QB],
                                        ctx.sb[f"msk{diag}"][:]).then_inc(ve, 1)
                    ve.inc(1)
                    M[("at", s2, "spfree")] = (ve, ve.n)
                    r.scalar.wait(ve)
                    r.scalar.activation(pT[s2][:, 0:QB], sd[s2][:, 0:QB],
                                        AF.Exp, bias=bcol, scale=1.0) \
                        .then_inc(sc, 1)
                    sc.inc(1)
                    M[("at", s2, "sdfree")] = (sc, sc.n)
                else:
                    r.scalar.wait(mm)
                    r.scalar.activation(pT[s2][:, 0:QB], sp[s2][:, 0:QB], AF.Exp,
                                        bias=bcol, scale=1.0).then_inc(sc, 1)
                    sc.inc(1)
                    M[("at", s2, "spfree")] = (sc, sc.n)
                expmark = sc.n
                if pend is not None:
                    _pv(ctx, r, pend, Vsb, onescol, op, lp, pT, mm, M)
                pend = (pend, hb, bi, kt, nkt, kiter % 3, expmark)
                kiter += 1
            _pv(ctx, r, pend, Vsb, onescol, op, lp, pT, mm, M)
            pend = None
            lq = bi % 2
            r.vector.wait(mm)
            if ("at", lq, "lsfree") in M:
                r.vector.wait(*M[("at", lq, "lsfree")])
            r.vector.tensor_scalar_add(lsb[lq][0:1, 0:QB], lp[lq][0:1, 0:QB],
                                       1e-30)
            r.vector.drain()
            r.vector.reciprocal(lsb[lq][0:1, 0:QB], lsb[lq][0:1, 0:QB])
            r.vector.drain().then_inc(ve, 1); ve.inc(1)
            M[("at", lq, "lpfree")] = (ve, ve.n)
            r.tensor.wait(ve)
            if ("at", "lbfree") in M:
                r.tensor.wait(*M[("at", "lbfree")])
            r.tensor.matmul(lb[:, 0:QB], onesrowf[0:1, :], lsb[lq][0:1, 0:QB],
                            start=True, stop=True).then_inc(mm, 1)
            mm.inc(1)
            M[("at", lq, "lsfree")] = (mm, mm.n)
            ob = (h * 4 + bi) % 2
            r.vector.wait(mm)
            if (("at", ob, "obfree") in M):
                r.vector.wait(*M[("at", ob, "obfree")])
            r.vector.tensor_copy(lbc[:, 0:QB], lb[:, 0:QB])
            r.vector.drain()
            M[("at", "lbfree")] = (ve, ve.n + 1)
            r.vector.tensor_mul(osb[ob][:, 0:QB], op[ob2][:, 0:QB], lbc[:, 0:QB]) \
                .then_inc(ve, 1)
            ve.inc(1)
            M[("at", ob2, "opfree")] = (ve, ve.n)
            r.scalar.wait(ve)
            r.scalar.dma(ctx.dram["ofm"][h * 128:(h + 1) * 128,
                                         bi * QB:(bi + 1) * QB],
                         osb[ob][:, 0:QB], st[ob])
            M[("at", ob, "obfree")] = (st[ob], st[ob].n)
        M[("at", hb, "kfree")] = (mm, mm.n)
        M[("at", hb, "ncfree")] = (mm, mm.n)


def _pv(ctx, r, pend, Vsb, onescol, op, lp, pT, mm, M):
    _, hb, bi, kt, nkt, s2, expmark = pend
    QB = ctx.cfg.QB
    r.tensor.wait(ctx.sems["sc"], expmark)
    if kt == 0:
        lq = bi % 2
        if ("at", lq, "lpfree") in M:
            r.tensor.wait(*M[("at", lq, "lpfree")])
    r.tensor.matmul(op[bi % 2][:, 0:QB], Vsb[hb][:, kt * 128:(kt + 1) * 128],
                    pT[s2][:, 0:QB], start=(kt == 0), stop=(kt == nkt - 1))
    lmm = r.tensor.matmul(lp[bi % 2][0:1, 0:QB], onescol[:, 0:1],
                          pT[s2][:, 0:QB], start=(kt == 0), stop=(kt == nkt - 1))
    lmm.then_inc(mm, 1)
    mm.inc(1)
    M[("at", s2, "ptfree")] = (mm, mm.n)



def mk_post_residual(ctx, res_name, dst_name, add_bb=False):
    r = ctx.rec
    ve = ctx.sems["ve"]
    st = [ctx.sems["st0"], ctx.sems["st1"], ctx.sems["st2"], ctx.sems["st3"]]
    rl = [ctx.sems["rl0"], ctx.sems["rl1"], ctx.sems["rl2"], ctx.sems["rl3"]]
    resb = [ctx.sb["pjr0"], ctx.sb["pjr1"], ctx.sb["pjr2"], ctx.sb["pjr3"]]
    outb = [ctx.sb["pjq0"], ctx.sb["pjq1"], ctx.sb["pjq2"], ctx.sb["pjq3"]]
    M = ctx.marks
    cnt = [0]

    def post(tt, oc_abs, pap, ow):
        b = cnt[0] % 4
        cnt[0] += 1
        if ("pr", res_name, b) in M:
            r.sync.wait(*M[("pr", res_name, b)])
        r.sync.dma(resb[b][:, 0:ow],
                   ctx.dram[res_name][tt * 128:(tt + 1) * 128,
                                      oc_abs:oc_abs + ow], rl[b])
        r.vector.wait(rl[b])
        r.vector.wait(ctx.sems["mm"])
        if ("po", dst_name, b) in M:
            r.vector.wait(*M[("po", dst_name, b)])
        r.vector.tensor_add(outb[b][:, 0:ow], pap, resb[b][:, 0:ow])
        if add_bb:
            r.vector.drain()
            r.vector.tensor_add(outb[b][:, 0:ow], outb[b][:, 0:ow],
                                ctx.sb["bbb"][:, oc_abs:oc_abs + ow])
        ir = r.vector.drain()
        ir.then_inc(ve, 1); ve.inc(1)
        pfree = (ve, ve.n)
        M[("pr", res_name, b)] = (ve, ve.n)
        r.scalar.wait(ve)
        r.scalar.dma(ctx.dram[dst_name][tt * 128:(tt + 1) * 128,
                                        oc_abs:oc_abs + ow],
                     outb[b][:, 0:ow], st[b])
        M[("po", dst_name, b)] = (st[b], st[b].n)
        return pfree
    return post


def emit_wout(ctx):
    cfg = ctx.cfg
    emit_proj_tm(ctx, "wo", "ofm", cfg.TOWN, "woutTb", 0, cfg.D, 1024,
                 mk_post_residual(ctx, "x_own", "xr1"))


def emit_ffn(ctx, xin, xout, tagn):
    cfg, r = ctx.cfg, ctx.rec
    D, TOWN, OT, DT = cfg.D, cfg.TOWN, cfg.OT, cfg.DT
    ve, sc, mm = ctx.sems["ve"], ctx.sems["sc"], ctx.sems["mm"]
    ld = [ctx.sems["ld0"], ctx.sems["ld1"]]
    M = ctx.marks

    emit_ln_transpose(ctx, xin, "xnofm", OT, "ln")
    ctx.barrier()
    ctx.psum_free_all()

    bbp = ctx.apsum("fbb", [128, 512], F32)
    r.sync.dma(ctx.sb["rbf"][0:1, 0:D], ctx.dram["bbase"], ctx.sems["ld"])
    for e in ENGINES:
        getattr(r, e).wait_ge(ctx.sems["ld"].h, ctx.sems["ld"].n)
    for c0 in range(0, D, 512):
        r.tensor.wait(ve)
        r.tensor.matmul(bbp[:, :], ctx.sb["onesrowf"][0:1, :],
                        ctx.sb["rbf"][0:1, c0:c0 + 512],
                        start=True, stop=True).then_inc(mm, 1)
        mm.inc(1)
        r.vector.wait(mm)
        r.vector.tensor_copy(ctx.sb["bbb"][:, c0:c0 + 512], bbp[:, :]) \
            .then_inc(ve, 1)
        ve.inc(1)
    r.sync.dma(ctx.sb["flb"][:], ctx.dram["lbCb"], ctx.sems["ld"])
    r.sync.dma(ctx.sb["fla"][:].rearrange("p (j c) -> p j c", c=128),
               ctx.dram["laTb"][:, :].rearrange("(j p) c -> p j c", p=128),
               ctx.sems["ld"])
    r.sync.dma(ctx.sb["fwr"][:].rearrange("p (j c) -> p j c", c=8),
               ctx.dram["wrTb"][:, :].rearrange("(j p) c -> p j c", p=128),
               ctx.sems["ld"])
    ldw = ctx.sems["ld"].n
    ctx.barrier()

    rtp = ctx.apsum("frt", [128, 8], F32)
    hp = ctx.apsum("fh", [128, 128], F32)
    gpp = ctx.apsum("fgp", [128, 128], BF16)
    rts = ctx.sb["frts"]; e8 = ctx.sb["fe8"]; p8 = ctx.sb["fp8"]
    w8 = ctx.sb["fw8"]; hs = ctx.sb["fhs"]; gb = ctx.sb["fgb"]
    gfm = ctx.sb["fgfm"]
    xnb = [ctx.sb["pjx0"], ctx.sb["pjx1"]]
    for tc in range(TOWN // 512):
        b = tc % 2
        if ("fx", b) in M:
            r.sync.wait(*M[("fx", b)])
        for jg in range(0, DT, 2):
            nj = min(2, DT - jg)
            dap = ctx.dram["xnofm"][jg * 128:(jg + nj) * 128,
                                    tc * 512:(tc + 1) * 512] \
                .rearrange("(j p) c -> p j c", p=128)
            sap = xnb[b][:, jg * 512:(jg + nj) * 512] \
                .rearrange("p (j c) -> p j c", c=512)
            r.sync.dma(sap, dap, ld[b])
        xm = ld[b].n
        for tti in range(4):
            tt = tc * 4 + tti
            r.tensor.wait(ld[b], xm)
            if tt == 0:
                r.tensor.wait(ctx.sems["ld"], ldw)
            if ("fpf",) in M:
                r.tensor.wait(*M[("fpf",)])
            for j in range(DT):
                lhs = xnb[b][:, j * 512 + tti * 128: j * 512 + (tti + 1) * 128]
                r.tensor.matmul(rtp[:, :], lhs, ctx.sb["fwr"][:, j * 8:(j + 1) * 8],
                                start=(j == 0), stop=(j == DT - 1))
                mmi = r.tensor.matmul(hp[:, :], lhs,
                                      ctx.sb["fla"][:, j * 128:(j + 1) * 128],
                                      start=(j == 0), stop=(j == DT - 1))
            mmi.then_inc(mm, 1)
            mm.inc(1)
            r.vector.wait(mm)
            r.vector.tensor_copy(rts[:, 0:8], rtp[:, :])
            r.vector.drain()
            r.vector.reduce_max(rts[:, 8:9], rts[:, 0:8], axis=AX.X, negate=True)
            r.vector.drain().then_inc(ve, 1); ve.inc(1)
            r.scalar.wait(ve)
            r.scalar.activation(e8[:, 0:8], rts[:, 0:8], AF.Exp,
                                bias=rts[:, 8:9], scale=1.0)
            r.scalar.drain().then_inc(sc, 1); sc.inc(1)
            r.vector.wait(sc)
            r.vector.max(p8[:, 0:8], e8[:, 0:8])
            r.vector.drain()
            r.vector.tensor_scalar(p8[:, 8:9], p8[:, 0:1], p8[:, 1:2], None,
                                   op0=ALU.add)
            r.vector.drain()
            r.vector.reciprocal(p8[:, 9:10], p8[:, 8:9])
            r.vector.drain()
            r.vector.tensor_scalar(w8[:, 0:8], e8[:, 0:8], p8[:, 1:2], None,
                                   op0=ALU.is_ge)
            r.vector.drain()
            r.vector.tensor_mul(w8[:, 0:8], w8[:, 0:8], e8[:, 0:8])
            r.vector.drain()
            r.vector.tensor_scalar(w8[:, 0:8], w8[:, 0:8], p8[:, 9:10], 2.0,
                                   op0=ALU.mult, op1=ALU.mult)
            r.vector.drain()
            r.vector.tensor_copy(hs[:, :], hp[:, :])
            r.vector.drain().then_inc(ve, 1); ve.inc(1)
            M[("fpf",)] = (ve, ve.n)
            r.scalar.wait(ve)
            for e in range(8):
                r.scalar.activation(gb[:, e * 16:(e + 1) * 16],
                                    hs[:, e * 16:(e + 1) * 16], AF.Copy,
                                    bias=0.0, scale=w8[:, e:e + 1])
            r.scalar.drain().then_inc(sc, 1); sc.inc(1)
            r.tensor.wait(sc)
            r.tensor.transpose(gpp[:, :], gb[:, :], ctx.sb["ident"][:]) \
                .then_inc(mm, 1)
            mm.inc(1)
            r.vector.wait(mm)
            r.vector.tensor_copy(gfm[:, tt * 128:(tt + 1) * 128], gpp[:, :]) \
                .then_inc(ve, 1)
            ve.inc(1)
            M[("fgdone", tt)] = (ve, ve.n)
        M[("fx", b)] = (mm, mm.n)
    ctx.barrier()
    ctx.psum_free_all()

    for k in range(4):
        ctx.apsum(f"pj{k}", [128, 512], F32)

    def extra(tt, oc_abs, pap, ow):
        return r.tensor.matmul(pap, gfm[:, tt * 128:(tt + 1) * 128],
                               ctx.sb["flb"][:, oc_abs:oc_abs + ow],
                               start=False, stop=True)

    emit_proj_tm(ctx, tagn, "xnofm", TOWN, "wbTb", 0, D, 1024,
                 mk_post_residual(ctx, xin, xout, add_bb=True),
                 extra_mm=extra)


def emit_attnb(ctx):
    cfg, r = ctx.cfg, ctx.rec
    D, TOWN, OT, DT, KT = cfg.D, cfg.TOWN, cfg.OT, cfg.DT, cfg.KT
    ve, sc, mm = ctx.sems["ve"], ctx.sems["sc"], ctx.sems["mm"]
    ld = [ctx.sems["ld0"], ctx.sems["ld1"]]
    st = [ctx.sems["st0"], ctx.sems["st1"]]
    M = ctx.marks

    emit_ln_transpose(ctx, "xr2", "vfm", OT, "ln", extra_out="vdbl")
    ctx.barrier()
    ctx.psum_free_all()

    rep = ctx.apsum("abRe", [128, TOWN], F32)
    imp = ctx.apsum("abIm", [128, TOWN], F32)
    cs = [ctx.sb["abc0"], ctx.sb["abc1"]]
    vres = ctx.sb["abvr"]
    sq1 = ctx.sb["absq1"]; sq2 = ctx.sb["absq2"]
    psb = [ctx.sb["abps0"], ctx.sb["abps1"]]
    for jg in range(0, DT, 2):
        nj = min(2, DT - jg)
        r.sync.dma(vres[:, jg * TOWN:(jg + nj) * TOWN]
                   .rearrange("p (j c) -> p j c", c=TOWN),
                   ctx.dram["vfm"][jg * 128:(jg + nj) * 128, :]
                   .rearrange("(j p) c -> p j c", p=128), ctx.sems["ld"])
    vl = ctx.sems["ld"].n
    r.tensor.wait_ge(ctx.sems["ld"].h, vl)
    for kt in range(KT):
        for j in range(DT):
            b = j % 2
            if ("abcs2", b) in M:
                r.sync.wait(*M[("abcs2", b)])
            r.sync.dma(cs[b][:, 0:128],
                       ctx.dram["cosFb"][j * 128:(j + 1) * 128,
                                         kt * 128:(kt + 1) * 128], ld[b])
            r.sync.dma(cs[b][:, 128:256],
                       ctx.dram["sinFb"][j * 128:(j + 1) * 128,
                                         kt * 128:(kt + 1) * 128], ld[b])
            r.tensor.wait(ld[b])
            if j <= 1 and ("abpf",) in M:
                r.tensor.wait(*M[("abpf",)])
            vt_j = vres[:, j * TOWN:(j + 1) * TOWN]
            for c0 in range(0, TOWN, 512):
                r.tensor.matmul(rep[:, c0:c0 + 512], cs[b][:, 0:128],
                                vt_j[:, c0:c0 + 512],
                                start=(j == 0), stop=(j == DT - 1))
            for c0 in range(0, TOWN, 512):
                mmi = r.tensor.matmul(imp[:, c0:c0 + 512], cs[b][:, 128:256],
                                      vt_j[:, c0:c0 + 512],
                                      start=(j == 0), stop=(j == DT - 1))
            mmi.then_inc(mm, 1)
            mm.inc(1)
            M[("abcs2", b)] = (mm, mm.n)
        r.scalar.wait(mm)
        if ("absqf",) in M:
            r.scalar.wait(*M[("absqf",)])
        r.scalar.activation(sq1[:], rep[:, :], AF.Square, bias=0.0, scale=1.0)
        r.scalar.drain()
        r.scalar.activation(sq2[:], imp[:, :], AF.Square, bias=0.0, scale=1.0)
        r.scalar.drain().then_inc(sc, 1); sc.inc(1)
        M[("abpf",)] = (sc, sc.n)
        pb_ = kt % 2
        r.vector.wait(sc)
        if ("abpsb", pb_) in M:
            r.vector.wait(*M[("abpsb", pb_)])
        r.vector.tensor_add(psb[pb_][:], sq1[:], sq2[:]).then_inc(ve, 1)
        ve.inc(1)
        M[("absqf",)] = (ve, ve.n)
        r.scalar.wait(ve)
        r.scalar.dma(ctx.dram["Ph"][kt * 128:(kt + 1) * 128, :], psb[pb_][:],
                     st[pb_])
        M[("abpsb", pb_)] = (st[pb_], st[pb_].n)
    ctx.barrier()
    ctx.psum_free_all()

    cp = ctx.apsum("abC", [128, 2048 if D >= 4096 else D], F32)
    inv = ctx.sb["abinv"]
    csb = ctx.sb["abcs"]
    pt = [ctx.sb["abpt0"], ctx.sb["abpt1"]]
    half = min(2048, D)
    for hf in range(D // half):
        if ("abinvfree",) in M:
            r.sync.wait(*M[("abinvfree",)])
        for jg in range(0, KT, 2):
            nj = min(2, KT - jg)
            dap = ctx.dram["invTb"][jg * 128:(jg + nj) * 128,
                                    hf * half:(hf + 1) * half] \
                .rearrange("(j p) c -> p j c", p=128)
            sap = inv[:, jg * half:(jg + nj) * half] \
                .rearrange("p (j c) -> p j c", c=half)
            r.sync.dma(sap, dap, ctx.sems["ld"])
        lv = ctx.sems["ld"].n
        r.tensor.wait_ge(ctx.sems["ld"].h, lv)
        for tt in range(OT):
            if ("abcf",) in M:
                r.tensor.wait(*M[("abcf",)])
            for kt in range(KT):
                ptb = (tt * KT + kt) % 2
                if ("abptf", ptb) in M:
                    r.sync.wait(*M[("abptf", ptb)])
                r.sync.dma(pt[ptb][:],
                           ctx.dram["Ph"][kt * 128:(kt + 1) * 128,
                                          tt * 128:(tt + 1) * 128], ld[ptb])
                r.tensor.wait(ld[ptb])
                for c0 in range(0, half, 512):
                    mmi = r.tensor.matmul(cp[:, c0:c0 + 512], pt[ptb][:],
                                          inv[:, kt * half + c0: kt * half + c0 + 512],
                                          start=(kt == 0), stop=(kt == KT - 1))
                mmi.then_inc(mm, 1)
                mm.inc(1)
                M[("abptf", ptb)] = (mm, mm.n)
            cb = tt % 2
            r.scalar.wait(mm)
            if ("abcsf", cb) in M:
                r.scalar.wait(*M[("abcsf", cb)])
            r.scalar.activation(csb[:, cb * half:(cb + 1) * half],
                                cp[:, 0:half], AF.Copy, bias=0.0, scale=1.0)
            r.scalar.drain().then_inc(sc, 1); sc.inc(1)
            M[("abcf",)] = (sc, sc.n)
            r.scalar.dma(ctx.dram["corrh"][tt * 128:(tt + 1) * 128,
                                           hf * half:(hf + 1) * half],
                         csb[:, cb * half:(cb + 1) * half], st[cb])
            M[("abcsf", cb)] = (st[cb], st[cb].n)
        M[("abinvfree",)] = (mm, mm.n)
    ctx.barrier()
    ctx.psum_free_all()

    r.sync.dma(ctx.sb["abrb"][:], ctx.dram["rowbase"], ctx.sems["ld"])
    for e in ("vector",):
        getattr(r, e).wait_ge(ctx.sems["ld"].h, ctx.sems["ld"].n)
    ct = [ctx.sb["abct0"], ctx.sb["abct1"]]
    t8 = ctx.sb["abt8"]; i8 = ctx.sb["abi8"]; w8r = ctx.sb["abw8"]
    off = ctx.sb["aboff"]
    acc = ctx.sb["abacc"]
    gt = [ctx.sb["abg0"], ctx.sb["abg1"]]
    gld = ctx.sems["gld"]
    for tt in range(OT):
        b = tt % 2
        if ("abct", b) in M:
            r.sync.wait(*M[("abct", b)])
        r.sync.dma(ct[b][:], ctx.dram["corrh"][tt * 128:(tt + 1) * 128, :], ld[b])
        if ("abacc",) in M:
            r.sync.wait(*M[("abacc",)])
        r.sync.dma(acc[:], ctx.dram["xr2"][tt * 128:(tt + 1) * 128, :], ld[b])
        r.vector.wait(ld[b])
        if ("abacc",) in M:
            r.vector.wait(*M[("abacc",)])
        r.vector.max(t8[:, 0:8], ct[b][:])
        r.vector.drain()
        r.vector.max_index(i8[:, 0:8], t8[:, 0:8], ct[b][:])
        r.vector.drain()
        r.vector.tensor_scalar_mul(t8[:, 8:9], t8[:, 0:1], -1.0)
        r.vector.drain().then_inc(ve, 1); ve.inc(1)
        M[("abct", b)] = (ve, ve.n)
        r.scalar.wait(ve)
        r.scalar.activation(t8[:, 16:24], t8[:, 0:8], AF.Exp,
                            bias=t8[:, 8:9], scale=1.0)
        r.scalar.drain().then_inc(sc, 1); sc.inc(1)
        r.vector.wait(sc)
        r.vector.reduce_sum(t8[:, 9:10], t8[:, 16:24], axis=AX.X)
        r.vector.drain()
        r.vector.reciprocal(t8[:, 10:11], t8[:, 9:10])
        r.vector.drain()
        r.vector.tensor_scalar(w8r[:, 0:8], t8[:, 16:24], t8[:, 10:11], None,
                               op0=ALU.mult)
        r.vector.drain()
        r.vector.tensor_copy(ctx.sb["abof"][:, 0:8], i8[:, 0:8])
        r.vector.drain()
        r.vector.tensor_scalar(ctx.sb["abof"][:, 0:8], ctx.sb["abof"][:, 0:8],
                               ctx.sb["abrb"][:, tt:tt + 1], None, op0=ALU.add)
        r.vector.drain()
        r.vector.tensor_copy(off[:, 0:8], ctx.sb["abof"][:, 0:8])
        r.vector.drain().then_inc(ve, 1); ve.inc(1)
        for k in range(8):
            gb_ = k % 2
            r.gpsimd.wait(ve)
            if ("abg", gb_) in M:
                r.gpsimd.wait(*M[("abg", gb_)])
            r.gpsimd.indirect_dma_start(
                gt[gb_][:], None, ctx.dram["vdbl"],
                IndirectOffsetOnAxis(off[:, k:k + 1], 1),
            ).then_inc(gld, 16)
            gld.inc(16)
            r.vector.wait(gld)
            r.vector.scalar_tensor_tensor(acc[:], gt[gb_][:], w8r[:, k:k + 1],
                                          acc[:], op0=ALU.mult, op1=ALU.add)
            r.vector.drain().then_inc(ve, 1); ve.inc(1)
            M[("abg", gb_)] = (ve, ve.n)
        M[("abacc",)] = (ve, ve.n)
        r.scalar.wait(ve)
        r.scalar.dma(ctx.dram["xr3"][tt * 128:(tt + 1) * 128, :], acc[:], st[b])
        M[("abacc",)] = (st[b], st[b].n)


def _replay_role(ctx, name, eng):
    needs_role = any(r.meth == "IF_ROLE" for r in ctx.rec.q[name])
    if needs_role:
        with eng.register(f"role_{name}") as rr:
            def loader():
                eng.wait_ge(ctx.sems["ld"].h, 16)
                eng.reg_load(rr, ctx.sb["rolesb"][0:1, 0:1])
                return rr
            replay(ctx.rec, name, eng, loader)
    else:
        replay(ctx.rec, name, eng, None)


def kernel(**inputs):
    cfg = CFG
    nc, ctx = build_program(cfg)
    in_maps = [host_core_inputs(cfg, inputs, c) for c in range(8)]
    res = run_bass_kernel_spmd(nc, in_maps, core_ids=list(range(8)))
    x = np.asarray(inputs["L0_attn_a_hidden_states"])
    out = np.empty_like(x, dtype=np.float32)
    for c in range(8):
        s, role = c // 2, c % 2
        yc = np.asarray(res.results[c]["y"])
        for bi, p in enumerate(cfg.QPOS[role]):
            out[s, p * cfg.QB:(p + 1) * cfg.QB] = yc[bi * cfg.QB:(bi + 1) * cfg.QB]
    return out


# version 08
# speedup vs baseline: 1.1237x; 1.1237x over previous
import numpy as np
import ml_dtypes
import concourse.bass as bass
import concourse.mybir as mybir
from concourse.bass_utils import run_bass_kernel_spmd
from concourse.bass_core_rust import IndirectOffsetOnAxis

F32 = mybir.dt.float32
BF16 = mybir.dt.bfloat16
I32 = mybir.dt.int32
U32 = mybir.dt.uint32
AF = mybir.ActivationFunctionType
ALU = mybir.AluOpType
AX = mybir.AxisListType

ENGINES = ("sync", "vector", "scalar", "tensor", "gpsimd")
LN_EPS = 1e-5
CLIP = 8.0
NH = 32
HD = 128
HSC = HD ** -0.5
PBASE = 10240
PSIZE = 196 * 1024


class Cfg:
    def __init__(self, S=4096, D=4096):
        self.S = S
        self.D = D
        self.TOWN = S // 2
        self.QB = max(128, S // 8)
        self.NBLK = S // self.QB
        self.QPOS = ([0, 1, self.NBLK - 2, self.NBLK - 1],
                     list(range(2, self.NBLK - 2)))
        self.DT = D // 128
        self.ST = S // 128
        self.OT = self.TOWN // 128
        self.KP = (D // 2 // 128 + 1) * 128
        self.KT = self.KP // 128


CFG = Cfg()


_CONST_CACHE = {}


def host_constants(cfg):
    key = (cfg.S, cfg.D)
    if key in _CONST_CACHE:
        return _CONST_CACHE[key]
    D = cfg.D
    n = np.arange(D, dtype=np.float64)[:, None]
    k = np.arange(cfg.KP, dtype=np.float64)[None, :]
    ang = 2.0 * np.pi * (n * k) / D
    cosF = np.cos(ang); cosF[:, D // 2 + 1:] = 0.0
    sinF = np.sin(ang); sinF[:, D // 2 + 1:] = 0.0
    kk = np.arange(cfg.KP, dtype=np.float64)[:, None]
    dd = np.arange(D // 2 + 1, dtype=np.float64)[None, :]
    w = np.full((cfg.KP, 1), 2.0); w[0] = 1.0; w[D // 2] = 1.0; w[D // 2 + 1:] = 0.0
    invH = (w * np.cos(2.0 * np.pi * (kk * dd) / D)) / D
    invT = np.zeros((cfg.KP, D))
    invT[:, :D // 2 + 1] = invH
    invT[:, D // 2 + 1:] = invH[:, 1:D // 2][:, ::-1]
    m = np.arange(1, NH + 1, dtype=np.float64) * (8.0 / NH)
    slopes = (1.0 / np.power(2.0, m)).astype(np.float64)
    iota_kt = (np.arange(128)[:, None] + 128.0 * np.arange(cfg.ST)[None, :]).astype(np.float32)
    nmask = cfg.QB // 128
    masks = np.stack([
        np.where(np.arange(128)[:, None] + o * 128 <= np.arange(cfg.QB)[None, :],
                 np.float32(0.0), np.float32(1e30))
        for o in range(nmask)]).astype(np.float32).astype(ml_dtypes.bfloat16)
    rowbase = ((np.arange(128)[:, None] + 128 * np.arange(cfg.OT)[None, :]).astype(np.int64)
               * (2 * D)).astype(np.float32)
    _CONST_CACHE[key] = dict(
        cosF=cosF.astype(ml_dtypes.bfloat16), sinF=sinF.astype(ml_dtypes.bfloat16),
        invT=invT.astype(ml_dtypes.bfloat16), slopes=slopes, iota_kt=iota_kt,
        masks=masks, rowbase=rowbase)
    return _CONST_CACHE[key]


def host_core_inputs(cfg, inputs, core):
    s, role = core // 2, core % 2
    x = np.asarray(inputs["L0_attn_a_hidden_states"], dtype=np.float32)
    xs = np.ascontiguousarray(x[s][:cfg.S, :cfg.D])
    qpos = cfg.QPOS[role]
    xo = np.ascontiguousarray(
        np.concatenate([xs[p * cfg.QB:(p + 1) * cfg.QB] for p in qpos], axis=0))
    posrow = np.concatenate(
        [np.arange(p * cfg.QB, (p + 1) * cfg.QB) for p in qpos]
    ).astype(np.float32)[None, :]
    cst = host_constants(cfg)
    D = cfg.D

    def f32c(a):
        return np.ascontiguousarray(np.asarray(a, dtype=np.float32))

    m = {
        "x_seq": xs,
        "x_own": xo,
        "role": np.array([[role]], dtype=np.int32),
        "posrow": posrow,
        "wqkvT": f32c(np.asarray(inputs["L0_attn_a_Wqkv"]).T),
        "woutT": f32c(np.asarray(inputs["L0_attn_a_Wout"]).T),
        "wbT": f32c(np.asarray(inputs["L0_ffn_W_base"]).T),
        "wrT": f32c(np.asarray(inputs["L0_ffn_W_router"]).T),
        "laT": f32c(np.asarray(inputs["L0_ffn_lora_A"]).reshape(128, D).T),
        "lbC": f32c(np.transpose(np.asarray(inputs["L0_ffn_lora_B"]), (0, 2, 1)).reshape(128, D)),
        "bbase": f32c(np.asarray(inputs["L0_ffn_b_base"])[None, :]),
        "qlnw": f32c(np.asarray(inputs["L0_attn_a_q_ln_w"])[None, :]),
        "qlnb": f32c(np.asarray(inputs["L0_attn_a_q_ln_b"])[None, :]),
        "klnw": f32c(np.asarray(inputs["L0_attn_a_k_ln_w"])[None, :]),
        "klnb": f32c(np.asarray(inputs["L0_attn_a_k_ln_b"])[None, :]),
        "cosFb": cst["cosF"], "sinFb": cst["sinF"], "invTb": cst["invT"],
        "iota_kt": cst["iota_kt"], "rowbase": cst["rowbase"],
    }
    for o in range(cst["masks"].shape[0]):
        m[f"mask{o}"] = np.ascontiguousarray(cst["masks"][o])
    return m


class SemT:
    def __init__(self, handle, name):
        self.h = handle; self.name = name; self.n = 0

    def inc(self, by):
        self.n += by
        return self.n


class InstRec:
    __slots__ = ("meth", "a", "kw", "incs")

    def __init__(self, meth, a, kw):
        self.meth = meth; self.a = a; self.kw = kw; self.incs = []

    def then_inc(self, sem, n=1):
        self.incs.append((sem, n))
        return self


class EngRec:
    def __init__(self, q):
        object.__setattr__(self, "_q", q)

    def __getattr__(self, meth):
        q = self._q

        def f(*a, **kw):
            r = InstRec(meth, a, kw)
            q.append(r)
            return r
        return f

    def wait(self, sem, val=None):
        self.wait_ge(sem.h, sem.n if val is None else val)

    def dma(self, out, in_, sem, **kw):
        self.dma_start(out=out, in_=in_, **kw).then_inc(sem, 16)
        return sem.inc(16)


class Rec:
    def __init__(self):
        self.q = {e: [] for e in ENGINES}
        for e in ENGINES:
            setattr(self, e, EngRec(self.q[e]))


def replay(rec, engine_name, eng, role_reg):
    cur = None
    for r in rec.q[engine_name]:
        if r.meth == "IF_ROLE":
            if callable(role_reg):
                role_reg = role_reg()
            cur = eng.If_eq(role_reg, r.a[0])
            cur.__enter__()
            continue
        if r.meth == "ELSE_ROLE":
            cur.__exit__(None, None, None)
            cur = eng.Else()
            cur.__enter__()
            continue
        if r.meth == "ENDIF_ROLE":
            cur.__exit__(None, None, None)
            cur = None
            eng.end_ifs()
            continue
        inst = getattr(eng, r.meth)(*r.a, **r.kw)
        for (sem, n) in r.incs:
            inst = inst.then_inc(sem.h, n)


class Ctx:
    DMA_SEMS = ("ld", "ld0", "ld1", "ld2", "st0", "st1", "st2", "st3",
                "wl0", "wl1", "rl0", "rl1", "rl2", "rl3",
                "vl0", "vl1", "cs0", "cs1", "gld", "gst")

    def __init__(self, nc, cfg):
        self.nc = nc; self.cfg = cfg
        self.dram = {}; self.sb = {}; self.ps = {}; self.sems = {}
        self.rec = Rec()
        self.marks = {}
        self._guards = []

    def din(self, name, shape, dt=F32):
        self.dram[name] = self.nc.dram_tensor(name, shape, dt, kind="ExternalInput").ap()
        return self.dram[name]

    def dint(self, name, shape, dt=F32):
        self.dram[name] = self.nc.dram_tensor(name, shape, dt, kind="Internal").ap()
        return self.dram[name]

    def dout(self, name, shape, dt=F32):
        self.dram[name] = self.nc.dram_tensor(name, shape, dt, kind="ExternalOutput").ap()
        return self.dram[name]

    def reserve_sbuf(self, nbytes):
        h = self.nc.alloc_sbuf_tensor("manual_region", [128, nbytes], mybir.dt.uint8)
        self._mb = self.nc.lookup_mloc(h).addr
        self._msz = nbytes

    def asb(self, name, shape, dt, offset):
        end = offset + int(np.prod(shape[1:])) * mybir.dt.size(dt)
        assert end <= self._msz, f"sbuf region overflow: {name} end {end}"
        t = self.nc.alloc_sbuf_tensor_at(name, shape, dt, offset=self._mb + offset)
        self.sb[name] = t
        return t

    def layout(self, base, specs):
        out = {}
        off = base
        for (name, shape, dt) in specs:
            out[name] = self.asb(name, shape, dt, off)
            off += (int(np.prod(shape[1:])) * mybir.dt.size(dt) + 63) & ~63
        assert off <= self._msz, f"layout overflow at {off}"
        out["_end"] = off
        return out

    def apsum(self, name, shape, dt=F32):
        self._psn = getattr(self, "_psn", 0) + 1
        g = self.nc.psum_tensor(f"{name}_u{self._psn}", shape, dt)
        t = g.__enter__()
        self._psum_guards = getattr(self, "_psum_guards", [])
        self._psum_guards.append((name, g))
        self.ps[name] = t
        return t

    def psum_free_all(self):
        for name, g in reversed(getattr(self, "_psum_guards", [])):
            g.__exit__(None, None, None)
            self.ps.pop(name, None)
        self._psum_guards = []

    def psum_get(self, name, shape, dt=F32):
        if name not in self.ps:
            self.apsum(name, shape, dt)
        return self.ps[name]

    def sem_snapshot(self):
        return {k: s.n for k, s in self.sems.items()}

    def sem_restore(self, snap):
        for k, v in snap.items():
            self.sems[k].n = v

    def sem(self, name):
        g = self.nc.semaphore(name)
        h = g.__enter__()
        self._guards.append(g)
        self.sems[name] = SemT(h, name)
        return self.sems[name]

    def barrier(self, engines=ENGINES):
        r = self.rec
        for e in engines:
            for s in self.DMA_SEMS:
                if s in self.sems and self.sems[s].n:
                    getattr(r, e).wait_ge(self.sems[s].h, self.sems[s].n)
        for e in engines:
            sT = self.sems[f"b_{e}"]
            eng = getattr(r, e)
            if e in ("tensor", "sync"):
                eng.nop().then_inc(sT, 1)
            else:
                eng.drain().then_inc(sT, 1)
            sT.inc(1)
        for e in engines:
            for o in engines:
                so = self.sems[f"b_{o}"]
                getattr(r, e).wait_ge(so.h, so.n)


def emit_prelude(ctx):
    r = ctx.rec
    ident = ctx.sb["ident"]
    ones = ctx.sb["onescol"]
    r.gpsimd.memset(ident[:], 1.0)
    r.gpsimd.drain()
    r.gpsimd.affine_select(ident[:], ident[:], pattern=[[1, 128]],
                           compare_op=ALU.is_equal, fill=0.0,
                           base=0, channel_multiplier=-1)
    r.gpsimd.drain().then_inc(ctx.sems["gp"], 1)
    ctx.sems["gp"].inc(1)
    r.vector.memset(ones[:], 1.0)
    r.vector.memset(ctx.sb["onesrow"][:], 1.0)
    r.vector.memset(ctx.sb["onesrowf"][:], 1.0)
    r.vector.drain().then_inc(ctx.sems["ve"], 1)
    ctx.sems["ve"].inc(1)
    ctx.barrier()


def emit_cast_weights(ctx, jobs):
    r = ctx.rec
    wc = ctx.sb["wcf"]; wb = ctx.sb["wcb"]
    ld = [ctx.sems["wl0"], ctx.sems["wl1"]]
    st = [ctx.sems["cs0"], ctx.sems["cs1"]]
    gp = ctx.sems["gp"]
    i = 0
    for (src, dst, rows, cols) in jobs:
        for r0 in range(0, rows, 128):
            for c0 in range(0, cols, 2048):
                cw = min(2048, cols - c0)
                b = i % 2
                if i >= 2:
                    r.sync.wait(*ctx.marks[("wc", i - 2)])
                r.sync.dma(wc[b][:, 0:cw],
                           ctx.dram[src][r0:r0 + 128, c0:c0 + cw], ld[b])
                r.gpsimd.wait(ld[b])
                if i >= 2:
                    r.gpsimd.wait(*ctx.marks[("wcb", i - 2)])
                r.gpsimd.tensor_copy(wb[b][:, 0:cw], wc[b][:, 0:cw]) \
                    .then_inc(gp, 1)
                gp.inc(1)
                ctx.marks[("wc", i)] = (gp, gp.n)
                r.scalar.wait(gp)
                r.scalar.dma(ctx.dram[dst][r0:r0 + 128, c0:c0 + cw],
                             wb[b][:, 0:cw], st[b])
                ctx.marks[("wcb", i)] = (st[b], st[b].n)
                i += 1


def emit_row_broadcast(ctx, jobs):
    r = ctx.rec
    rowf = ctx.sb["rbf"]; rowb = ctx.sb["rbb"]
    pbc = ctx.ps["rbp"]
    ld = ctx.sems["ld"]
    ve, mm = ctx.sems["ve"], ctx.sems["mm"]
    ones = ctx.sb["onesrow"]
    for (src, dstn, width, scale) in jobs:
        dst = ctx.sb[dstn]
        r.sync.wait(ve, ve.n)
        r.sync.dma(rowf[0:1, 0:width], ctx.dram[src], ld)
        r.vector.wait(ld)
        if scale != 1.0:
            r.vector.tensor_scalar_mul(rowf[0:1, 0:width], rowf[0:1, 0:width], scale)
            r.vector.drain()
        r.vector.tensor_copy(rowb[0:1, 0:width], rowf[0:1, 0:width])
        r.vector.drain().then_inc(ve, 1); ve.inc(1)
        r.tensor.wait(ve)
        for c0 in range(0, width, 512):
            cw = min(512, width - c0)
            pb = (c0 // 512) % 2
            if c0 >= 1024:
                r.tensor.wait(*ctx.marks[("rb", dstn, c0 - 1024)])
            r.tensor.matmul(pbc[pb][:, 0:cw], ones[0:1, :],
                            rowb[0:1, c0:c0 + cw], start=True, stop=True) \
                .then_inc(mm, 1)
            mm.inc(1)
            r.vector.wait(mm)
            r.vector.tensor_copy(dst[:, c0:c0 + cw], pbc[pb][:, 0:cw]) \
                .then_inc(ve, 1)
            ve.inc(1)
            ctx.marks[("rb", dstn, c0)] = (ve, ve.n)


def emit_ln_transpose(ctx, src, dst, ntiles, tag, src_dt=F32, gb=None,
                      extra_out=None, eps=LN_EPS, store_q="scalar"):
    cfg, r = ctx.cfg, ctx.rec
    D, DT = cfg.D, cfg.DT
    if src_dt == F32:
        xt = [ctx.sb[f"{tag}xt0"], ctx.sb[f"{tag}xt1"]]
    else:
        xt = [ctx.sb[f"{tag}xtb0"], ctx.sb[f"{tag}xtb1"]]
    tmp = ctx.sb[f"{tag}tmp"]
    bn = ctx.sb[f"{tag}bn"]; mv = ctx.sb[f"{tag}mv"]; rstd = ctx.sb[f"{tag}rs"]
    xn = [ctx.sb[f"{tag}xn0"], ctx.sb[f"{tag}xn1"]]
    fm = [ctx.sb[f"{tag}fm0"], ctx.sb[f"{tag}fm1"]]
    ptr = [ctx.psum_get("ptr0", [128, 128], BF16),
           ctx.psum_get("ptr1", [128, 128], BF16)]
    ident = ctx.sb["ident"]
    ld = [ctx.sems["ld0"], ctx.sems["ld1"]]
    st = [ctx.sems["st0"], ctx.sems["st1"]]
    ve, sc, mm = (ctx.sems[k] for k in ("ve", "sc", "mm"))
    M = ctx.marks

    for i in range(ntiles):
        b = i % 2
        if i >= 2:
            r.sync.wait(*M[(tag, i - 2, "xtfree")])
        r.sync.dma(xt[b][:], ctx.dram[src][i * 128:(i + 1) * 128, :], ld[b])
        ldmark = ld[b].n
        r.vector.wait(ld[b], ldmark)
        for c in range(D // 512):
            r.vector.bn_stats(bn[:, 6 * c:6 * c + 6], xt[b][:, c * 512:(c + 1) * 512])
        r.vector.drain()
        r.vector.bn_aggr(mv[:], bn[:])
        r.vector.drain()
        r.vector.tensor_scalar_add(mv[:, 1:2], mv[:, 1:2], eps)
        r.vector.drain().then_inc(ve, 1); ve.inc(1)
        r.scalar.wait(ve)
        r.scalar.activation(rstd[:], mv[:, 1:2], AF.Ln, bias=0.0, scale=1.0)
        r.scalar.drain()
        r.scalar.activation(rstd[:], rstd[:], AF.Exp, bias=0.0, scale=-0.5)
        r.scalar.drain().then_inc(sc, 1); sc.inc(1)
        r.vector.wait(sc)
        if i >= 2:
            r.vector.wait(*M[(tag, i - 2, "xnfree")])
        if gb is None:
            r.vector.tensor_scalar(xn[b][:], xt[b][:], mv[:, 0:1], rstd[:, 0:1],
                                   op0=ALU.subtract, op1=ALU.mult)
        else:
            gbc, bbc = ctx.sb[gb[0]], ctx.sb[gb[1]]
            r.vector.tensor_scalar(tmp[:], xt[b][:], mv[:, 0:1], rstd[:, 0:1],
                                   op0=ALU.subtract, op1=ALU.mult) \
                .then_inc(ve, 1)
            ve.inc(1)
            r.gpsimd.wait(ve)
            r.gpsimd.tensor_mul(tmp[:], tmp[:], gbc[:])
            r.gpsimd.drain()
            r.gpsimd.tensor_add(xn[b][:], tmp[:], bbc[:])
            r.gpsimd.drain().then_inc(ctx.sems["gp"], 1)
            ctx.sems["gp"].inc(1)
            r.vector.wait(ctx.sems["gp"])
        r.vector.drain().then_inc(ve, 1); ve.inc(1)
        M[(tag, i, "xtfree")] = (ve, ve.n)
        if extra_out is not None:
            r.sync.wait(ve)
            r.sync.dma(ctx.dram[extra_out][i * 128:(i + 1) * 128, 0:D], xn[b][:], ld[b])
            r.scalar.wait(ve)
            vls = ctx.sems[f"vl{b}"]
            r.scalar.dma(ctx.dram[extra_out][i * 128:(i + 1) * 128, D:2 * D], xn[b][:],
                         vls)
            M[(tag, i, "xnextra")] = (vls, vls.n)
        if store_q != "scalar":
            pass
        r.tensor.wait(ve)
        for j in range(DT):
            pb = j % 2
            if j >= 2:
                r.tensor.wait(*M[(tag, i, "evac", j - 2)])

            r.tensor.transpose(ptr[pb][:], xn[b][:, j * 128:(j + 1) * 128],
                               ident[:]).then_inc(mm, 1)
            mm.inc(1)
            r.vector.wait(mm)
            if j == 0 and i >= 2:
                r.vector.wait(*M[(tag, i - 2, "stored")])
            r.vector.tensor_copy(fm[b][:, j * 128:(j + 1) * 128], ptr[pb][:]) \
                .then_inc(ve, 1)
            ve.inc(1)
            M[(tag, i, "evac", j)] = (ve, ve.n)
        if extra_out is not None:
            M[(tag, i, "xnfree")] = M[(tag, i, "xnextra")]
        else:
            M[(tag, i, "xnfree")] = (mm, mm.n)
        getattr(r, store_q).wait(ve)
        ng = max(1, DT // 2)
        for g in range(ng):
            nj = min(2, DT - g * 2)
            sbap = fm[b][:, g * 2 * 128:(g * 2 + nj) * 128] \
                .rearrange("p (j c) -> p j c", c=128)
            dap = ctx.dram[dst][g * 2 * 128:(g * 2 + nj) * 128,
                                i * 128:(i + 1) * 128] \
                .rearrange("(j p) c -> p j c", p=128)
            getattr(r, store_q).dma(dap, sbap, st[b])
        M[(tag, i, "stored")] = (st[b], st[b].n)


def emit_proj_tm(ctx, tag, src_fm, T, wname, col0, ncols, grp, post,
                 tcw=512, extra_mm=None):
    cfg, r = ctx.cfg, ctx.rec
    D, DT = cfg.D, cfg.DT
    w_sb = [ctx.sb["pjw0"], ctx.sb["pjw1"]]
    xnb = [ctx.sb["pjx0"], ctx.sb["pjx1"]]
    ld = [ctx.sems["ld0"], ctx.sems["ld1"]]
    wl = [ctx.sems["wl0"], ctx.sems["wl1"]]
    mm = ctx.sems["mm"]
    M = ctx.marks
    grp = 512
    nGRP = (ncols + grp - 1) // grp
    nTC = (T + tcw - 1) // tcw
    psums = [ctx.ps["pj0"], ctx.ps["pj1"], ctx.ps["pj2"], ctx.ps["pj3"]]
    pcnt = 0

    for g in range(nGRP):
        gw = min(grp, ncols - g * grp)
        wb = g % 2
        if ("pjwfree", wb) in M:
            r.sync.wait(*M[("pjwfree", wb)])
        for jg in range(0, DT, 4):
            nj = min(4, DT - jg)
            dap = ctx.dram[wname][jg * 128:(jg + nj) * 128,
                                  col0 + g * grp: col0 + g * grp + gw] \
                .rearrange("(j p) c -> p j c", p=128)
            sap = w_sb[wb][:, jg * gw:(jg + nj) * gw] \
                .rearrange("p (j c) -> p j c", c=gw)
            r.sync.dma(sap, dap, wl[wb])
        wmark = wl[wb].n
        for tc in range(nTC):
            b = tc % 2
            if ("pjxfree", b) in M:
                r.sync.wait(*M[("pjxfree", b)])
                r.vector.wait(*M[("pjxfree", b)])
            for jg in range(0, DT, 2):
                nj = min(2, DT - jg)
                dap = ctx.dram[src_fm][jg * 128:(jg + nj) * 128,
                                       tc * tcw:tc * tcw + tcw] \
                    .rearrange("(j p) c -> p j c", p=128)
                sap = xnb[b][:, jg * tcw:(jg + nj) * tcw] \
                    .rearrange("p (j c) -> p j c", c=tcw)
                r.sync.dma(sap, dap, ld[b])
            xmark = ld[b].n
            ntt = tcw // 128
            for tti in range(ntt):
                tt = tc * ntt + tti
                r.tensor.wait(ld[b], xmark)
                if tc == 0 and tti == 0:
                    r.tensor.wait(wl[wb], wmark)
                pb = pcnt % 4
                pcnt += 1
                if ("pjpfree", pb) in M:
                    r.tensor.wait(*M[("pjpfree", pb)])
                last = DT - 1 if extra_mm is None else DT
                for j in range(DT):
                    lhs = xnb[b][:, j * tcw + tti * 128:
                                 j * tcw + (tti + 1) * 128]
                    rhs = w_sb[wb][:, j * gw: j * gw + gw]
                    mmi = r.tensor.matmul(psums[pb][:, 0:gw], lhs, rhs,
                                          start=(j == 0), stop=(j == last))
                if extra_mm is not None:
                    mmi = extra_mm(tt, g * grp, psums[pb][:, 0:gw], gw)
                mmi.then_inc(mm, 1)
                mm.inc(1)
                M[("pjpfree", pb)] = post(tt, g * grp, psums[pb][:, 0:gw], gw)
            M[("pjxfree", b)] = (mm, mm.n)
        M[("pjwfree", wb)] = (mm, mm.n)


def build_program(cfg, phases=("all",)):
    nc = bass.Bass("TRN2", target_bir_lowering=False, num_devices=8)
    D, S, TOWN = cfg.D, cfg.S, cfg.TOWN
    ctx = Ctx(nc, cfg)

    ctx.din("x_seq", [S, D]); ctx.din("x_own", [TOWN, D])
    ctx.din("role", [1, 1], I32); ctx.din("posrow", [1, TOWN])
    ctx.din("wqkvT", [D, 3 * D]); ctx.din("woutT", [D, D]); ctx.din("wbT", [D, D])
    ctx.din("wrT", [D, 8]); ctx.din("laT", [D, 128]); ctx.din("lbC", [128, D])
    ctx.din("bbase", [1, D])
    for nm in ("qlnw", "qlnb", "klnw", "klnb"):
        ctx.din(nm, [1, D])
    ctx.din("cosFb", [D, cfg.KP], BF16); ctx.din("sinFb", [D, cfg.KP], BF16)
    ctx.din("invTb", [cfg.KP, D], BF16)
    ctx.din("iota_kt", [128, cfg.ST])
    ctx.din("rowbase", [128, cfg.OT], F32)
    for o in range(cfg.QB // 128):
        ctx.din(f"mask{o}", [128, cfg.QB], BF16)

    ctx.dint("xnfm", [D, S], BF16)
    ctx.dint("xnofm", [D, TOWN], BF16)
    ctx.dint("ktm", [S, D], BF16)
    ctx.dint("vtm", [S, D], BF16)
    ctx.dint("qtm", [TOWN, D], BF16)
    ctx.dint("kfm", [D, S], BF16)
    ctx.dint("qfm", [D, TOWN], BF16)
    ctx.dint("ofm", [D, TOWN], BF16)
    ctx.dint("xr1", [TOWN, D]); ctx.dint("xr2", [TOWN, D]); ctx.dint("xr3", [TOWN, D])
    ctx.dint("vdbl", [TOWN, 2 * D], BF16)
    ctx.dint("vfm", [D, TOWN], BF16)
    ctx.dint("corrh", [TOWN, D], BF16)
    ctx.dint("Ph", [cfg.KP, TOWN], BF16)
    ctx.dint("wqkvTb", [D, 3 * D], BF16)
    ctx.dint("woutTb", [D, D], BF16)
    ctx.dint("wbTb", [D, D], BF16)
    ctx.dint("laTb", [D, 128], BF16)
    ctx.dint("wrTb", [D, 8], BF16)
    ctx.dint("lbCb", [128, D], BF16)

    ctx.dout("y", [TOWN, D])

    for nm in ("ld", "ld0", "ld1", "ld2", "st0", "st1", "wl0", "wl1",
               "rl0", "rl1", "rl2", "rl3", "st2", "st3",
               "vl0", "vl1", "cs0", "cs1", "ve", "sc", "mm", "gp",
               "gld", "gst",
               "b_sync", "b_vector", "b_scalar", "b_tensor", "b_gpsimd"):
        ctx.sem(nm)

    ctx.reserve_sbuf(PSIZE)
    ctx.asb("ident", [128, 128], BF16, 0)
    ctx.asb("onescol", [128, 1], BF16, 256)
    ctx.asb("onesrow", [1, 128], BF16, 288)
    ctx.asb("onesrowf", [1, 128], F32, 8832)
    ctx.asb("iotasb", [128, max(cfg.ST, 2)], F32, 9344)
    ctx.asb("rolesb", [1, 1], I32, 576)
    ctx.asb("posrsb", [1, max(D, 2048)], F32, 640)
    ctx.asb("rbf", [1, D], F32, PBASE)
    ctx.asb("rbb", [1, D], BF16, PBASE + D * 4)

    GBB = PSIZE - 4 * (D * 2 + 64) - 512
    ctx.asb("qgbc", [128, D], BF16, GBB)
    ctx.asb("qbbc", [128, D], BF16, GBB + (D * 2 + 64))
    ctx.asb("kgbc", [128, D], BF16, GBB + 2 * (D * 2 + 64))
    ctx.asb("kbbc", [128, D], BF16, GBB + 3 * (D * 2 + 64))

    L = ctx.layout(PBASE, [
        ("lnxt0", [128, D], F32), ("lnxt1", [128, D], F32),
        ("lntmp", [128, D], F32),
        ("lnxn0", [128, D], BF16), ("lnxn1", [128, D], BF16),
        ("lnfm0", [128, D], BF16), ("lnfm1", [128, D], BF16),
        ("lnxtb0", [128, D], BF16), ("lnxtb1", [128, D], BF16),
        ("lnbn", [128, 6 * (D // 512)], F32),
        ("lnmv", [128, 2], F32), ("lnrs", [128, 1], F32),
    ])
    LN_END = L["_end"]
    C = ctx.layout(LN_END, [
        ("wcf0", [128, 2048], F32), ("wcf1", [128, 2048], F32),
        ("wcb0", [128, 2048], BF16), ("wcb1", [128, 2048], BF16),
    ])
    ctx.sb["wcf"] = [ctx.sb["wcf0"], ctx.sb["wcf1"]]
    ctx.sb["wcb"] = [ctx.sb["wcb0"], ctx.sb["wcb1"]]
    assert C["_end"] <= GBB

    GRP = 512
    P = ctx.layout(PBASE, [
        ("pjw0", [128, (D // 128) * GRP], BF16),
        ("pjw1", [128, (D // 128) * GRP], BF16),
        ("pjx0", [128, (D // 128) * 512], BF16),
        ("pjx1", [128, (D // 128) * 512], BF16),
        ("pjo0", [128, 512], BF16), ("pjo1", [128, 512], BF16),
        ("pjo2", [128, 512], BF16), ("pjo3", [128, 512], BF16),
        ("pjr0", [128, 512], F32), ("pjr1", [128, 512], F32),
        ("pjq0", [128, 512], F32), ("pjq1", [128, 512], F32),
        ("pjr2", [128, 512], F32), ("pjr3", [128, 512], F32),
        ("pjq2", [128, 512], F32), ("pjq3", [128, 512], F32),
    ])
    assert P["_end"] <= GBB and L["_end"] <= GBB
    FB = ctx.layout(P["_end"], [
        ("bbb", [128, D], F32),
        ("flb", [128, D], BF16),
        ("fla", [128, D], BF16),
        ("fwr", [128, cfg.DT * 8], BF16),
        ("frts", [128, 16], F32),
        ("fe8", [128, 8], F32),
        ("fp8", [128, 16], F32),
        ("fw8", [128, 8], F32),
        ("fhs", [128, 128], F32),
        ("fgb", [128, 128], BF16),
        ("fgfm", [128, TOWN], BF16),
    ])
    assert FB["_end"] <= PSIZE
    ABF = ctx.layout(PBASE, [
        ("abc0", [128, 256], BF16), ("abc1", [128, 256], BF16),
        ("absq1", [128, TOWN], F32), ("absq2", [128, TOWN], F32),
        ("abps0", [128, TOWN], BF16), ("abps1", [128, TOWN], BF16),
        ("abvr", [128, cfg.DT * TOWN], BF16),
    ])
    assert ABF["_end"] <= PSIZE
    ABI = ctx.layout(PBASE, [
        ("abpt0", [128, 128], BF16), ("abpt1", [128, 128], BF16),
        ("abcs", [128, 2 * min(2048, D)], BF16),
        ("abinv", [128, cfg.KT * min(2048, D)], BF16),
    ])
    assert ABI["_end"] <= PSIZE
    AG = ctx.layout(PBASE, [
        ("abct0", [128, D], BF16), ("abct1", [128, D], BF16),
        ("abt8", [128, 24], F32),
        ("abi8", [128, 8], U32),
        ("abw8", [128, 8], F32),
        ("aboff", [128, 8], U32),
        ("abrb", [128, cfg.OT], F32),
        ("abof", [128, 8], F32),
        ("abacc", [128, D], F32),
        ("abg0", [128, D], BF16), ("abg1", [128, D], BF16),
    ])
    assert AG["_end"] <= PSIZE
    A = ctx.layout(PBASE, [
        ("atK0", [128, S], BF16), ("atK1", [128, S], BF16),
        ("atK2", [128, S], BF16),
        ("atV0", [128, S], BF16), ("atV1", [128, S], BF16),
        ("atV2", [128, S], BF16),
        ("atQ0", [128, TOWN], BF16), ("atQ1", [128, TOWN], BF16),
        ("atQ2", [128, TOWN], BF16),
        ("atsq0", [128, S + TOWN], BF16), ("atsq1", [128, S + TOWN], BF16),
        ("atp0", [128, cfg.QB], BF16), ("atp1", [128, cfg.QB], BF16),
        ("atp2", [128, cfg.QB], BF16),
        ("atncb0", [1, TOWN], BF16), ("atncb1", [1, TOWN], BF16),
        ("atncb2", [1, TOWN], BF16),
        ("atbt0", [128, 4 * cfg.ST], F32), ("atbt1", [128, 4 * cfg.ST], F32),
        ("atbt2", [128, 4 * cfg.ST], F32),
        ("atnrm", [1, S + TOWN + 4], F32),
        ("atkm", [1, 64], F32), ("atqm", [1, 64], F32),
        ("atbo", [1, 64], F32), ("atbc", [128, 64], F32),
        ("atsd0", [128, cfg.QB], F32), ("atsd1", [128, cfg.QB], F32),
        ("atsd2", [128, cfg.QB], F32),
        ("atlsb0", [1, cfg.QB], F32), ("atlsb1", [1, cfg.QB], F32),
        ("atlbc", [128, cfg.QB], F32),
        ("atob0", [128, cfg.QB], BF16), ("atob1", [128, cfg.QB], BF16),
    ] + [(f"msk{o}", [128, cfg.QB], BF16) for o in range(cfg.QB // 128)])
    assert A["_end"] <= GBB

    ctx.apsum("ptr0", [128, 128], BF16)
    ctx.apsum("ptr1", [128, 128], BF16)
    ctx.apsum("rbp0", [128, 512], F32)
    ctx.apsum("rbp1", [128, 512], F32)
    ctx.ps["rbp"] = [ctx.ps["rbp0"], ctx.ps["rbp1"]]
    for k in range(4):
        ctx.apsum(f"pj{k}", [128, 512], F32)

    r = ctx.rec
    r.sync.dma(ctx.sb["rolesb"][:], ctx.dram["role"], ctx.sems["ld"])
    emit_prelude(ctx)

    allp = "all" in phases

    if allp or "cast" in phases:
        emit_row_broadcast(ctx, [
            ("qlnw", "qgbc", D, HSC), ("qlnb", "qbbc", D, HSC),
            ("klnw", "kgbc", D, 1.0), ("klnb", "kbbc", D, 1.0),
        ])
        ctx.barrier()
        emit_cast_weights(ctx, [
            ("wqkvT", "wqkvTb", D, 3 * D),
            ("woutT", "woutTb", D, D),
            ("wbT", "wbTb", D, D),
            ("laT", "laTb", D, 128),
            ("wrT", "wrTb", D, 8),
            ("lbC", "lbCb", 128, D),
        ])

    if allp or "ln1" in phases:
        emit_ln_transpose(ctx, "x_seq", "xnfm", cfg.ST, "ln")
        ctx.barrier()
        emit_ln_transpose(ctx, "x_own", "xnofm", cfg.OT, "ln")
        ctx.barrier()

    if allp or "qkv" in phases:
        emit_qkv(ctx)
        ctx.barrier()
        emit_ln_transpose(ctx, "ktm", "kfm", cfg.ST, "ln", src_dt=BF16,
                          gb=("kgbc", "kbbc"))
        ctx.barrier()
        emit_ln_transpose(ctx, "qtm", "qfm", cfg.OT, "ln", src_dt=BF16,
                          gb=("qgbc", "qbbc"))
        ctx.barrier()

    if allp or "attn" in phases:
        emit_attention(ctx)
        ctx.barrier()

    if allp or "wout" in phases:
        ctx.psum_free_all()
        for k in range(4):
            ctx.apsum(f"pj{k}", [128, 512], F32)
        emit_wout(ctx)
        ctx.barrier()

    if allp or "ffn1" in phases:
        emit_ffn(ctx, "xr1", "xr2", "f1")
        ctx.barrier()

    if allp or "attnb" in phases:
        emit_attnb(ctx)
        ctx.barrier()

    if allp or "ffn2" in phases:
        emit_ffn(ctx, "xr3", "y", "f2")
        ctx.barrier()

    with nc.Block() as block:
        @block.sync
        def _(eng):
            _replay_role(ctx, "sync", eng)

        @block.vector
        def _(eng):
            _replay_role(ctx, "vector", eng)

        @block.scalar
        def _(eng):
            _replay_role(ctx, "scalar", eng)

        @block.tensor
        def _(eng):
            _replay_role(ctx, "tensor", eng)

        @block.gpsimd
        def _(eng):
            _replay_role(ctx, "gpsimd", eng)
    return nc, ctx


def emit_qkv(ctx):
    cfg, r = ctx.cfg, ctx.rec
    D, S, TOWN = cfg.D, cfg.S, cfg.TOWN
    ve = ctx.sems["ve"]
    st = [ctx.sems["st0"], ctx.sems["st1"], ctx.sems["st2"], ctx.sems["st3"]]
    ob = [ctx.sb["pjo0"], ctx.sb["pjo1"], ctx.sb["pjo2"], ctx.sb["pjo3"]]
    M = ctx.marks

    def mk_post(dst_name):
        cnt = [0]

        def post(tt, oc_abs, pap, ow):
            b = cnt[0] % 4
            cnt[0] += 1
            r.vector.wait(ctx.sems["mm"])
            if (dst_name, "of", b) in M:
                r.vector.wait(*M[(dst_name, "of", b)])
            r.vector.tensor_scalar(ob[b][:, 0:ow], pap, -CLIP, CLIP,
                                   op0=ALU.max, op1=ALU.min) \
                .then_inc(ve, 1)
            ve.inc(1)
            pfree = (ve, ve.n)
            r.scalar.wait(ve)
            dst = ctx.dram[dst_name]
            r.scalar.dma(dst[tt * 128:(tt + 1) * 128, oc_abs:oc_abs + ow],
                         ob[b][:, 0:ow], st[b])
            M[(dst_name, "of", b)] = (st[b], st[b].n)
            return pfree
        return post

    emit_proj_tm(ctx, "k", "xnfm", S, "wqkvTb", D, D, 1024, mk_post("ktm"))
    ctx.barrier()
    emit_proj_tm(ctx, "v", "xnfm", S, "wqkvTb", 2 * D, D, 1024, mk_post("vtm"))
    ctx.barrier()
    emit_proj_tm(ctx, "q", "xnofm", TOWN, "wqkvTb", 0, D, 1024, mk_post("qtm"))


ROLE_ENGINES = ("sync", "vector", "scalar", "tensor")


def emit_attention(ctx):
    cfg, r = ctx.cfg, ctx.rec
    emit_attn_prep(ctx, host_constants(ctx.cfg))
    ctx.barrier()
    ctx.psum_free_all()
    for nm2 in ("atS0", "atS1", "atS2", "atO0", "atO1", "atLB"):
        ctx.apsum(nm2, [128, cfg.QB], F32)
    ctx.apsum("atL0", [1, cfg.QB], F32)
    ctx.apsum("atL1", [1, cfg.QB], F32)
    ld = ctx.sems["ld"]
    for o in range(cfg.QB // 128):
        r.sync.dma(ctx.sb[f"msk{o}"][:], ctx.dram[f"mask{o}"], ld)
    r.sync.dma(ctx.sb["iotasb"][:], ctx.dram["iota_kt"], ld)
    r.sync.dma(ctx.sb["posrsb"][0:1, 0:cfg.TOWN], ctx.dram["posrow"], ld)
    ldv = ld.n
    for e in ROLE_ENGINES:
        getattr(r, e).wait_ge(ld.h, ldv)

    cst = host_constants(cfg)
    for e in ROLE_ENGINES:
        getattr(r, e).IF_ROLE(0)
    snap = ctx.sem_snapshot()
    msnap = dict(ctx.marks)
    _emit_attn_role(ctx, 0, cst)
    end0 = ctx.sem_snapshot()
    ctx.sem_restore(snap)
    ctx.marks = msnap
    for e in ROLE_ENGINES:
        getattr(r, e).ELSE_ROLE()
    _emit_attn_role(ctx, 1, cst)
    end1 = ctx.sem_snapshot()
    assert end0 == end1, f"role sem mismatch: { {k: (end0[k], end1[k]) for k in end0 if end0[k] != end1[k]} }"
    for e in ROLE_ENGINES:
        getattr(r, e).ENDIF_ROLE()


def emit_attn_prep(ctx, cst):
    cfg, r = ctx.cfg, ctx.rec
    S, TOWN = cfg.S, cfg.TOWN
    ve, sc, mm = ctx.sems["ve"], ctx.sems["sc"], ctx.sems["mm"]
    ld = [ctx.sems["ld0"], ctx.sems["ld1"]]
    M = ctx.marks
    ctx.psum_free_all()
    pp = ctx.apsum("app", [1, 512], F32)
    pb = ctx.apsum("apb", [128, 64], F32)
    K2 = [ctx.sb["atK0"], ctx.sb["atK1"]]
    Q2 = [ctx.sb["atQ0"], ctx.sb["atQ1"]]
    sq = [ctx.sb["atsq0"], ctx.sb["atsq1"]]
    nrm = ctx.sb["atnrm"]
    km = ctx.sb["atkm"]; qm = ctx.sb["atqm"]
    onescol = ctx.sb["onescol"]; onesrowf = ctx.sb["onesrowf"]
    CW = 512
    for h in range(NH):
        b = h % 2
        if ("ap", b) in M:
            r.sync.wait(*M[("ap", b)])
        r.sync.dma(K2[b][:, 0:S], ctx.dram["kfm"][h * 128:(h + 1) * 128, :], ld[b])
        r.sync.dma(Q2[b][:, 0:TOWN], ctx.dram["qfm"][h * 128:(h + 1) * 128, :], ld[b])
        r.scalar.wait(ld[b])
        if ("apsq", b) in M:
            r.scalar.wait(*M[("apsq", b)])
        r.scalar.activation(sq[b][:, 0:S], K2[b][:, 0:S], AF.Square,
                            bias=0.0, scale=1.0)
        r.scalar.drain()
        r.scalar.activation(sq[b][:, S:S + TOWN], Q2[b][:, 0:TOWN], AF.Square,
                            bias=0.0, scale=1.0)
        r.scalar.drain().then_inc(sc, 1); sc.inc(1)
        r.tensor.wait(sc)
        for c in range((S + TOWN) // CW):
            if ("applb",) in M:
                r.tensor.wait(*M[("applb",)])
            r.tensor.matmul(pp[0:1, 0:CW], onescol[:, 0:1],
                            sq[b][:, c * CW:(c + 1) * CW],
                            start=True, stop=True).then_inc(mm, 1)
            mm.inc(1)
            r.vector.wait(mm)
            r.vector.tensor_copy(nrm[0:1, c * CW:(c + 1) * CW], pp[0:1, 0:CW]) \
                .then_inc(ve, 1)
            ve.inc(1)
            M[("applb",)] = (ve, ve.n)
        M[("apsq", b)] = (mm, mm.n)
        M[("ap", b)] = (mm, mm.n)
        r.vector.drain()
        r.vector.reduce_max(km[0:1, h:h + 1], nrm[0:1, 0:S], axis=AX.X)
        r.vector.drain()
        r.vector.reduce_max(qm[0:1, h:h + 1], nrm[0:1, S:S + TOWN], axis=AX.X)
        r.vector.drain()
    ii = r.vector.nop(); ii.then_inc(ve, 1); ve.inc(1)
    r.scalar.wait(ve)
    r.scalar.activation(km[0:1, 0:NH], km[0:1, 0:NH], AF.Ln, bias=0.0, scale=1.0)
    r.scalar.drain()
    r.scalar.activation(qm[0:1, 0:NH], qm[0:1, 0:NH], AF.Ln, bias=0.0, scale=1.0)
    r.scalar.drain().then_inc(sc, 1); sc.inc(1)
    r.vector.wait(sc)
    r.vector.tensor_add(km[0:1, 0:NH], km[0:1, 0:NH], qm[0:1, 0:NH])
    r.vector.drain().then_inc(ve, 1); ve.inc(1)
    r.scalar.wait(ve)
    r.scalar.activation(ctx.sb["atbo"][0:1, 0:NH], km[0:1, 0:NH], AF.Exp,
                        bias=0.0, scale=0.5)
    r.scalar.drain().then_inc(sc, 1); sc.inc(1)
    r.tensor.wait(sc)
    r.tensor.matmul(pb[:, 0:NH], onesrowf[0:1, :], ctx.sb["atbo"][0:1, 0:NH],
                    start=True, stop=True).then_inc(mm, 1)
    mm.inc(1)
    r.vector.wait(mm)
    r.vector.tensor_copy(ctx.sb["atbc"][:, 0:NH], pb[:, 0:NH])
    r.vector.drain().then_inc(ve, 1); ve.inc(1)


def _emit_attn_role(ctx, role, cst):
    cfg, r = ctx.cfg, ctx.rec
    D, S, TOWN, QB = cfg.D, cfg.S, cfg.TOWN, cfg.QB
    QBT = QB // 128
    qpos = cfg.QPOS[role]
    M = ctx.marks
    ve, sc, mm = ctx.sems["ve"], ctx.sems["sc"], ctx.sems["mm"]
    ld = [ctx.sems["ld0"], ctx.sems["ld1"], ctx.sems["ld2"]]
    st = [ctx.sems["st0"], ctx.sems["st1"]]
    sp = [ctx.ps["atS0"], ctx.ps["atS1"], ctx.ps["atS2"]]
    op = [ctx.ps["atO0"], ctx.ps["atO1"]]
    lp = [ctx.ps["atL0"], ctx.ps["atL1"]]
    lb = ctx.ps["atLB"]
    onesrow = ctx.sb["onesrow"]; onescol = ctx.sb["onescol"]
    onesrowf = ctx.sb["onesrowf"]
    Ksb = [ctx.sb["atK0"], ctx.sb["atK1"], ctx.sb["atK2"]]
    Vsb = [ctx.sb["atV0"], ctx.sb["atV1"], ctx.sb["atV2"]]
    Qsb = [ctx.sb["atQ0"], ctx.sb["atQ1"], ctx.sb["atQ2"]]
    sq = [ctx.sb["atsq0"], ctx.sb["atsq1"]]
    pT = [ctx.sb["atp0"], ctx.sb["atp1"], ctx.sb["atp2"]]
    sd = [ctx.sb["atsd0"], ctx.sb["atsd1"], ctx.sb["atsd2"]]
    negcb = [ctx.sb["atncb0"], ctx.sb["atncb1"], ctx.sb["atncb2"]]
    biast = [ctx.sb["atbt0"], ctx.sb["atbt1"], ctx.sb["atbt2"]]
    nrm = ctx.sb["atnrm"]
    lsb = [ctx.sb["atlsb0"], ctx.sb["atlsb1"]]
    lbc = ctx.sb["atlbc"]
    osb = [ctx.sb["atob0"], ctx.sb["atob1"]]
    skipk1 = [float(cst["slopes"][h]) * (QB - 1) <= 20.0 for h in range(NH)]

    def mini_prep(h, hb):
        slope = float(cst["slopes"][h])
        if (("at", hb, "ncfree") in M):
            r.vector.wait(*M[("at", hb, "ncfree")])
        r.vector.tensor_scalar(negcb[hb][0:1, 0:TOWN],
                               ctx.sb["posrsb"][0:1, 0:TOWN], -slope,
                               ctx.sb["atbo"][0:1, h:h + 1],
                               op0=ALU.mult, op1=ALU.subtract)
        r.vector.drain()
        if skipk1[h]:
            for bi in range(4):
                qbe = (qpos[bi] + 1) * QB - 1
                r.vector.tensor_scalar(
                    biast[hb][:, bi * cfg.ST:(bi + 1) * cfg.ST],
                    ctx.sb["iotasb"][:, 0:cfg.ST], slope,
                    ctx.sb["atbc"][:, h:h + 1],
                    op0=ALU.mult, op1=ALU.subtract)
                r.vector.drain()
                r.vector.tensor_scalar_add(
                    biast[hb][:, bi * cfg.ST:(bi + 1) * cfg.ST],
                    biast[hb][:, bi * cfg.ST:(bi + 1) * cfg.ST],
                    -slope * qbe)
                r.vector.drain()
        else:
            r.vector.tensor_scalar_mul(biast[hb][:, 0:cfg.ST],
                                       ctx.sb["iotasb"][:, 0:cfg.ST], slope)
            r.vector.drain()
        ii = r.vector.nop()
        ii.then_inc(ve, 1); ve.inc(1)
        return ve.n

    def head_loads(h, hb):
        if (("at", hb, "kfree") in M):
            r.sync.wait(*M[("at", hb, "kfree")])
        r.sync.dma(Ksb[hb][:, 0:S], ctx.dram["kfm"][h * 128:(h + 1) * 128, :],
                   ld[hb])
        for vg in range(8):
            vs0, vs1 = vg * (S // 8), (vg + 1) * (S // 8)
            r.sync.dma(Vsb[hb][:, vs0:vs1].rearrange("p (kt c) -> p kt c", c=128),
                       ctx.dram["vtm"][vs0:vs1, h * 128:(h + 1) * 128]
                       .rearrange("(kt p) c -> p kt c", p=128), ld[hb])
        r.sync.dma(Qsb[hb][:, 0:TOWN], ctx.dram["qfm"][h * 128:(h + 1) * 128, :],
                   ld[hb])
        return ld[hb].n

    hld = {0: head_loads(0, 0), 1: head_loads(1, 1)}
    prep_ve = {0: mini_prep(0, 0)}

    for h in range(NH):
        hb = h % 3
        if h + 2 < NH:
            hld[h + 2] = head_loads(h + 2, (h + 2) % 3)
        if h + 1 < NH:
            prep_ve[h + 1] = mini_prep(h + 1, (h + 1) % 3)
        r.tensor.wait(ve, prep_ve[h])
        r.tensor.wait(ld[hb], hld[h])
        kiter = 0
        pend = None
        for bi in range(4):
            p = qpos[bi]
            nkt = (p + 1) * QBT
            ob2 = bi % 2
            for kt in range(nkt):
                s2 = kiter % 3
                if (("at", s2, "spfree") in M):
                    r.tensor.wait(*M[("at", s2, "spfree")])
                if (("at", s2, "ptfree") in M):
                    r.tensor.wait(*M[("at", s2, "ptfree")])
                if kt == 0 and (("at", ob2, "opfree") in M):
                    r.tensor.wait(*M[("at", ob2, "opfree")])
                if skipk1[h]:
                    r.tensor.matmul(sp[s2][:, 0:QB],
                                    Ksb[hb][:, kt * 128:(kt + 1) * 128],
                                    Qsb[hb][:, bi * QB:(bi + 1) * QB],
                                    start=True, stop=True).then_inc(mm, 1)
                else:
                    r.tensor.matmul(sp[s2][:, 0:QB], onesrow[0:1, :],
                                    negcb[hb][0:1, bi * QB:(bi + 1) * QB],
                                    start=True, stop=False)
                    r.tensor.matmul(sp[s2][:, 0:QB],
                                    Ksb[hb][:, kt * 128:(kt + 1) * 128],
                                    Qsb[hb][:, bi * QB:(bi + 1) * QB],
                                    start=False, stop=True).then_inc(mm, 1)
                mm.inc(1)
                diag = kt - (nkt - QBT)
                bcol = biast[hb][:, (bi * cfg.ST if skipk1[h] else 0) + kt:
                                 (bi * cfg.ST if skipk1[h] else 0) + kt + 1]
                if diag >= 0:
                    r.vector.wait(mm)
                    if ("at", s2, "sdfree") in M:
                        r.vector.wait(*M[("at", s2, "sdfree")])
                    r.vector.tensor_sub(sd[s2][:, 0:QB], sp[s2][:, 0:QB],
                                        ctx.sb[f"msk{diag}"][:]).then_inc(ve, 1)
                    ve.inc(1)
                    M[("at", s2, "spfree")] = (ve, ve.n)
                    r.scalar.wait(ve)
                    r.scalar.activation(pT[s2][:, 0:QB], sd[s2][:, 0:QB],
                                        AF.Exp, bias=bcol, scale=1.0) \
                        .then_inc(sc, 1)
                    sc.inc(1)
                    M[("at", s2, "sdfree")] = (sc, sc.n)
                else:
                    r.scalar.wait(mm)
                    r.scalar.activation(pT[s2][:, 0:QB], sp[s2][:, 0:QB], AF.Exp,
                                        bias=bcol, scale=1.0).then_inc(sc, 1)
                    sc.inc(1)
                    M[("at", s2, "spfree")] = (sc, sc.n)
                expmark = sc.n
                if pend is not None:
                    _pv(ctx, r, pend, Vsb, onescol, op, lp, pT, mm, M)
                pend = (pend, hb, bi, kt, nkt, kiter % 3, expmark)
                kiter += 1
            _pv(ctx, r, pend, Vsb, onescol, op, lp, pT, mm, M)
            pend = None
            lq = bi % 2
            r.vector.wait(mm)
            if ("at", lq, "lsfree") in M:
                r.vector.wait(*M[("at", lq, "lsfree")])
            r.vector.tensor_scalar_add(lsb[lq][0:1, 0:QB], lp[lq][0:1, 0:QB],
                                       1e-30)
            r.vector.drain()
            r.vector.reciprocal(lsb[lq][0:1, 0:QB], lsb[lq][0:1, 0:QB])
            r.vector.drain().then_inc(ve, 1); ve.inc(1)
            M[("at", lq, "lpfree")] = (ve, ve.n)
            r.tensor.wait(ve)
            if ("at", "lbfree") in M:
                r.tensor.wait(*M[("at", "lbfree")])
            r.tensor.matmul(lb[:, 0:QB], onesrowf[0:1, :], lsb[lq][0:1, 0:QB],
                            start=True, stop=True).then_inc(mm, 1)
            mm.inc(1)
            M[("at", lq, "lsfree")] = (mm, mm.n)
            ob = (h * 4 + bi) % 2
            r.vector.wait(mm)
            if (("at", ob, "obfree") in M):
                r.vector.wait(*M[("at", ob, "obfree")])
            r.vector.tensor_copy(lbc[:, 0:QB], lb[:, 0:QB])
            r.vector.drain()
            M[("at", "lbfree")] = (ve, ve.n + 1)
            r.vector.tensor_mul(osb[ob][:, 0:QB], op[ob2][:, 0:QB], lbc[:, 0:QB]) \
                .then_inc(ve, 1)
            ve.inc(1)
            M[("at", ob2, "opfree")] = (ve, ve.n)
            r.scalar.wait(ve)
            r.scalar.dma(ctx.dram["ofm"][h * 128:(h + 1) * 128,
                                         bi * QB:(bi + 1) * QB],
                         osb[ob][:, 0:QB], st[ob])
            M[("at", ob, "obfree")] = (st[ob], st[ob].n)
        M[("at", hb, "kfree")] = (mm, mm.n)
        M[("at", hb, "ncfree")] = (mm, mm.n)


def _pv(ctx, r, pend, Vsb, onescol, op, lp, pT, mm, M):
    _, hb, bi, kt, nkt, s2, expmark = pend
    QB = ctx.cfg.QB
    r.tensor.wait(ctx.sems["sc"], expmark)
    if kt == 0:
        lq = bi % 2
        if ("at", lq, "lpfree") in M:
            r.tensor.wait(*M[("at", lq, "lpfree")])
    r.tensor.matmul(op[bi % 2][:, 0:QB], Vsb[hb][:, kt * 128:(kt + 1) * 128],
                    pT[s2][:, 0:QB], start=(kt == 0), stop=(kt == nkt - 1))
    lmm = r.tensor.matmul(lp[bi % 2][0:1, 0:QB], onescol[:, 0:1],
                          pT[s2][:, 0:QB], start=(kt == 0), stop=(kt == nkt - 1))
    lmm.then_inc(mm, 1)
    mm.inc(1)
    M[("at", s2, "ptfree")] = (mm, mm.n)



def mk_post_residual(ctx, res_name, dst_name, add_bb=False):
    r = ctx.rec
    ve = ctx.sems["ve"]
    st = [ctx.sems["st0"], ctx.sems["st1"], ctx.sems["st2"], ctx.sems["st3"]]
    rl = [ctx.sems["rl0"], ctx.sems["rl1"], ctx.sems["rl2"], ctx.sems["rl3"]]
    resb = [ctx.sb["pjr0"], ctx.sb["pjr1"], ctx.sb["pjr2"], ctx.sb["pjr3"]]
    outb = [ctx.sb["pjq0"], ctx.sb["pjq1"], ctx.sb["pjq2"], ctx.sb["pjq3"]]
    M = ctx.marks
    cnt = [0]

    def post(tt, oc_abs, pap, ow):
        b = cnt[0] % 4
        cnt[0] += 1
        if ("pr", res_name, b) in M:
            r.sync.wait(*M[("pr", res_name, b)])
        r.sync.dma(resb[b][:, 0:ow],
                   ctx.dram[res_name][tt * 128:(tt + 1) * 128,
                                      oc_abs:oc_abs + ow], rl[b])
        r.vector.wait(rl[b])
        r.vector.wait(ctx.sems["mm"])
        if ("po", dst_name, b) in M:
            r.vector.wait(*M[("po", dst_name, b)])
        r.vector.tensor_add(outb[b][:, 0:ow], pap, resb[b][:, 0:ow])
        if add_bb:
            r.vector.drain()
            r.vector.tensor_add(outb[b][:, 0:ow], outb[b][:, 0:ow],
                                ctx.sb["bbb"][:, oc_abs:oc_abs + ow])
        ir = r.vector.drain()
        ir.then_inc(ve, 1); ve.inc(1)
        pfree = (ve, ve.n)
        M[("pr", res_name, b)] = (ve, ve.n)
        r.scalar.wait(ve)
        r.scalar.dma(ctx.dram[dst_name][tt * 128:(tt + 1) * 128,
                                        oc_abs:oc_abs + ow],
                     outb[b][:, 0:ow], st[b])
        M[("po", dst_name, b)] = (st[b], st[b].n)
        return pfree
    return post


def emit_wout(ctx):
    cfg = ctx.cfg
    emit_proj_tm(ctx, "wo", "ofm", cfg.TOWN, "woutTb", 0, cfg.D, 1024,
                 mk_post_residual(ctx, "x_own", "xr1"))


def emit_ffn(ctx, xin, xout, tagn):
    cfg, r = ctx.cfg, ctx.rec
    D, TOWN, OT, DT = cfg.D, cfg.TOWN, cfg.OT, cfg.DT
    ve, sc, mm = ctx.sems["ve"], ctx.sems["sc"], ctx.sems["mm"]
    ld = [ctx.sems["ld0"], ctx.sems["ld1"]]
    M = ctx.marks

    emit_ln_transpose(ctx, xin, "xnofm", OT, "ln")
    ctx.barrier()
    ctx.psum_free_all()

    bbp = ctx.apsum("fbb", [128, 512], F32)
    r.sync.dma(ctx.sb["rbf"][0:1, 0:D], ctx.dram["bbase"], ctx.sems["ld"])
    for e in ENGINES:
        getattr(r, e).wait_ge(ctx.sems["ld"].h, ctx.sems["ld"].n)
    for c0 in range(0, D, 512):
        r.tensor.wait(ve)
        r.tensor.matmul(bbp[:, :], ctx.sb["onesrowf"][0:1, :],
                        ctx.sb["rbf"][0:1, c0:c0 + 512],
                        start=True, stop=True).then_inc(mm, 1)
        mm.inc(1)
        r.vector.wait(mm)
        r.vector.tensor_copy(ctx.sb["bbb"][:, c0:c0 + 512], bbp[:, :]) \
            .then_inc(ve, 1)
        ve.inc(1)
    r.sync.dma(ctx.sb["flb"][:], ctx.dram["lbCb"], ctx.sems["ld"])
    r.sync.dma(ctx.sb["fla"][:].rearrange("p (j c) -> p j c", c=128),
               ctx.dram["laTb"][:, :].rearrange("(j p) c -> p j c", p=128),
               ctx.sems["ld"])
    r.sync.dma(ctx.sb["fwr"][:].rearrange("p (j c) -> p j c", c=8),
               ctx.dram["wrTb"][:, :].rearrange("(j p) c -> p j c", p=128),
               ctx.sems["ld"])
    ldw = ctx.sems["ld"].n
    ctx.barrier()

    rtp = ctx.apsum("frt", [128, 8], F32)
    hp = ctx.apsum("fh", [128, 128], F32)
    gpp = ctx.apsum("fgp", [128, 128], BF16)
    rts = ctx.sb["frts"]; e8 = ctx.sb["fe8"]; p8 = ctx.sb["fp8"]
    w8 = ctx.sb["fw8"]; hs = ctx.sb["fhs"]; gb = ctx.sb["fgb"]
    gfm = ctx.sb["fgfm"]
    xnb = [ctx.sb["pjx0"], ctx.sb["pjx1"]]
    for tc in range(TOWN // 512):
        b = tc % 2
        if ("fx", b) in M:
            r.sync.wait(*M[("fx", b)])
        for jg in range(0, DT, 2):
            nj = min(2, DT - jg)
            dap = ctx.dram["xnofm"][jg * 128:(jg + nj) * 128,
                                    tc * 512:(tc + 1) * 512] \
                .rearrange("(j p) c -> p j c", p=128)
            sap = xnb[b][:, jg * 512:(jg + nj) * 512] \
                .rearrange("p (j c) -> p j c", c=512)
            r.sync.dma(sap, dap, ld[b])
        xm = ld[b].n
        for tti in range(4):
            tt = tc * 4 + tti
            r.tensor.wait(ld[b], xm)
            if tt == 0:
                r.tensor.wait(ctx.sems["ld"], ldw)
            if ("fpf",) in M:
                r.tensor.wait(*M[("fpf",)])
            for j in range(DT):
                lhs = xnb[b][:, j * 512 + tti * 128: j * 512 + (tti + 1) * 128]
                r.tensor.matmul(rtp[:, :], lhs, ctx.sb["fwr"][:, j * 8:(j + 1) * 8],
                                start=(j == 0), stop=(j == DT - 1))
                mmi = r.tensor.matmul(hp[:, :], lhs,
                                      ctx.sb["fla"][:, j * 128:(j + 1) * 128],
                                      start=(j == 0), stop=(j == DT - 1))
            mmi.then_inc(mm, 1)
            mm.inc(1)
            r.vector.wait(mm)
            r.vector.tensor_copy(rts[:, 0:8], rtp[:, :])
            r.vector.drain()
            r.vector.reduce_max(rts[:, 8:9], rts[:, 0:8], axis=AX.X, negate=True)
            r.vector.drain().then_inc(ve, 1); ve.inc(1)
            r.scalar.wait(ve)
            r.scalar.activation(e8[:, 0:8], rts[:, 0:8], AF.Exp,
                                bias=rts[:, 8:9], scale=1.0)
            r.scalar.drain().then_inc(sc, 1); sc.inc(1)
            r.vector.wait(sc)
            r.vector.max(p8[:, 0:8], e8[:, 0:8])
            r.vector.drain()
            r.vector.tensor_scalar(p8[:, 8:9], p8[:, 0:1], p8[:, 1:2], None,
                                   op0=ALU.add)
            r.vector.drain()
            r.vector.reciprocal(p8[:, 9:10], p8[:, 8:9])
            r.vector.drain()
            r.vector.tensor_scalar(w8[:, 0:8], e8[:, 0:8], p8[:, 1:2], None,
                                   op0=ALU.is_ge)
            r.vector.drain()
            r.vector.tensor_mul(w8[:, 0:8], w8[:, 0:8], e8[:, 0:8])
            r.vector.drain()
            r.vector.tensor_scalar(w8[:, 0:8], w8[:, 0:8], p8[:, 9:10], 2.0,
                                   op0=ALU.mult, op1=ALU.mult)
            r.vector.drain()
            r.vector.tensor_copy(hs[:, :], hp[:, :])
            r.vector.drain().then_inc(ve, 1); ve.inc(1)
            M[("fpf",)] = (ve, ve.n)
            r.scalar.wait(ve)
            for e in range(8):
                r.scalar.activation(gb[:, e * 16:(e + 1) * 16],
                                    hs[:, e * 16:(e + 1) * 16], AF.Copy,
                                    bias=0.0, scale=w8[:, e:e + 1])
            r.scalar.drain().then_inc(sc, 1); sc.inc(1)
            r.tensor.wait(sc)
            r.tensor.transpose(gpp[:, :], gb[:, :], ctx.sb["ident"][:]) \
                .then_inc(mm, 1)
            mm.inc(1)
            r.vector.wait(mm)
            r.vector.tensor_copy(gfm[:, tt * 128:(tt + 1) * 128], gpp[:, :]) \
                .then_inc(ve, 1)
            ve.inc(1)
            M[("fgdone", tt)] = (ve, ve.n)
        M[("fx", b)] = (mm, mm.n)
    ctx.barrier()
    ctx.psum_free_all()

    for k in range(4):
        ctx.apsum(f"pj{k}", [128, 512], F32)

    def extra(tt, oc_abs, pap, ow):
        return r.tensor.matmul(pap, gfm[:, tt * 128:(tt + 1) * 128],
                               ctx.sb["flb"][:, oc_abs:oc_abs + ow],
                               start=False, stop=True)

    emit_proj_tm(ctx, tagn, "xnofm", TOWN, "wbTb", 0, D, 1024,
                 mk_post_residual(ctx, xin, xout, add_bb=True),
                 extra_mm=extra)


def emit_attnb(ctx):
    cfg, r = ctx.cfg, ctx.rec
    D, TOWN, OT, DT, KT = cfg.D, cfg.TOWN, cfg.OT, cfg.DT, cfg.KT
    ve, sc, mm = ctx.sems["ve"], ctx.sems["sc"], ctx.sems["mm"]
    ld = [ctx.sems["ld0"], ctx.sems["ld1"]]
    st = [ctx.sems["st0"], ctx.sems["st1"]]
    M = ctx.marks

    emit_ln_transpose(ctx, "xr2", "vfm", OT, "ln", extra_out="vdbl")
    ctx.barrier()
    ctx.psum_free_all()

    rep = ctx.apsum("abRe", [128, TOWN], F32)
    imp = ctx.apsum("abIm", [128, TOWN], F32)
    cs = [ctx.sb["abc0"], ctx.sb["abc1"]]
    vres = ctx.sb["abvr"]
    sq1 = ctx.sb["absq1"]; sq2 = ctx.sb["absq2"]
    psb = [ctx.sb["abps0"], ctx.sb["abps1"]]
    for jg in range(0, DT, 2):
        nj = min(2, DT - jg)
        r.sync.dma(vres[:, jg * TOWN:(jg + nj) * TOWN]
                   .rearrange("p (j c) -> p j c", c=TOWN),
                   ctx.dram["vfm"][jg * 128:(jg + nj) * 128, :]
                   .rearrange("(j p) c -> p j c", p=128), ctx.sems["ld"])
    vl = ctx.sems["ld"].n
    r.tensor.wait_ge(ctx.sems["ld"].h, vl)
    for kt in range(KT):
        for j in range(DT):
            b = j % 2
            if ("abcs2", b) in M:
                r.sync.wait(*M[("abcs2", b)])
            r.sync.dma(cs[b][:, 0:128],
                       ctx.dram["cosFb"][j * 128:(j + 1) * 128,
                                         kt * 128:(kt + 1) * 128], ld[b])
            r.sync.dma(cs[b][:, 128:256],
                       ctx.dram["sinFb"][j * 128:(j + 1) * 128,
                                         kt * 128:(kt + 1) * 128], ld[b])
            r.tensor.wait(ld[b])
            if j <= 1 and ("abpf",) in M:
                r.tensor.wait(*M[("abpf",)])
            vt_j = vres[:, j * TOWN:(j + 1) * TOWN]
            for c0 in range(0, TOWN, 512):
                r.tensor.matmul(rep[:, c0:c0 + 512], cs[b][:, 0:128],
                                vt_j[:, c0:c0 + 512],
                                start=(j == 0), stop=(j == DT - 1))
            for c0 in range(0, TOWN, 512):
                mmi = r.tensor.matmul(imp[:, c0:c0 + 512], cs[b][:, 128:256],
                                      vt_j[:, c0:c0 + 512],
                                      start=(j == 0), stop=(j == DT - 1))
            mmi.then_inc(mm, 1)
            mm.inc(1)
            M[("abcs2", b)] = (mm, mm.n)
        r.scalar.wait(mm)
        if ("absqf",) in M:
            r.scalar.wait(*M[("absqf",)])
        r.scalar.activation(sq1[:], rep[:, :], AF.Square, bias=0.0, scale=1.0)
        r.scalar.drain()
        r.scalar.activation(sq2[:], imp[:, :], AF.Square, bias=0.0, scale=1.0)
        r.scalar.drain().then_inc(sc, 1); sc.inc(1)
        M[("abpf",)] = (sc, sc.n)
        pb_ = kt % 2
        r.vector.wait(sc)
        if ("abpsb", pb_) in M:
            r.vector.wait(*M[("abpsb", pb_)])
        r.vector.tensor_add(psb[pb_][:], sq1[:], sq2[:]).then_inc(ve, 1)
        ve.inc(1)
        M[("absqf",)] = (ve, ve.n)
        r.scalar.wait(ve)
        r.scalar.dma(ctx.dram["Ph"][kt * 128:(kt + 1) * 128, :], psb[pb_][:],
                     st[pb_])
        M[("abpsb", pb_)] = (st[pb_], st[pb_].n)
    ctx.barrier()
    ctx.psum_free_all()

    cp = ctx.apsum("abC", [128, 2048 if D >= 4096 else D], F32)
    inv = ctx.sb["abinv"]
    csb = ctx.sb["abcs"]
    pt = [ctx.sb["abpt0"], ctx.sb["abpt1"]]
    half = min(2048, D)
    for hf in range(D // half):
        if ("abinvfree",) in M:
            r.sync.wait(*M[("abinvfree",)])
        for jg in range(0, KT, 2):
            nj = min(2, KT - jg)
            dap = ctx.dram["invTb"][jg * 128:(jg + nj) * 128,
                                    hf * half:(hf + 1) * half] \
                .rearrange("(j p) c -> p j c", p=128)
            sap = inv[:, jg * half:(jg + nj) * half] \
                .rearrange("p (j c) -> p j c", c=half)
            r.sync.dma(sap, dap, ctx.sems["ld"])
        lv = ctx.sems["ld"].n
        r.tensor.wait_ge(ctx.sems["ld"].h, lv)
        for tt in range(OT):
            if ("abcf",) in M:
                r.tensor.wait(*M[("abcf",)])
            for kt in range(KT):
                ptb = (tt * KT + kt) % 2
                if ("abptf", ptb) in M:
                    r.sync.wait(*M[("abptf", ptb)])
                r.sync.dma(pt[ptb][:],
                           ctx.dram["Ph"][kt * 128:(kt + 1) * 128,
                                          tt * 128:(tt + 1) * 128], ld[ptb])
                r.tensor.wait(ld[ptb])
                for c0 in range(0, half, 512):
                    mmi = r.tensor.matmul(cp[:, c0:c0 + 512], pt[ptb][:],
                                          inv[:, kt * half + c0: kt * half + c0 + 512],
                                          start=(kt == 0), stop=(kt == KT - 1))
                mmi.then_inc(mm, 1)
                mm.inc(1)
                M[("abptf", ptb)] = (mm, mm.n)
            cb = tt % 2
            r.scalar.wait(mm)
            if ("abcsf", cb) in M:
                r.scalar.wait(*M[("abcsf", cb)])
            r.scalar.activation(csb[:, cb * half:(cb + 1) * half],
                                cp[:, 0:half], AF.Copy, bias=0.0, scale=1.0)
            r.scalar.drain().then_inc(sc, 1); sc.inc(1)
            M[("abcf",)] = (sc, sc.n)
            r.scalar.dma(ctx.dram["corrh"][tt * 128:(tt + 1) * 128,
                                           hf * half:(hf + 1) * half],
                         csb[:, cb * half:(cb + 1) * half], st[cb])
            M[("abcsf", cb)] = (st[cb], st[cb].n)
        M[("abinvfree",)] = (mm, mm.n)
    ctx.barrier()
    ctx.psum_free_all()

    r.sync.dma(ctx.sb["abrb"][:], ctx.dram["rowbase"], ctx.sems["ld"])
    for e in ("vector",):
        getattr(r, e).wait_ge(ctx.sems["ld"].h, ctx.sems["ld"].n)
    ct = [ctx.sb["abct0"], ctx.sb["abct1"]]
    t8 = ctx.sb["abt8"]; i8 = ctx.sb["abi8"]; w8r = ctx.sb["abw8"]
    off = ctx.sb["aboff"]
    acc = ctx.sb["abacc"]
    gt = [ctx.sb["abg0"], ctx.sb["abg1"]]
    gld = ctx.sems["gld"]
    for tt in range(OT):
        b = tt % 2
        if ("abct", b) in M:
            r.sync.wait(*M[("abct", b)])
        r.sync.dma(ct[b][:], ctx.dram["corrh"][tt * 128:(tt + 1) * 128, :], ld[b])
        if ("abacc",) in M:
            r.sync.wait(*M[("abacc",)])
        r.sync.dma(acc[:], ctx.dram["xr2"][tt * 128:(tt + 1) * 128, :], ld[b])
        r.vector.wait(ld[b])
        if ("abacc",) in M:
            r.vector.wait(*M[("abacc",)])
        r.vector.max(t8[:, 0:8], ct[b][:])
        r.vector.drain()
        r.vector.max_index(i8[:, 0:8], t8[:, 0:8], ct[b][:])
        r.vector.drain()
        r.vector.tensor_scalar_mul(t8[:, 8:9], t8[:, 0:1], -1.0)
        r.vector.drain().then_inc(ve, 1); ve.inc(1)
        M[("abct", b)] = (ve, ve.n)
        r.scalar.wait(ve)
        r.scalar.activation(t8[:, 16:24], t8[:, 0:8], AF.Exp,
                            bias=t8[:, 8:9], scale=1.0)
        r.scalar.drain().then_inc(sc, 1); sc.inc(1)
        r.vector.wait(sc)
        r.vector.reduce_sum(t8[:, 9:10], t8[:, 16:24], axis=AX.X)
        r.vector.drain()
        r.vector.reciprocal(t8[:, 10:11], t8[:, 9:10])
        r.vector.drain()
        r.vector.tensor_scalar(w8r[:, 0:8], t8[:, 16:24], t8[:, 10:11], None,
                               op0=ALU.mult)
        r.vector.drain()
        r.vector.tensor_copy(ctx.sb["abof"][:, 0:8], i8[:, 0:8])
        r.vector.drain()
        r.vector.tensor_scalar(ctx.sb["abof"][:, 0:8], ctx.sb["abof"][:, 0:8],
                               ctx.sb["abrb"][:, tt:tt + 1], None, op0=ALU.add)
        r.vector.drain()
        r.vector.tensor_copy(off[:, 0:8], ctx.sb["abof"][:, 0:8])
        r.vector.drain().then_inc(ve, 1); ve.inc(1)
        for k in range(8):
            gb_ = k % 2
            r.gpsimd.wait(ve)
            if ("abg", gb_) in M:
                r.gpsimd.wait(*M[("abg", gb_)])
            r.gpsimd.indirect_dma_start(
                gt[gb_][:], None, ctx.dram["vdbl"],
                IndirectOffsetOnAxis(off[:, k:k + 1], 1),
            ).then_inc(gld, 16)
            gld.inc(16)
            r.vector.wait(gld)
            r.vector.scalar_tensor_tensor(acc[:], gt[gb_][:], w8r[:, k:k + 1],
                                          acc[:], op0=ALU.mult, op1=ALU.add)
            r.vector.drain().then_inc(ve, 1); ve.inc(1)
            M[("abg", gb_)] = (ve, ve.n)
        M[("abacc",)] = (ve, ve.n)
        r.scalar.wait(ve)
        r.scalar.dma(ctx.dram["xr3"][tt * 128:(tt + 1) * 128, :], acc[:], st[b])
        M[("abacc",)] = (st[b], st[b].n)


def _replay_role(ctx, name, eng):
    needs_role = any(r.meth == "IF_ROLE" for r in ctx.rec.q[name])
    if needs_role:
        with eng.register(f"role_{name}") as rr:
            def loader():
                eng.wait_ge(ctx.sems["ld"].h, 16)
                eng.reg_load(rr, ctx.sb["rolesb"][0:1, 0:1])
                return rr
            replay(ctx.rec, name, eng, loader)
    else:
        replay(ctx.rec, name, eng, None)


def kernel(**inputs):
    cfg = CFG
    nc, ctx = build_program(cfg)
    in_maps = [host_core_inputs(cfg, inputs, c) for c in range(8)]
    res = run_bass_kernel_spmd(nc, in_maps, core_ids=list(range(8)))
    x = np.asarray(inputs["L0_attn_a_hidden_states"])
    out = np.empty_like(x, dtype=np.float32)
    for c in range(8):
        s, role = c // 2, c % 2
        yc = np.asarray(res.results[c]["y"])
        for bi, p in enumerate(cfg.QPOS[role]):
            out[s, p * cfg.QB:(p + 1) * cfg.QB] = yc[bi * cfg.QB:(bi + 1) * cfg.QB]
    return out


# version 10
# speedup vs baseline: 1.1850x; 1.0545x over previous
import numpy as np
import ml_dtypes
import concourse.bass as bass
import concourse.mybir as mybir
from concourse.bass_utils import run_bass_kernel_spmd
from concourse.bass_core_rust import IndirectOffsetOnAxis

F32 = mybir.dt.float32
BF16 = mybir.dt.bfloat16
I32 = mybir.dt.int32
U32 = mybir.dt.uint32
AF = mybir.ActivationFunctionType
ALU = mybir.AluOpType
AX = mybir.AxisListType

ENGINES = ("sync", "vector", "scalar", "tensor", "gpsimd")
LN_EPS = 1e-5
CLIP = 8.0
NH = 32
HD = 128
HSC = HD ** -0.5
PBASE = 10240
PSIZE = 196 * 1024


class Cfg:
    def __init__(self, S=4096, D=4096):
        self.S = S
        self.D = D
        self.TOWN = S // 2
        self.QB = max(128, S // 8)
        self.NBLK = S // self.QB
        self.QPOS = ([0, 1, self.NBLK - 2, self.NBLK - 1],
                     list(range(2, self.NBLK - 2)))
        self.DT = D // 128
        self.ST = S // 128
        self.OT = self.TOWN // 128
        self.KP = (D // 2 // 128 + 1) * 128
        self.KT = self.KP // 128


CFG = Cfg()


_CONST_CACHE = {}


def host_constants(cfg):
    key = (cfg.S, cfg.D)
    if key in _CONST_CACHE:
        return _CONST_CACHE[key]
    D = cfg.D
    n = np.arange(D, dtype=np.float64)[:, None]
    k = np.arange(cfg.KP, dtype=np.float64)[None, :]
    ang = 2.0 * np.pi * (n * k) / D
    cosF = np.cos(ang); cosF[:, D // 2 + 1:] = 0.0
    sinF = np.sin(ang); sinF[:, D // 2 + 1:] = 0.0
    kk = np.arange(cfg.KP, dtype=np.float64)[:, None]
    dd = np.arange(D // 2 + 1, dtype=np.float64)[None, :]
    w = np.full((cfg.KP, 1), 2.0); w[0] = 1.0; w[D // 2] = 1.0; w[D // 2 + 1:] = 0.0
    invH = (w * np.cos(2.0 * np.pi * (kk * dd) / D)) / D
    invT = np.zeros((cfg.KP, D))
    invT[:, :D // 2 + 1] = invH
    invT[:, D // 2 + 1:] = invH[:, 1:D // 2][:, ::-1]
    m = np.arange(1, NH + 1, dtype=np.float64) * (8.0 / NH)
    slopes = (1.0 / np.power(2.0, m)).astype(np.float64)
    iota_kt = (np.arange(128)[:, None] + 128.0 * np.arange(cfg.ST)[None, :]).astype(np.float32)
    nmask = cfg.QB // 128
    masks = np.stack([
        np.where(np.arange(128)[:, None] + o * 128 <= np.arange(cfg.QB)[None, :],
                 np.float32(0.0), np.float32(1e30))
        for o in range(nmask)]).astype(np.float32).astype(ml_dtypes.bfloat16)
    rowbase = ((np.arange(128)[:, None] + 128 * np.arange(cfg.OT)[None, :]).astype(np.int64)
               * (2 * D)).astype(np.float32)
    _CONST_CACHE[key] = dict(
        cosF=cosF.astype(ml_dtypes.bfloat16), sinF=sinF.astype(ml_dtypes.bfloat16),
        invT=invT.astype(ml_dtypes.bfloat16), slopes=slopes, iota_kt=iota_kt,
        masks=masks, rowbase=rowbase)
    return _CONST_CACHE[key]


def host_core_inputs(cfg, inputs, core):
    s, role = core // 2, core % 2
    x = np.asarray(inputs["L0_attn_a_hidden_states"], dtype=np.float32)
    xs = np.ascontiguousarray(x[s][:cfg.S, :cfg.D])
    qpos = cfg.QPOS[role]
    xo = np.ascontiguousarray(
        np.concatenate([xs[p * cfg.QB:(p + 1) * cfg.QB] for p in qpos], axis=0))
    posrow = np.concatenate(
        [np.arange(p * cfg.QB, (p + 1) * cfg.QB) for p in qpos]
    ).astype(np.float32)[None, :]
    cst = host_constants(cfg)
    D = cfg.D

    def f32c(a):
        return np.ascontiguousarray(np.asarray(a, dtype=np.float32))

    m = {
        "x_seq": xs,
        "x_own": xo,
        "role": np.array([[role]], dtype=np.int32),
        "posrow": posrow,
        "wqkvT": f32c(np.asarray(inputs["L0_attn_a_Wqkv"]).T),
        "woutT": f32c(np.asarray(inputs["L0_attn_a_Wout"]).T),
        "wbT": f32c(np.asarray(inputs["L0_ffn_W_base"]).T),
        "wrT": f32c(np.asarray(inputs["L0_ffn_W_router"]).T),
        "laT": f32c(np.asarray(inputs["L0_ffn_lora_A"]).reshape(128, D).T),
        "lbC": f32c(np.transpose(np.asarray(inputs["L0_ffn_lora_B"]), (0, 2, 1)).reshape(128, D)),
        "bbase": f32c(np.asarray(inputs["L0_ffn_b_base"])[None, :]),
        "qlnw": f32c(np.asarray(inputs["L0_attn_a_q_ln_w"])[None, :]),
        "qlnb": f32c(np.asarray(inputs["L0_attn_a_q_ln_b"])[None, :]),
        "klnw": f32c(np.asarray(inputs["L0_attn_a_k_ln_w"])[None, :]),
        "klnb": f32c(np.asarray(inputs["L0_attn_a_k_ln_b"])[None, :]),
        "cosFb": cst["cosF"], "sinFb": cst["sinF"], "invTb": cst["invT"],
        "iota_kt": cst["iota_kt"], "rowbase": cst["rowbase"],
    }
    for o in range(cst["masks"].shape[0]):
        m[f"mask{o}"] = np.ascontiguousarray(cst["masks"][o])
    return m


class SemT:
    def __init__(self, handle, name):
        self.h = handle; self.name = name; self.n = 0

    def inc(self, by):
        self.n += by
        return self.n


class InstRec:
    __slots__ = ("meth", "a", "kw", "incs")

    def __init__(self, meth, a, kw):
        self.meth = meth; self.a = a; self.kw = kw; self.incs = []

    def then_inc(self, sem, n=1):
        self.incs.append((sem, n))
        return self


class EngRec:
    def __init__(self, q):
        object.__setattr__(self, "_q", q)

    def __getattr__(self, meth):
        q = self._q

        def f(*a, **kw):
            r = InstRec(meth, a, kw)
            q.append(r)
            return r
        return f

    def wait(self, sem, val=None):
        self.wait_ge(sem.h, sem.n if val is None else val)

    def dma(self, out, in_, sem, **kw):
        self.dma_start(out=out, in_=in_, **kw).then_inc(sem, 16)
        return sem.inc(16)


class Rec:
    def __init__(self):
        self.q = {e: [] for e in ENGINES}
        for e in ENGINES:
            setattr(self, e, EngRec(self.q[e]))


def replay(rec, engine_name, eng, role_reg):
    cur = None
    for r in rec.q[engine_name]:
        if r.meth == "IF_ROLE":
            if callable(role_reg):
                role_reg = role_reg()
            cur = eng.If_eq(role_reg, r.a[0])
            cur.__enter__()
            continue
        if r.meth == "ELSE_ROLE":
            cur.__exit__(None, None, None)
            cur = eng.Else()
            cur.__enter__()
            continue
        if r.meth == "ENDIF_ROLE":
            cur.__exit__(None, None, None)
            cur = None
            eng.end_ifs()
            continue
        inst = getattr(eng, r.meth)(*r.a, **r.kw)
        for (sem, n) in r.incs:
            inst = inst.then_inc(sem.h, n)


class Ctx:
    DMA_SEMS = ("ld", "ld0", "ld1", "ld2", "st0", "st1", "st2", "st3",
                "wl0", "wl1", "rl0", "rl1", "rl2", "rl3",
                "vl0", "vl1", "cs0", "cs1", "gld", "gst")

    def __init__(self, nc, cfg):
        self.nc = nc; self.cfg = cfg
        self.dram = {}; self.sb = {}; self.ps = {}; self.sems = {}
        self.rec = Rec()
        self.marks = {}
        self._guards = []

    def din(self, name, shape, dt=F32):
        self.dram[name] = self.nc.dram_tensor(name, shape, dt, kind="ExternalInput").ap()
        return self.dram[name]

    def dint(self, name, shape, dt=F32):
        self.dram[name] = self.nc.dram_tensor(name, shape, dt, kind="Internal").ap()
        return self.dram[name]

    def dout(self, name, shape, dt=F32):
        self.dram[name] = self.nc.dram_tensor(name, shape, dt, kind="ExternalOutput").ap()
        return self.dram[name]

    def reserve_sbuf(self, nbytes):
        h = self.nc.alloc_sbuf_tensor("manual_region", [128, nbytes], mybir.dt.uint8)
        self._mb = self.nc.lookup_mloc(h).addr
        self._msz = nbytes

    def asb(self, name, shape, dt, offset):
        end = offset + int(np.prod(shape[1:])) * mybir.dt.size(dt)
        assert end <= self._msz, f"sbuf region overflow: {name} end {end}"
        t = self.nc.alloc_sbuf_tensor_at(name, shape, dt, offset=self._mb + offset)
        self.sb[name] = t
        return t

    def layout(self, base, specs):
        out = {}
        off = base
        for (name, shape, dt) in specs:
            out[name] = self.asb(name, shape, dt, off)
            off += (int(np.prod(shape[1:])) * mybir.dt.size(dt) + 63) & ~63
        assert off <= self._msz, f"layout overflow at {off}"
        out["_end"] = off
        return out

    def apsum(self, name, shape, dt=F32):
        self._psn = getattr(self, "_psn", 0) + 1
        g = self.nc.psum_tensor(f"{name}_u{self._psn}", shape, dt)
        t = g.__enter__()
        self._psum_guards = getattr(self, "_psum_guards", [])
        self._psum_guards.append((name, g))
        self.ps[name] = t
        return t

    def psum_free_all(self):
        for name, g in reversed(getattr(self, "_psum_guards", [])):
            g.__exit__(None, None, None)
            self.ps.pop(name, None)
        self._psum_guards = []

    def psum_get(self, name, shape, dt=F32):
        if name not in self.ps:
            self.apsum(name, shape, dt)
        return self.ps[name]

    def sem_snapshot(self):
        return {k: s.n for k, s in self.sems.items()}

    def sem_restore(self, snap):
        for k, v in snap.items():
            self.sems[k].n = v

    def sem(self, name):
        g = self.nc.semaphore(name)
        h = g.__enter__()
        self._guards.append(g)
        self.sems[name] = SemT(h, name)
        return self.sems[name]

    def barrier(self, engines=ENGINES):
        r = self.rec
        for e in engines:
            for s in self.DMA_SEMS:
                if s in self.sems and self.sems[s].n:
                    getattr(r, e).wait_ge(self.sems[s].h, self.sems[s].n)
        for e in engines:
            sT = self.sems[f"b_{e}"]
            eng = getattr(r, e)
            if e in ("tensor", "sync"):
                eng.nop().then_inc(sT, 1)
            else:
                eng.drain().then_inc(sT, 1)
            sT.inc(1)
        for e in engines:
            for o in engines:
                so = self.sems[f"b_{o}"]
                getattr(r, e).wait_ge(so.h, so.n)


def emit_prelude(ctx):
    r = ctx.rec
    ident = ctx.sb["ident"]
    ones = ctx.sb["onescol"]
    r.gpsimd.memset(ident[:], 1.0)
    r.gpsimd.drain()
    r.gpsimd.affine_select(ident[:], ident[:], pattern=[[1, 128]],
                           compare_op=ALU.is_equal, fill=0.0,
                           base=0, channel_multiplier=-1)
    r.gpsimd.drain().then_inc(ctx.sems["gp"], 1)
    ctx.sems["gp"].inc(1)
    r.vector.memset(ones[:], 1.0)
    r.vector.memset(ctx.sb["onesrow"][:], 1.0)
    r.vector.memset(ctx.sb["onesrowf"][:], 1.0)
    r.vector.drain().then_inc(ctx.sems["ve"], 1)
    ctx.sems["ve"].inc(1)
    ctx.barrier()


def emit_cast_weights(ctx, jobs):
    r = ctx.rec
    wc = ctx.sb["wcf"]; wb = ctx.sb["wcb"]
    ld = [ctx.sems["wl0"], ctx.sems["wl1"]]
    st = [ctx.sems["cs0"], ctx.sems["cs1"]]
    csem = {0: ctx.sems["gp"], 1: ctx.sems["ve"], 2: ctx.sems["sc"]}
    i = 0
    for (src, dst, rows, cols) in jobs:
        for r0 in range(0, rows, 128):
            for c0 in range(0, cols, 2048):
                cw = min(2048, cols - c0)
                b = i % 2
                if i >= 2:
                    r.sync.wait(*ctx.marks[("wc", i - 2)])
                r.sync.dma(wc[b][:, 0:cw],
                           ctx.dram[src][r0:r0 + 128, c0:c0 + cw], ld[b])
                ceng = (r.gpsimd, r.vector, r.scalar)[i % 3]
                cs_ = csem[i % 3]
                ceng.wait(ld[b])
                if i >= 2:
                    ceng.wait(*ctx.marks[("wcb", i - 2)])
                if i % 3 == 2:
                    ceng.activation(wb[b][:, 0:cw], wc[b][:, 0:cw], AF.Copy,
                                    bias=0.0, scale=1.0).then_inc(cs_, 1)
                else:
                    ceng.tensor_copy(wb[b][:, 0:cw], wc[b][:, 0:cw]) \
                        .then_inc(cs_, 1)
                cs_.inc(1)
                ctx.marks[("wc", i)] = (cs_, cs_.n)
                r.scalar.wait(cs_)
                r.scalar.dma(ctx.dram[dst][r0:r0 + 128, c0:c0 + cw],
                             wb[b][:, 0:cw], st[b])
                ctx.marks[("wcb", i)] = (st[b], st[b].n)
                i += 1


def emit_row_broadcast(ctx, jobs):
    r = ctx.rec
    rowf = ctx.sb["rbf"]; rowb = ctx.sb["rbb"]
    pbc = ctx.ps["rbp"]
    ld = ctx.sems["ld"]
    ve, mm = ctx.sems["ve"], ctx.sems["mm"]
    ones = ctx.sb["onesrow"]
    for (src, dstn, width, scale) in jobs:
        dst = ctx.sb[dstn]
        r.sync.wait(ve, ve.n)
        r.sync.dma(rowf[0:1, 0:width], ctx.dram[src], ld)
        r.vector.wait(ld)
        if scale != 1.0:
            r.vector.tensor_scalar_mul(rowf[0:1, 0:width], rowf[0:1, 0:width], scale)
            r.vector.drain()
        r.vector.tensor_copy(rowb[0:1, 0:width], rowf[0:1, 0:width])
        r.vector.drain().then_inc(ve, 1); ve.inc(1)
        r.tensor.wait(ve)
        for c0 in range(0, width, 512):
            cw = min(512, width - c0)
            pb = (c0 // 512) % 2
            if c0 >= 1024:
                r.tensor.wait(*ctx.marks[("rb", dstn, c0 - 1024)])
            r.tensor.matmul(pbc[pb][:, 0:cw], ones[0:1, :],
                            rowb[0:1, c0:c0 + cw], start=True, stop=True) \
                .then_inc(mm, 1)
            mm.inc(1)
            r.vector.wait(mm)
            r.vector.tensor_copy(dst[:, c0:c0 + cw], pbc[pb][:, 0:cw]) \
                .then_inc(ve, 1)
            ve.inc(1)
            ctx.marks[("rb", dstn, c0)] = (ve, ve.n)


def emit_ln_transpose(ctx, src, dst, ntiles, tag, src_dt=F32, gb=None,
                      extra_out=None, eps=LN_EPS, store_q="scalar"):
    cfg, r = ctx.cfg, ctx.rec
    D, DT = cfg.D, cfg.DT
    if src_dt == F32:
        xt = [ctx.sb[f"{tag}xt0"], ctx.sb[f"{tag}xt1"]]
    else:
        xt = [ctx.sb[f"{tag}xtb0"], ctx.sb[f"{tag}xtb1"]]
    tmp = ctx.sb[f"{tag}tmp"]
    bn = ctx.sb[f"{tag}bn"]; mv = ctx.sb[f"{tag}mv"]; rstd = ctx.sb[f"{tag}rs"]
    xn = [ctx.sb[f"{tag}xn0"], ctx.sb[f"{tag}xn1"]]
    fm = [ctx.sb[f"{tag}fm0"], ctx.sb[f"{tag}fm1"]]
    ptr = [ctx.psum_get("ptr0", [128, 128], BF16),
           ctx.psum_get("ptr1", [128, 128], BF16)]
    ident = ctx.sb["ident"]
    ld = [ctx.sems["ld0"], ctx.sems["ld1"]]
    st = [ctx.sems["st0"], ctx.sems["st1"]]
    ve, sc, mm = (ctx.sems[k] for k in ("ve", "sc", "mm"))
    M = ctx.marks

    for i in range(ntiles):
        b = i % 2
        if i >= 2:
            r.sync.wait(*M[(tag, i - 2, "xtfree")])
        r.sync.dma(xt[b][:], ctx.dram[src][i * 128:(i + 1) * 128, :], ld[b])
        ldmark = ld[b].n
        r.vector.wait(ld[b], ldmark)
        for c in range(D // 512):
            r.vector.bn_stats(bn[:, 6 * c:6 * c + 6], xt[b][:, c * 512:(c + 1) * 512])
        r.vector.drain()
        r.vector.bn_aggr(mv[:], bn[:])
        r.vector.drain()
        r.vector.tensor_scalar_add(mv[:, 1:2], mv[:, 1:2], eps)
        r.vector.drain().then_inc(ve, 1); ve.inc(1)
        r.scalar.wait(ve)
        r.scalar.activation(rstd[:], mv[:, 1:2], AF.Ln, bias=0.0, scale=1.0)
        r.scalar.drain()
        r.scalar.activation(rstd[:], rstd[:], AF.Exp, bias=0.0, scale=-0.5)
        r.scalar.drain().then_inc(sc, 1); sc.inc(1)
        r.vector.wait(sc)
        if i >= 2:
            r.vector.wait(*M[(tag, i - 2, "xnfree")])
        if gb is None:
            r.vector.tensor_scalar(xn[b][:], xt[b][:], mv[:, 0:1], rstd[:, 0:1],
                                   op0=ALU.subtract, op1=ALU.mult)
        else:
            gbc, bbc = ctx.sb[gb[0]], ctx.sb[gb[1]]
            r.vector.tensor_scalar(tmp[:], xt[b][:], mv[:, 0:1], rstd[:, 0:1],
                                   op0=ALU.subtract, op1=ALU.mult) \
                .then_inc(ve, 1)
            ve.inc(1)
            r.gpsimd.wait(ve)
            r.gpsimd.tensor_mul(tmp[:], tmp[:], gbc[:])
            r.gpsimd.drain()
            r.gpsimd.tensor_add(xn[b][:], tmp[:], bbc[:])
            r.gpsimd.drain().then_inc(ctx.sems["gp"], 1)
            ctx.sems["gp"].inc(1)
            r.vector.wait(ctx.sems["gp"])
        r.vector.drain().then_inc(ve, 1); ve.inc(1)
        M[(tag, i, "xtfree")] = (ve, ve.n)
        if extra_out is not None:
            r.sync.wait(ve)
            r.sync.dma(ctx.dram[extra_out][i * 128:(i + 1) * 128, 0:D], xn[b][:], ld[b])
            r.scalar.wait(ve)
            vls = ctx.sems[f"vl{b}"]
            r.scalar.dma(ctx.dram[extra_out][i * 128:(i + 1) * 128, D:2 * D], xn[b][:],
                         vls)
            M[(tag, i, "xnextra")] = (vls, vls.n)
        if store_q != "scalar":
            pass
        r.tensor.wait(ve)
        for j in range(DT):
            pb = j % 2
            if j >= 2:
                r.tensor.wait(*M[(tag, i, "evac", j - 2)])

            r.tensor.transpose(ptr[pb][:], xn[b][:, j * 128:(j + 1) * 128],
                               ident[:]).then_inc(mm, 1)
            mm.inc(1)
            r.vector.wait(mm)
            if j == 0 and i >= 2:
                r.vector.wait(*M[(tag, i - 2, "stored")])
            r.vector.tensor_copy(fm[b][:, j * 128:(j + 1) * 128], ptr[pb][:]) \
                .then_inc(ve, 1)
            ve.inc(1)
            M[(tag, i, "evac", j)] = (ve, ve.n)
        if extra_out is not None:
            M[(tag, i, "xnfree")] = M[(tag, i, "xnextra")]
        else:
            M[(tag, i, "xnfree")] = (mm, mm.n)
        getattr(r, store_q).wait(ve)
        ng = max(1, DT // 2)
        for g in range(ng):
            nj = min(2, DT - g * 2)
            sbap = fm[b][:, g * 2 * 128:(g * 2 + nj) * 128] \
                .rearrange("p (j c) -> p j c", c=128)
            dap = ctx.dram[dst][g * 2 * 128:(g * 2 + nj) * 128,
                                i * 128:(i + 1) * 128] \
                .rearrange("(j p) c -> p j c", p=128)
            getattr(r, store_q).dma(dap, sbap, st[b])
        M[(tag, i, "stored")] = (st[b], st[b].n)


def emit_proj_tm(ctx, tag, src_fm, T, wname, col0, ncols, grp, post,
                 tcw=512, extra_mm=None):
    cfg, r = ctx.cfg, ctx.rec
    D, DT = cfg.D, cfg.DT
    w_sb = [ctx.sb["pjw0"], ctx.sb["pjw1"]]
    xnb = [ctx.sb["pjx0"], ctx.sb["pjx1"]]
    ld = [ctx.sems["ld0"], ctx.sems["ld1"]]
    wl = [ctx.sems["wl0"], ctx.sems["wl1"]]
    mm = ctx.sems["mm"]
    M = ctx.marks
    grp = 512
    nGRP = (ncols + grp - 1) // grp
    nTC = (T + tcw - 1) // tcw
    psums = [ctx.ps["pj0"], ctx.ps["pj1"], ctx.ps["pj2"], ctx.ps["pj3"]]
    pcnt = 0

    for g in range(nGRP):
        gw = min(grp, ncols - g * grp)
        wb = g % 2
        if ("pjwfree", wb) in M:
            r.sync.wait(*M[("pjwfree", wb)])
        for jg in range(0, DT, 4):
            nj = min(4, DT - jg)
            dap = ctx.dram[wname][jg * 128:(jg + nj) * 128,
                                  col0 + g * grp: col0 + g * grp + gw] \
                .rearrange("(j p) c -> p j c", p=128)
            sap = w_sb[wb][:, jg * gw:(jg + nj) * gw] \
                .rearrange("p (j c) -> p j c", c=gw)
            r.sync.dma(sap, dap, wl[wb])
        wmark = wl[wb].n
        for tc in range(nTC):
            b = tc % 2
            if ("pjxfree", b) in M:
                r.sync.wait(*M[("pjxfree", b)])
                r.vector.wait(*M[("pjxfree", b)])
            for jg in range(0, DT, 2):
                nj = min(2, DT - jg)
                dap = ctx.dram[src_fm][jg * 128:(jg + nj) * 128,
                                       tc * tcw:tc * tcw + tcw] \
                    .rearrange("(j p) c -> p j c", p=128)
                sap = xnb[b][:, jg * tcw:(jg + nj) * tcw] \
                    .rearrange("p (j c) -> p j c", c=tcw)
                r.sync.dma(sap, dap, ld[b])
            xmark = ld[b].n
            ntt = tcw // 128
            for tti in range(ntt):
                tt = tc * ntt + tti
                r.tensor.wait(ld[b], xmark)
                if tc == 0 and tti == 0:
                    r.tensor.wait(wl[wb], wmark)
                pb = pcnt % 4
                pcnt += 1
                if ("pjpfree", pb) in M:
                    r.tensor.wait(*M[("pjpfree", pb)])
                last = DT - 1 if extra_mm is None else DT
                for j in range(DT):
                    lhs = xnb[b][:, j * tcw + tti * 128:
                                 j * tcw + (tti + 1) * 128]
                    rhs = w_sb[wb][:, j * gw: j * gw + gw]
                    mmi = r.tensor.matmul(psums[pb][:, 0:gw], lhs, rhs,
                                          start=(j == 0), stop=(j == last))
                if extra_mm is not None:
                    mmi = extra_mm(tt, g * grp, psums[pb][:, 0:gw], gw)
                mmi.then_inc(mm, 1)
                mm.inc(1)
                M[("pjpfree", pb)] = post(tt, g * grp, psums[pb][:, 0:gw], gw)
            M[("pjxfree", b)] = (mm, mm.n)
        M[("pjwfree", wb)] = (mm, mm.n)


def build_program(cfg, phases=("all",)):
    nc = bass.Bass("TRN2", target_bir_lowering=False, num_devices=8)
    D, S, TOWN = cfg.D, cfg.S, cfg.TOWN
    ctx = Ctx(nc, cfg)

    ctx.din("x_seq", [S, D]); ctx.din("x_own", [TOWN, D])
    ctx.din("role", [1, 1], I32); ctx.din("posrow", [1, TOWN])
    ctx.din("wqkvT", [D, 3 * D]); ctx.din("woutT", [D, D]); ctx.din("wbT", [D, D])
    ctx.din("wrT", [D, 8]); ctx.din("laT", [D, 128]); ctx.din("lbC", [128, D])
    ctx.din("bbase", [1, D])
    for nm in ("qlnw", "qlnb", "klnw", "klnb"):
        ctx.din(nm, [1, D])
    ctx.din("cosFb", [D, cfg.KP], BF16); ctx.din("sinFb", [D, cfg.KP], BF16)
    ctx.din("invTb", [cfg.KP, D], BF16)
    ctx.din("iota_kt", [128, cfg.ST])
    ctx.din("rowbase", [128, cfg.OT], F32)
    for o in range(cfg.QB // 128):
        ctx.din(f"mask{o}", [128, cfg.QB], BF16)

    ctx.dint("xnfm", [D, S], BF16)
    ctx.dint("xnofm", [D, TOWN], BF16)
    ctx.dint("ktm", [S, D], BF16)
    ctx.dint("vtm", [S, D], BF16)
    ctx.dint("qtm", [TOWN, D], BF16)
    ctx.dint("kfm", [D, S], BF16)
    ctx.dint("qfm", [D, TOWN], BF16)
    ctx.dint("ofm", [D, TOWN], BF16)
    ctx.dint("xr1", [TOWN, D]); ctx.dint("xr2", [TOWN, D]); ctx.dint("xr3", [TOWN, D])
    ctx.dint("vdbl", [TOWN, 2 * D], BF16)
    ctx.dint("vfm", [D, TOWN], BF16)
    ctx.dint("corrh", [TOWN, D], BF16)
    ctx.dint("Ph", [cfg.KP, TOWN], BF16)
    ctx.dint("wqkvTb", [D, 3 * D], BF16)
    ctx.dint("woutTb", [D, D], BF16)
    ctx.dint("wbTb", [D, D], BF16)
    ctx.dint("laTb", [D, 128], BF16)
    ctx.dint("wrTb", [D, 8], BF16)
    ctx.dint("lbCb", [128, D], BF16)

    ctx.dout("y", [TOWN, D])

    for nm in ("ld", "ld0", "ld1", "ld2", "st0", "st1", "wl0", "wl1",
               "rl0", "rl1", "rl2", "rl3", "st2", "st3",
               "vl0", "vl1", "cs0", "cs1", "ve", "sc", "mm", "gp",
               "gld", "gst",
               "b_sync", "b_vector", "b_scalar", "b_tensor", "b_gpsimd"):
        ctx.sem(nm)

    ctx.reserve_sbuf(PSIZE)
    ctx.asb("ident", [128, 128], BF16, 0)
    ctx.asb("onescol", [128, 1], BF16, 256)
    ctx.asb("onesrow", [1, 128], BF16, 288)
    ctx.asb("onesrowf", [1, 128], F32, 8832)
    ctx.asb("iotasb", [128, max(cfg.ST, 2)], F32, 9344)
    ctx.asb("rolesb", [1, 1], I32, 576)
    ctx.asb("posrsb", [1, max(D, 2048)], F32, 640)
    ctx.asb("rbf", [1, D], F32, PBASE)
    ctx.asb("rbb", [1, D], BF16, PBASE + D * 4)

    GBB = PSIZE - 4 * (D * 2 + 64) - 512
    ctx.asb("qgbc", [128, D], BF16, GBB)
    ctx.asb("qbbc", [128, D], BF16, GBB + (D * 2 + 64))
    ctx.asb("kgbc", [128, D], BF16, GBB + 2 * (D * 2 + 64))
    ctx.asb("kbbc", [128, D], BF16, GBB + 3 * (D * 2 + 64))

    L = ctx.layout(PBASE, [
        ("lnxt0", [128, D], F32), ("lnxt1", [128, D], F32),
        ("lntmp", [128, D], F32),
        ("lnxn0", [128, D], BF16), ("lnxn1", [128, D], BF16),
        ("lnfm0", [128, D], BF16), ("lnfm1", [128, D], BF16),
        ("lnxtb0", [128, D], BF16), ("lnxtb1", [128, D], BF16),
        ("lnbn", [128, 6 * (D // 512)], F32),
        ("lnmv", [128, 2], F32), ("lnrs", [128, 1], F32),
    ])
    LN_END = L["_end"]
    C = ctx.layout(LN_END, [
        ("wcf0", [128, 2048], F32), ("wcf1", [128, 2048], F32),
        ("wcb0", [128, 2048], BF16), ("wcb1", [128, 2048], BF16),
    ])
    ctx.sb["wcf"] = [ctx.sb["wcf0"], ctx.sb["wcf1"]]
    ctx.sb["wcb"] = [ctx.sb["wcb0"], ctx.sb["wcb1"]]
    assert C["_end"] <= GBB

    GRP = 512
    P = ctx.layout(PBASE, [
        ("pjw0", [128, (D // 128) * GRP], BF16),
        ("pjw1", [128, (D // 128) * GRP], BF16),
        ("pjx0", [128, (D // 128) * 512], BF16),
        ("pjx1", [128, (D // 128) * 512], BF16),
        ("pjo0", [128, 512], BF16), ("pjo1", [128, 512], BF16),
        ("pjo2", [128, 512], BF16), ("pjo3", [128, 512], BF16),
        ("pjr0", [128, 512], F32), ("pjr1", [128, 512], F32),
        ("pjq0", [128, 512], F32), ("pjq1", [128, 512], F32),
        ("pjr2", [128, 512], F32), ("pjr3", [128, 512], F32),
        ("pjq2", [128, 512], F32), ("pjq3", [128, 512], F32),
    ])
    assert P["_end"] <= GBB and L["_end"] <= GBB
    FB = ctx.layout(P["_end"], [
        ("bbb", [128, D], F32),
        ("flb", [128, D], BF16),
        ("fla", [128, D], BF16),
        ("fwr", [128, cfg.DT * 8], BF16),
        ("frts", [128, 16], F32),
        ("fe8", [128, 8], F32),
        ("fp8", [128, 16], F32),
        ("fw8", [128, 8], F32),
        ("fhs", [128, 128], F32),
        ("fgb", [128, 128], BF16),
        ("fgfm", [128, TOWN], BF16),
    ])
    assert FB["_end"] <= PSIZE
    ABF = ctx.layout(PBASE, [
        ("abc0", [128, 256], BF16), ("abc1", [128, 256], BF16),
        ("absq1", [128, TOWN], F32), ("absq2", [128, TOWN], F32),
        ("abps0", [128, TOWN], BF16), ("abps1", [128, TOWN], BF16),
        ("abvr", [128, cfg.DT * TOWN], BF16),
    ])
    assert ABF["_end"] <= PSIZE
    ABI = ctx.layout(PBASE, [
        ("abpt0", [128, 128], BF16), ("abpt1", [128, 128], BF16),
        ("abcs", [128, 2 * min(2048, D)], BF16),
        ("abinv", [128, cfg.KT * min(2048, D)], BF16),
    ])
    assert ABI["_end"] <= PSIZE
    AG = ctx.layout(PBASE, [
        ("abct0", [128, D], BF16), ("abct1", [128, D], BF16),
        ("abt8", [128, 24], F32),
        ("abi8", [128, 8], U32),
        ("abw8", [128, 8], F32),
        ("aboff", [128, 8], U32),
        ("abrb", [128, cfg.OT], F32),
        ("abof", [128, 8], F32),
        ("abacc", [128, D], F32),
        ("abg0", [128, D], BF16), ("abg1", [128, D], BF16),
    ])
    assert AG["_end"] <= PSIZE
    A = ctx.layout(PBASE, [
        ("atK0", [128, S], BF16), ("atK1", [128, S], BF16),
        ("atK2", [128, S], BF16),
        ("atV0", [128, S], BF16), ("atV1", [128, S], BF16),
        ("atV2", [128, S], BF16),
        ("atQ0", [128, TOWN], BF16), ("atQ1", [128, TOWN], BF16),
        ("atQ2", [128, TOWN], BF16),
        ("atsq0", [128, S + TOWN], BF16), ("atsq1", [128, S + TOWN], BF16),
        ("atp0", [128, cfg.QB], BF16), ("atp1", [128, cfg.QB], BF16),
        ("atp2", [128, cfg.QB], BF16),
        ("atncb0", [1, TOWN], BF16), ("atncb1", [1, TOWN], BF16),
        ("atncb2", [1, TOWN], BF16),
        ("atbt0", [128, 4 * cfg.ST], F32), ("atbt1", [128, 4 * cfg.ST], F32),
        ("atbt2", [128, 4 * cfg.ST], F32),
        ("atnrm", [1, S + TOWN + 4], F32),
        ("atkm", [1, 64], F32), ("atqm", [1, 64], F32),
        ("atbo", [1, 64], F32), ("atbc", [128, 64], F32),
        ("atsd0", [128, cfg.QB], F32), ("atsd1", [128, cfg.QB], F32),
        ("atsd2", [128, cfg.QB], F32),
        ("atlsb0", [1, cfg.QB], F32), ("atlsb1", [1, cfg.QB], F32),
        ("atlbc", [128, cfg.QB], F32),
        ("atob0", [128, cfg.QB], BF16), ("atob1", [128, cfg.QB], BF16),
    ] + [(f"msk{o}", [128, cfg.QB], BF16) for o in range(cfg.QB // 128)])
    assert A["_end"] <= GBB

    ctx.apsum("ptr0", [128, 128], BF16)
    ctx.apsum("ptr1", [128, 128], BF16)
    ctx.apsum("rbp0", [128, 512], F32)
    ctx.apsum("rbp1", [128, 512], F32)
    ctx.ps["rbp"] = [ctx.ps["rbp0"], ctx.ps["rbp1"]]
    for k in range(4):
        ctx.apsum(f"pj{k}", [128, 512], F32)

    r = ctx.rec
    r.sync.dma(ctx.sb["rolesb"][:], ctx.dram["role"], ctx.sems["ld"])
    emit_prelude(ctx)

    allp = "all" in phases

    if allp or "cast" in phases:
        emit_row_broadcast(ctx, [
            ("qlnw", "qgbc", D, HSC), ("qlnb", "qbbc", D, HSC),
            ("klnw", "kgbc", D, 1.0), ("klnb", "kbbc", D, 1.0),
        ])
        ctx.barrier()
        emit_cast_weights(ctx, [
            ("wqkvT", "wqkvTb", D, 3 * D),
            ("woutT", "woutTb", D, D),
            ("wbT", "wbTb", D, D),
            ("laT", "laTb", D, 128),
            ("wrT", "wrTb", D, 8),
            ("lbC", "lbCb", 128, D),
        ])

    if allp or "ln1" in phases:
        emit_ln_transpose(ctx, "x_seq", "xnfm", cfg.ST, "ln")
        ctx.barrier()
        emit_ln_transpose(ctx, "x_own", "xnofm", cfg.OT, "ln")
        ctx.barrier()

    if allp or "qkv" in phases:
        emit_qkv(ctx)
        ctx.barrier()
        emit_ln_transpose(ctx, "ktm", "kfm", cfg.ST, "ln", src_dt=BF16,
                          gb=("kgbc", "kbbc"))
        ctx.barrier()
        emit_ln_transpose(ctx, "qtm", "qfm", cfg.OT, "ln", src_dt=BF16,
                          gb=("qgbc", "qbbc"))
        ctx.barrier()

    if allp or "attn" in phases:
        emit_attention(ctx)
        ctx.barrier()

    if allp or "wout" in phases:
        ctx.psum_free_all()
        for k in range(4):
            ctx.apsum(f"pj{k}", [128, 512], F32)
        emit_wout(ctx)
        ctx.barrier()

    if allp or "ffn1" in phases:
        emit_ffn(ctx, "xr1", "xr2", "f1")
        ctx.barrier()

    if allp or "attnb" in phases:
        emit_attnb(ctx)
        ctx.barrier()

    if allp or "ffn2" in phases:
        emit_ffn(ctx, "xr3", "y", "f2")
        ctx.barrier()

    with nc.Block() as block:
        @block.sync
        def _(eng):
            _replay_role(ctx, "sync", eng)

        @block.vector
        def _(eng):
            _replay_role(ctx, "vector", eng)

        @block.scalar
        def _(eng):
            _replay_role(ctx, "scalar", eng)

        @block.tensor
        def _(eng):
            _replay_role(ctx, "tensor", eng)

        @block.gpsimd
        def _(eng):
            _replay_role(ctx, "gpsimd", eng)
    return nc, ctx


def emit_qkv(ctx):
    cfg, r = ctx.cfg, ctx.rec
    D, S, TOWN = cfg.D, cfg.S, cfg.TOWN
    ve = ctx.sems["ve"]
    st = [ctx.sems["st0"], ctx.sems["st1"], ctx.sems["st2"], ctx.sems["st3"]]
    ob = [ctx.sb["pjo0"], ctx.sb["pjo1"], ctx.sb["pjo2"], ctx.sb["pjo3"]]
    M = ctx.marks

    def mk_post(dst_name):
        cnt = [0]

        def post(tt, oc_abs, pap, ow):
            b = cnt[0] % 4
            cnt[0] += 1
            r.vector.wait(ctx.sems["mm"])
            if (dst_name, "of", b) in M:
                r.vector.wait(*M[(dst_name, "of", b)])
            r.vector.tensor_scalar(ob[b][:, 0:ow], pap, -CLIP, CLIP,
                                   op0=ALU.max, op1=ALU.min) \
                .then_inc(ve, 1)
            ve.inc(1)
            pfree = (ve, ve.n)
            r.scalar.wait(ve)
            dst = ctx.dram[dst_name]
            r.scalar.dma(dst[tt * 128:(tt + 1) * 128, oc_abs:oc_abs + ow],
                         ob[b][:, 0:ow], st[b])
            M[(dst_name, "of", b)] = (st[b], st[b].n)
            return pfree
        return post

    emit_proj_tm(ctx, "k", "xnfm", S, "wqkvTb", D, D, 1024, mk_post("ktm"))
    ctx.barrier()
    emit_proj_tm(ctx, "v", "xnfm", S, "wqkvTb", 2 * D, D, 1024, mk_post("vtm"))
    ctx.barrier()
    emit_proj_tm(ctx, "q", "xnofm", TOWN, "wqkvTb", 0, D, 1024, mk_post("qtm"))


ROLE_ENGINES = ("sync", "vector", "scalar", "tensor")


def emit_attention(ctx):
    cfg, r = ctx.cfg, ctx.rec
    emit_attn_prep(ctx, host_constants(ctx.cfg))
    ctx.barrier()
    ctx.psum_free_all()
    for nm2 in ("atS0", "atS1", "atS2", "atO0", "atO1", "atLB"):
        ctx.apsum(nm2, [128, cfg.QB], F32)
    ctx.apsum("atL0", [1, cfg.QB], F32)
    ctx.apsum("atL1", [1, cfg.QB], F32)
    ld = ctx.sems["ld"]
    for o in range(cfg.QB // 128):
        r.sync.dma(ctx.sb[f"msk{o}"][:], ctx.dram[f"mask{o}"], ld)
    r.sync.dma(ctx.sb["iotasb"][:], ctx.dram["iota_kt"], ld)
    r.sync.dma(ctx.sb["posrsb"][0:1, 0:cfg.TOWN], ctx.dram["posrow"], ld)
    ldv = ld.n
    for e in ROLE_ENGINES:
        getattr(r, e).wait_ge(ld.h, ldv)

    cst = host_constants(cfg)
    for e in ROLE_ENGINES:
        getattr(r, e).IF_ROLE(0)
    snap = ctx.sem_snapshot()
    msnap = dict(ctx.marks)
    _emit_attn_role(ctx, 0, cst)
    end0 = ctx.sem_snapshot()
    saved_rec, saved_marks, snap_after0 = ctx.rec, ctx.marks, ctx.sem_snapshot()
    ctx.rec = Rec()
    ctx.marks = dict(msnap)
    ctx.sem_restore(snap)
    _emit_attn_role(ctx, 1, cst)
    end1t = ctx.sem_snapshot()
    ctx.rec, ctx.marks = saved_rec, saved_marks
    ctx.sem_restore(snap_after0)
    r = ctx.rec
    target = {k: max(end0.get(k, 0), end1t.get(k, 0)) for k in set(end0) | set(end1t)}
    _pad_to(ctx, target)
    for e in ROLE_ENGINES:
        getattr(r, e).ELSE_ROLE()
    ctx.sem_restore(snap)
    ctx.marks = dict(msnap)
    _emit_attn_role(ctx, 1, cst)
    _pad_to(ctx, target)
    for e in ROLE_ENGINES:
        getattr(r, e).ENDIF_ROLE()


def _pad_to(ctx, target):
    r = ctx.rec
    cur = ctx.sem_snapshot()
    for k, tv in target.items():
        a = cur.get(k, 0)
        if a == tv:
            continue
        assert a < tv and k in ("mm", "ve", "sc"), f"cannot pad sem {k}: {a} -> {tv}"
        sem = ctx.sems[k]
        r.vector.wait_ge(sem.h, a)
        for _ in range(tv - a):
            r.vector.nop().then_inc(sem, 1)
        sem.n = tv


def emit_attn_prep(ctx, cst):
    cfg, r = ctx.cfg, ctx.rec
    S, TOWN = cfg.S, cfg.TOWN
    ve, sc, mm = ctx.sems["ve"], ctx.sems["sc"], ctx.sems["mm"]
    ld = [ctx.sems["ld0"], ctx.sems["ld1"]]
    M = ctx.marks
    ctx.psum_free_all()
    pp = ctx.apsum("app", [1, 512], F32)
    pb = ctx.apsum("apb", [128, 64], F32)
    K2 = [ctx.sb["atK0"], ctx.sb["atK1"]]
    Q2 = [ctx.sb["atQ0"], ctx.sb["atQ1"]]
    sq = [ctx.sb["atsq0"], ctx.sb["atsq1"]]
    nrm = ctx.sb["atnrm"]
    km = ctx.sb["atkm"]; qm = ctx.sb["atqm"]
    onescol = ctx.sb["onescol"]; onesrowf = ctx.sb["onesrowf"]
    CW = 512
    for h in range(NH):
        b = h % 2
        if ("ap", b) in M:
            r.sync.wait(*M[("ap", b)])
        r.sync.dma(K2[b][:, 0:S], ctx.dram["kfm"][h * 128:(h + 1) * 128, :], ld[b])
        r.sync.dma(Q2[b][:, 0:TOWN], ctx.dram["qfm"][h * 128:(h + 1) * 128, :], ld[b])
        r.scalar.wait(ld[b])
        if ("apsq", b) in M:
            r.scalar.wait(*M[("apsq", b)])
        r.scalar.activation(sq[b][:, 0:S], K2[b][:, 0:S], AF.Square,
                            bias=0.0, scale=1.0)
        r.scalar.drain()
        r.scalar.activation(sq[b][:, S:S + TOWN], Q2[b][:, 0:TOWN], AF.Square,
                            bias=0.0, scale=1.0)
        r.scalar.drain().then_inc(sc, 1); sc.inc(1)
        r.tensor.wait(sc)
        for c in range((S + TOWN) // CW):
            if ("applb",) in M:
                r.tensor.wait(*M[("applb",)])
            r.tensor.matmul(pp[0:1, 0:CW], onescol[:, 0:1],
                            sq[b][:, c * CW:(c + 1) * CW],
                            start=True, stop=True).then_inc(mm, 1)
            mm.inc(1)
            r.vector.wait(mm)
            r.vector.tensor_copy(nrm[0:1, c * CW:(c + 1) * CW], pp[0:1, 0:CW]) \
                .then_inc(ve, 1)
            ve.inc(1)
            M[("applb",)] = (ve, ve.n)
        M[("apsq", b)] = (mm, mm.n)
        M[("ap", b)] = (mm, mm.n)
        r.vector.drain()
        r.vector.reduce_max(km[0:1, h:h + 1], nrm[0:1, 0:S], axis=AX.X)
        r.vector.drain()
        r.vector.reduce_max(qm[0:1, h:h + 1], nrm[0:1, S:S + TOWN], axis=AX.X)
        r.vector.drain()
    ii = r.vector.nop(); ii.then_inc(ve, 1); ve.inc(1)
    r.scalar.wait(ve)
    r.scalar.activation(km[0:1, 0:NH], km[0:1, 0:NH], AF.Ln, bias=0.0, scale=1.0)
    r.scalar.drain()
    r.scalar.activation(qm[0:1, 0:NH], qm[0:1, 0:NH], AF.Ln, bias=0.0, scale=1.0)
    r.scalar.drain().then_inc(sc, 1); sc.inc(1)
    r.vector.wait(sc)
    r.vector.tensor_add(km[0:1, 0:NH], km[0:1, 0:NH], qm[0:1, 0:NH])
    r.vector.drain().then_inc(ve, 1); ve.inc(1)
    r.scalar.wait(ve)
    r.scalar.activation(ctx.sb["atbo"][0:1, 0:NH], km[0:1, 0:NH], AF.Exp,
                        bias=0.0, scale=0.5)
    r.scalar.drain().then_inc(sc, 1); sc.inc(1)
    r.tensor.wait(sc)
    r.tensor.matmul(pb[:, 0:NH], onesrowf[0:1, :], ctx.sb["atbo"][0:1, 0:NH],
                    start=True, stop=True).then_inc(mm, 1)
    mm.inc(1)
    r.vector.wait(mm)
    r.vector.tensor_copy(ctx.sb["atbc"][:, 0:NH], pb[:, 0:NH])
    r.vector.drain().then_inc(ve, 1); ve.inc(1)


def _emit_attn_role(ctx, role, cst):
    cfg, r = ctx.cfg, ctx.rec
    D, S, TOWN, QB = cfg.D, cfg.S, cfg.TOWN, cfg.QB
    QBT = QB // 128
    qpos = cfg.QPOS[role]
    M = ctx.marks
    ve, sc, mm = ctx.sems["ve"], ctx.sems["sc"], ctx.sems["mm"]
    ld = [ctx.sems["ld0"], ctx.sems["ld1"], ctx.sems["ld2"]]
    st = [ctx.sems["st0"], ctx.sems["st1"]]
    sp = [ctx.ps["atS0"], ctx.ps["atS1"], ctx.ps["atS2"]]
    op = [ctx.ps["atO0"], ctx.ps["atO1"]]
    lp = [ctx.ps["atL0"], ctx.ps["atL1"]]
    lb = ctx.ps["atLB"]
    onesrow = ctx.sb["onesrow"]; onescol = ctx.sb["onescol"]
    onesrowf = ctx.sb["onesrowf"]
    Ksb = [ctx.sb["atK0"], ctx.sb["atK1"], ctx.sb["atK2"]]
    Vsb = [ctx.sb["atV0"], ctx.sb["atV1"], ctx.sb["atV2"]]
    Qsb = [ctx.sb["atQ0"], ctx.sb["atQ1"], ctx.sb["atQ2"]]
    sq = [ctx.sb["atsq0"], ctx.sb["atsq1"]]
    pT = [ctx.sb["atp0"], ctx.sb["atp1"], ctx.sb["atp2"]]
    sd = [ctx.sb["atsd0"], ctx.sb["atsd1"], ctx.sb["atsd2"]]
    negcb = [ctx.sb["atncb0"], ctx.sb["atncb1"], ctx.sb["atncb2"]]
    biast = [ctx.sb["atbt0"], ctx.sb["atbt1"], ctx.sb["atbt2"]]
    nrm = ctx.sb["atnrm"]
    lsb = [ctx.sb["atlsb0"], ctx.sb["atlsb1"]]
    lbc = ctx.sb["atlbc"]
    osb = [ctx.sb["atob0"], ctx.sb["atob1"]]
    skipk1 = [float(cst["slopes"][h]) * (QB - 1) <= 20.0 for h in range(NH)]

    def mini_prep(h, hb):
        slope = float(cst["slopes"][h])
        if (("at", hb, "ncfree") in M):
            r.vector.wait(*M[("at", hb, "ncfree")])
        r.vector.tensor_scalar(negcb[hb][0:1, 0:TOWN],
                               ctx.sb["posrsb"][0:1, 0:TOWN], -slope,
                               ctx.sb["atbo"][0:1, h:h + 1],
                               op0=ALU.mult, op1=ALU.subtract)
        r.vector.drain()
        if skipk1[h]:
            for bi in range(4):
                qbe = (qpos[bi] + 1) * QB - 1
                r.vector.tensor_scalar(
                    biast[hb][:, bi * cfg.ST:(bi + 1) * cfg.ST],
                    ctx.sb["iotasb"][:, 0:cfg.ST], slope,
                    ctx.sb["atbc"][:, h:h + 1],
                    op0=ALU.mult, op1=ALU.subtract)
                r.vector.drain()
                r.vector.tensor_scalar_add(
                    biast[hb][:, bi * cfg.ST:(bi + 1) * cfg.ST],
                    biast[hb][:, bi * cfg.ST:(bi + 1) * cfg.ST],
                    -slope * qbe)
                r.vector.drain()
        else:
            r.vector.tensor_scalar_mul(biast[hb][:, 0:cfg.ST],
                                       ctx.sb["iotasb"][:, 0:cfg.ST], slope)
            r.vector.drain()
        ii = r.vector.nop()
        ii.then_inc(ve, 1); ve.inc(1)
        return ve.n

    def head_loads(h, hb):
        if (("at", hb, "kfree") in M):
            r.sync.wait(*M[("at", hb, "kfree")])
        r.sync.dma(Ksb[hb][:, 0:S], ctx.dram["kfm"][h * 128:(h + 1) * 128, :],
                   ld[hb])
        for vg in range(8):
            vs0, vs1 = vg * (S // 8), (vg + 1) * (S // 8)
            r.sync.dma(Vsb[hb][:, vs0:vs1].rearrange("p (kt c) -> p kt c", c=128),
                       ctx.dram["vtm"][vs0:vs1, h * 128:(h + 1) * 128]
                       .rearrange("(kt p) c -> p kt c", p=128), ld[hb])
        r.sync.dma(Qsb[hb][:, 0:TOWN], ctx.dram["qfm"][h * 128:(h + 1) * 128, :],
                   ld[hb])
        return ld[hb].n

    hld = {0: head_loads(0, 0), 1: head_loads(1, 1)}
    prep_ve = {0: mini_prep(0, 0)}

    for h in range(NH):
        hb = h % 3
        if h + 2 < NH:
            hld[h + 2] = head_loads(h + 2, (h + 2) % 3)
        if h + 1 < NH:
            prep_ve[h + 1] = mini_prep(h + 1, (h + 1) % 3)
        r.tensor.wait(ve, prep_ve[h])
        r.tensor.wait(ld[hb], hld[h])
        kiter = 0
        pend = None
        slope_h = float(cst["slopes"][h])
        for bi in range(4):
            p = qpos[bi]
            nkt = (p + 1) * QBT
            ob2 = bi % 2
            qbs = p * QB
            kws = 0
            while (kws + 1) * 128 - 1 < qbs and \
                  slope_h * (qbs - ((kws + 1) * 128 - 1)) >= 250.0:
                kws += 1
            for kt in range(kws, nkt):
                s2 = kiter % 3
                if (("at", s2, "spfree") in M):
                    r.tensor.wait(*M[("at", s2, "spfree")])
                if (("at", s2, "ptfree") in M):
                    r.tensor.wait(*M[("at", s2, "ptfree")])
                if kt == kws and (("at", ob2, "opfree") in M):
                    r.tensor.wait(*M[("at", ob2, "opfree")])
                if skipk1[h]:
                    r.tensor.matmul(sp[s2][:, 0:QB],
                                    Ksb[hb][:, kt * 128:(kt + 1) * 128],
                                    Qsb[hb][:, bi * QB:(bi + 1) * QB],
                                    start=True, stop=True).then_inc(mm, 1)
                else:
                    r.tensor.matmul(sp[s2][:, 0:QB], onesrow[0:1, :],
                                    negcb[hb][0:1, bi * QB:(bi + 1) * QB],
                                    start=True, stop=False)
                    r.tensor.matmul(sp[s2][:, 0:QB],
                                    Ksb[hb][:, kt * 128:(kt + 1) * 128],
                                    Qsb[hb][:, bi * QB:(bi + 1) * QB],
                                    start=False, stop=True).then_inc(mm, 1)
                mm.inc(1)
                diag = kt - (nkt - QBT)
                bcol = biast[hb][:, (bi * cfg.ST if skipk1[h] else 0) + kt:
                                 (bi * cfg.ST if skipk1[h] else 0) + kt + 1]
                if diag >= 0:
                    r.vector.wait(mm)
                    if ("at", s2, "sdfree") in M:
                        r.vector.wait(*M[("at", s2, "sdfree")])
                    r.vector.tensor_sub(sd[s2][:, 0:QB], sp[s2][:, 0:QB],
                                        ctx.sb[f"msk{diag}"][:]).then_inc(ve, 1)
                    ve.inc(1)
                    M[("at", s2, "spfree")] = (ve, ve.n)
                    r.scalar.wait(ve)
                    r.scalar.activation(pT[s2][:, 0:QB], sd[s2][:, 0:QB],
                                        AF.Exp, bias=bcol, scale=1.0) \
                        .then_inc(sc, 1)
                    sc.inc(1)
                    M[("at", s2, "sdfree")] = (sc, sc.n)
                else:
                    r.scalar.wait(mm)
                    r.scalar.activation(pT[s2][:, 0:QB], sp[s2][:, 0:QB], AF.Exp,
                                        bias=bcol, scale=1.0).then_inc(sc, 1)
                    sc.inc(1)
                    M[("at", s2, "spfree")] = (sc, sc.n)
                expmark = sc.n
                if pend is not None:
                    _pv(ctx, r, pend, Vsb, onescol, op, lp, pT, mm, M)
                pend = (pend, hb, bi, kt, nkt, kiter % 3, expmark, kws)
                kiter += 1
            _pv(ctx, r, pend, Vsb, onescol, op, lp, pT, mm, M)
            pend = None
            lq = bi % 2
            r.vector.wait(mm)
            if ("at", lq, "lsfree") in M:
                r.vector.wait(*M[("at", lq, "lsfree")])
            r.vector.tensor_scalar_add(lsb[lq][0:1, 0:QB], lp[lq][0:1, 0:QB],
                                       1e-30)
            r.vector.drain()
            r.vector.reciprocal(lsb[lq][0:1, 0:QB], lsb[lq][0:1, 0:QB])
            r.vector.drain().then_inc(ve, 1); ve.inc(1)
            M[("at", lq, "lpfree")] = (ve, ve.n)
            r.tensor.wait(ve)
            if ("at", "lbfree") in M:
                r.tensor.wait(*M[("at", "lbfree")])
            r.tensor.matmul(lb[:, 0:QB], onesrowf[0:1, :], lsb[lq][0:1, 0:QB],
                            start=True, stop=True).then_inc(mm, 1)
            mm.inc(1)
            M[("at", lq, "lsfree")] = (mm, mm.n)
            ob = (h * 4 + bi) % 2
            r.vector.wait(mm)
            if (("at", ob, "obfree") in M):
                r.vector.wait(*M[("at", ob, "obfree")])
            r.vector.tensor_copy(lbc[:, 0:QB], lb[:, 0:QB])
            r.vector.drain()
            M[("at", "lbfree")] = (ve, ve.n + 1)
            r.vector.tensor_mul(osb[ob][:, 0:QB], op[ob2][:, 0:QB], lbc[:, 0:QB]) \
                .then_inc(ve, 1)
            ve.inc(1)
            M[("at", ob2, "opfree")] = (ve, ve.n)
            r.scalar.wait(ve)
            r.scalar.dma(ctx.dram["ofm"][h * 128:(h + 1) * 128,
                                         bi * QB:(bi + 1) * QB],
                         osb[ob][:, 0:QB], st[ob])
            M[("at", ob, "obfree")] = (st[ob], st[ob].n)
        M[("at", hb, "kfree")] = (mm, mm.n)
        M[("at", hb, "ncfree")] = (mm, mm.n)


def _pv(ctx, r, pend, Vsb, onescol, op, lp, pT, mm, M):
    _, hb, bi, kt, nkt, s2, expmark, kws = pend
    QB = ctx.cfg.QB
    r.tensor.wait(ctx.sems["sc"], expmark)
    if kt == kws:
        lq = bi % 2
        if ("at", lq, "lpfree") in M:
            r.tensor.wait(*M[("at", lq, "lpfree")])
    r.tensor.matmul(op[bi % 2][:, 0:QB], Vsb[hb][:, kt * 128:(kt + 1) * 128],
                    pT[s2][:, 0:QB], start=(kt == kws), stop=(kt == nkt - 1))
    lmm = r.tensor.matmul(lp[bi % 2][0:1, 0:QB], onescol[:, 0:1],
                          pT[s2][:, 0:QB], start=(kt == kws), stop=(kt == nkt - 1))
    lmm.then_inc(mm, 1)
    mm.inc(1)
    M[("at", s2, "ptfree")] = (mm, mm.n)



def mk_post_residual(ctx, res_name, dst_name, add_bb=False):
    r = ctx.rec
    ve = ctx.sems["ve"]
    st = [ctx.sems["st0"], ctx.sems["st1"], ctx.sems["st2"], ctx.sems["st3"]]
    rl = [ctx.sems["rl0"], ctx.sems["rl1"], ctx.sems["rl2"], ctx.sems["rl3"]]
    resb = [ctx.sb["pjr0"], ctx.sb["pjr1"], ctx.sb["pjr2"], ctx.sb["pjr3"]]
    outb = [ctx.sb["pjq0"], ctx.sb["pjq1"], ctx.sb["pjq2"], ctx.sb["pjq3"]]
    M = ctx.marks
    cnt = [0]

    def post(tt, oc_abs, pap, ow):
        b = cnt[0] % 4
        cnt[0] += 1
        if ("pr", res_name, b) in M:
            r.sync.wait(*M[("pr", res_name, b)])
        r.sync.dma(resb[b][:, 0:ow],
                   ctx.dram[res_name][tt * 128:(tt + 1) * 128,
                                      oc_abs:oc_abs + ow], rl[b])
        r.vector.wait(rl[b])
        r.vector.wait(ctx.sems["mm"])
        if ("po", dst_name, b) in M:
            r.vector.wait(*M[("po", dst_name, b)])
        r.vector.tensor_add(outb[b][:, 0:ow], pap, resb[b][:, 0:ow])
        if add_bb:
            r.vector.drain()
            r.vector.tensor_add(outb[b][:, 0:ow], outb[b][:, 0:ow],
                                ctx.sb["bbb"][:, oc_abs:oc_abs + ow])
        ir = r.vector.drain()
        ir.then_inc(ve, 1); ve.inc(1)
        pfree = (ve, ve.n)
        M[("pr", res_name, b)] = (ve, ve.n)
        r.scalar.wait(ve)
        r.scalar.dma(ctx.dram[dst_name][tt * 128:(tt + 1) * 128,
                                        oc_abs:oc_abs + ow],
                     outb[b][:, 0:ow], st[b])
        M[("po", dst_name, b)] = (st[b], st[b].n)
        return pfree
    return post


def emit_wout(ctx):
    cfg = ctx.cfg
    emit_proj_tm(ctx, "wo", "ofm", cfg.TOWN, "woutTb", 0, cfg.D, 1024,
                 mk_post_residual(ctx, "x_own", "xr1"))


def emit_ffn(ctx, xin, xout, tagn):
    cfg, r = ctx.cfg, ctx.rec
    D, TOWN, OT, DT = cfg.D, cfg.TOWN, cfg.OT, cfg.DT
    ve, sc, mm = ctx.sems["ve"], ctx.sems["sc"], ctx.sems["mm"]
    ld = [ctx.sems["ld0"], ctx.sems["ld1"]]
    M = ctx.marks

    emit_ln_transpose(ctx, xin, "xnofm", OT, "ln")
    ctx.barrier()
    ctx.psum_free_all()

    bbp = ctx.apsum("fbb", [128, 512], F32)
    r.sync.dma(ctx.sb["rbf"][0:1, 0:D], ctx.dram["bbase"], ctx.sems["ld"])
    for e in ENGINES:
        getattr(r, e).wait_ge(ctx.sems["ld"].h, ctx.sems["ld"].n)
    for c0 in range(0, D, 512):
        r.tensor.wait(ve)
        r.tensor.matmul(bbp[:, :], ctx.sb["onesrowf"][0:1, :],
                        ctx.sb["rbf"][0:1, c0:c0 + 512],
                        start=True, stop=True).then_inc(mm, 1)
        mm.inc(1)
        r.vector.wait(mm)
        r.vector.tensor_copy(ctx.sb["bbb"][:, c0:c0 + 512], bbp[:, :]) \
            .then_inc(ve, 1)
        ve.inc(1)
    r.sync.dma(ctx.sb["flb"][:], ctx.dram["lbCb"], ctx.sems["ld"])
    r.sync.dma(ctx.sb["fla"][:].rearrange("p (j c) -> p j c", c=128),
               ctx.dram["laTb"][:, :].rearrange("(j p) c -> p j c", p=128),
               ctx.sems["ld"])
    r.sync.dma(ctx.sb["fwr"][:].rearrange("p (j c) -> p j c", c=8),
               ctx.dram["wrTb"][:, :].rearrange("(j p) c -> p j c", p=128),
               ctx.sems["ld"])
    ldw = ctx.sems["ld"].n
    ctx.barrier()

    rtp = ctx.apsum("frt", [128, 8], F32)
    hp = ctx.apsum("fh", [128, 128], F32)
    gpp = ctx.apsum("fgp", [128, 128], BF16)
    rts = ctx.sb["frts"]; e8 = ctx.sb["fe8"]; p8 = ctx.sb["fp8"]
    w8 = ctx.sb["fw8"]; hs = ctx.sb["fhs"]; gb = ctx.sb["fgb"]
    gfm = ctx.sb["fgfm"]
    xnb = [ctx.sb["pjx0"], ctx.sb["pjx1"]]
    for tc in range(TOWN // 512):
        b = tc % 2
        if ("fx", b) in M:
            r.sync.wait(*M[("fx", b)])
        for jg in range(0, DT, 2):
            nj = min(2, DT - jg)
            dap = ctx.dram["xnofm"][jg * 128:(jg + nj) * 128,
                                    tc * 512:(tc + 1) * 512] \
                .rearrange("(j p) c -> p j c", p=128)
            sap = xnb[b][:, jg * 512:(jg + nj) * 512] \
                .rearrange("p (j c) -> p j c", c=512)
            r.sync.dma(sap, dap, ld[b])
        xm = ld[b].n
        for tti in range(4):
            tt = tc * 4 + tti
            r.tensor.wait(ld[b], xm)
            if tt == 0:
                r.tensor.wait(ctx.sems["ld"], ldw)
            if ("fpf",) in M:
                r.tensor.wait(*M[("fpf",)])
            for j in range(DT):
                lhs = xnb[b][:, j * 512 + tti * 128: j * 512 + (tti + 1) * 128]
                r.tensor.matmul(rtp[:, :], lhs, ctx.sb["fwr"][:, j * 8:(j + 1) * 8],
                                start=(j == 0), stop=(j == DT - 1))
                mmi = r.tensor.matmul(hp[:, :], lhs,
                                      ctx.sb["fla"][:, j * 128:(j + 1) * 128],
                                      start=(j == 0), stop=(j == DT - 1))
            mmi.then_inc(mm, 1)
            mm.inc(1)
            r.vector.wait(mm)
            r.vector.tensor_copy(rts[:, 0:8], rtp[:, :])
            r.vector.drain()
            r.vector.reduce_max(rts[:, 8:9], rts[:, 0:8], axis=AX.X, negate=True)
            r.vector.drain().then_inc(ve, 1); ve.inc(1)
            r.scalar.wait(ve)
            r.scalar.activation(e8[:, 0:8], rts[:, 0:8], AF.Exp,
                                bias=rts[:, 8:9], scale=1.0)
            r.scalar.drain().then_inc(sc, 1); sc.inc(1)
            r.vector.wait(sc)
            r.vector.max(p8[:, 0:8], e8[:, 0:8])
            r.vector.drain()
            r.vector.tensor_scalar(p8[:, 8:9], p8[:, 0:1], p8[:, 1:2], None,
                                   op0=ALU.add)
            r.vector.drain()
            r.vector.reciprocal(p8[:, 9:10], p8[:, 8:9])
            r.vector.drain()
            r.vector.tensor_scalar(w8[:, 0:8], e8[:, 0:8], p8[:, 1:2], None,
                                   op0=ALU.is_ge)
            r.vector.drain()
            r.vector.tensor_mul(w8[:, 0:8], w8[:, 0:8], e8[:, 0:8])
            r.vector.drain()
            r.vector.tensor_scalar(w8[:, 0:8], w8[:, 0:8], p8[:, 9:10], 2.0,
                                   op0=ALU.mult, op1=ALU.mult)
            r.vector.drain()
            r.vector.tensor_copy(hs[:, :], hp[:, :])
            r.vector.drain().then_inc(ve, 1); ve.inc(1)
            M[("fpf",)] = (ve, ve.n)
            r.scalar.wait(ve)
            for e in range(8):
                r.scalar.activation(gb[:, e * 16:(e + 1) * 16],
                                    hs[:, e * 16:(e + 1) * 16], AF.Copy,
                                    bias=0.0, scale=w8[:, e:e + 1])
            r.scalar.drain().then_inc(sc, 1); sc.inc(1)
            r.tensor.wait(sc)
            r.tensor.transpose(gpp[:, :], gb[:, :], ctx.sb["ident"][:]) \
                .then_inc(mm, 1)
            mm.inc(1)
            r.vector.wait(mm)
            r.vector.tensor_copy(gfm[:, tt * 128:(tt + 1) * 128], gpp[:, :]) \
                .then_inc(ve, 1)
            ve.inc(1)
            M[("fgdone", tt)] = (ve, ve.n)
        M[("fx", b)] = (mm, mm.n)
    ctx.barrier()
    ctx.psum_free_all()

    for k in range(4):
        ctx.apsum(f"pj{k}", [128, 512], F32)

    def extra(tt, oc_abs, pap, ow):
        return r.tensor.matmul(pap, gfm[:, tt * 128:(tt + 1) * 128],
                               ctx.sb["flb"][:, oc_abs:oc_abs + ow],
                               start=False, stop=True)

    emit_proj_tm(ctx, tagn, "xnofm", TOWN, "wbTb", 0, D, 1024,
                 mk_post_residual(ctx, xin, xout, add_bb=True),
                 extra_mm=extra)


def emit_attnb(ctx):
    cfg, r = ctx.cfg, ctx.rec
    D, TOWN, OT, DT, KT = cfg.D, cfg.TOWN, cfg.OT, cfg.DT, cfg.KT
    ve, sc, mm = ctx.sems["ve"], ctx.sems["sc"], ctx.sems["mm"]
    ld = [ctx.sems["ld0"], ctx.sems["ld1"]]
    st = [ctx.sems["st0"], ctx.sems["st1"]]
    M = ctx.marks

    emit_ln_transpose(ctx, "xr2", "vfm", OT, "ln", extra_out="vdbl")
    ctx.barrier()
    ctx.psum_free_all()

    rep = ctx.apsum("abRe", [128, TOWN], F32)
    imp = ctx.apsum("abIm", [128, TOWN], F32)
    cs = [ctx.sb["abc0"], ctx.sb["abc1"]]
    vres = ctx.sb["abvr"]
    sq1 = ctx.sb["absq1"]; sq2 = ctx.sb["absq2"]
    psb = [ctx.sb["abps0"], ctx.sb["abps1"]]
    for jg in range(0, DT, 2):
        nj = min(2, DT - jg)
        r.sync.dma(vres[:, jg * TOWN:(jg + nj) * TOWN]
                   .rearrange("p (j c) -> p j c", c=TOWN),
                   ctx.dram["vfm"][jg * 128:(jg + nj) * 128, :]
                   .rearrange("(j p) c -> p j c", p=128), ctx.sems["ld"])
    vl = ctx.sems["ld"].n
    r.tensor.wait_ge(ctx.sems["ld"].h, vl)
    for kt in range(KT):
        for j in range(DT):
            b = j % 2
            if ("abcs2", b) in M:
                r.sync.wait(*M[("abcs2", b)])
            r.sync.dma(cs[b][:, 0:128],
                       ctx.dram["cosFb"][j * 128:(j + 1) * 128,
                                         kt * 128:(kt + 1) * 128], ld[b])
            r.sync.dma(cs[b][:, 128:256],
                       ctx.dram["sinFb"][j * 128:(j + 1) * 128,
                                         kt * 128:(kt + 1) * 128], ld[b])
            r.tensor.wait(ld[b])
            if j <= 1 and ("abpf",) in M:
                r.tensor.wait(*M[("abpf",)])
            vt_j = vres[:, j * TOWN:(j + 1) * TOWN]
            for c0 in range(0, TOWN, 512):
                r.tensor.matmul(rep[:, c0:c0 + 512], cs[b][:, 0:128],
                                vt_j[:, c0:c0 + 512],
                                start=(j == 0), stop=(j == DT - 1))
            for c0 in range(0, TOWN, 512):
                mmi = r.tensor.matmul(imp[:, c0:c0 + 512], cs[b][:, 128:256],
                                      vt_j[:, c0:c0 + 512],
                                      start=(j == 0), stop=(j == DT - 1))
            mmi.then_inc(mm, 1)
            mm.inc(1)
            M[("abcs2", b)] = (mm, mm.n)
        r.scalar.wait(mm)
        if ("absqf",) in M:
            r.scalar.wait(*M[("absqf",)])
        r.scalar.activation(sq1[:], rep[:, :], AF.Square, bias=0.0, scale=1.0)
        r.scalar.drain()
        r.scalar.activation(sq2[:], imp[:, :], AF.Square, bias=0.0, scale=1.0)
        r.scalar.drain().then_inc(sc, 1); sc.inc(1)
        M[("abpf",)] = (sc, sc.n)
        pb_ = kt % 2
        r.vector.wait(sc)
        if ("abpsb", pb_) in M:
            r.vector.wait(*M[("abpsb", pb_)])
        r.vector.tensor_add(psb[pb_][:], sq1[:], sq2[:]).then_inc(ve, 1)
        ve.inc(1)
        M[("absqf",)] = (ve, ve.n)
        r.scalar.wait(ve)
        r.scalar.dma(ctx.dram["Ph"][kt * 128:(kt + 1) * 128, :], psb[pb_][:],
                     st[pb_])
        M[("abpsb", pb_)] = (st[pb_], st[pb_].n)
    ctx.barrier()
    ctx.psum_free_all()

    cp = ctx.apsum("abC", [128, 2048 if D >= 4096 else D], F32)
    inv = ctx.sb["abinv"]
    csb = ctx.sb["abcs"]
    pt = [ctx.sb["abpt0"], ctx.sb["abpt1"]]
    half = min(2048, D)
    for hf in range(D // half):
        if ("abinvfree",) in M:
            r.sync.wait(*M[("abinvfree",)])
        for jg in range(0, KT, 2):
            nj = min(2, KT - jg)
            dap = ctx.dram["invTb"][jg * 128:(jg + nj) * 128,
                                    hf * half:(hf + 1) * half] \
                .rearrange("(j p) c -> p j c", p=128)
            sap = inv[:, jg * half:(jg + nj) * half] \
                .rearrange("p (j c) -> p j c", c=half)
            r.sync.dma(sap, dap, ctx.sems["ld"])
        lv = ctx.sems["ld"].n
        r.tensor.wait_ge(ctx.sems["ld"].h, lv)
        for tt in range(OT):
            if ("abcf",) in M:
                r.tensor.wait(*M[("abcf",)])
            for kt in range(KT):
                ptb = (tt * KT + kt) % 2
                if ("abptf", ptb) in M:
                    r.sync.wait(*M[("abptf", ptb)])
                r.sync.dma(pt[ptb][:],
                           ctx.dram["Ph"][kt * 128:(kt + 1) * 128,
                                          tt * 128:(tt + 1) * 128], ld[ptb])
                r.tensor.wait(ld[ptb])
                for c0 in range(0, half, 512):
                    mmi = r.tensor.matmul(cp[:, c0:c0 + 512], pt[ptb][:],
                                          inv[:, kt * half + c0: kt * half + c0 + 512],
                                          start=(kt == 0), stop=(kt == KT - 1))
                mmi.then_inc(mm, 1)
                mm.inc(1)
                M[("abptf", ptb)] = (mm, mm.n)
            cb = tt % 2
            r.scalar.wait(mm)
            if ("abcsf", cb) in M:
                r.scalar.wait(*M[("abcsf", cb)])
            r.scalar.activation(csb[:, cb * half:(cb + 1) * half],
                                cp[:, 0:half], AF.Copy, bias=0.0, scale=1.0)
            r.scalar.drain().then_inc(sc, 1); sc.inc(1)
            M[("abcf",)] = (sc, sc.n)
            r.scalar.dma(ctx.dram["corrh"][tt * 128:(tt + 1) * 128,
                                           hf * half:(hf + 1) * half],
                         csb[:, cb * half:(cb + 1) * half], st[cb])
            M[("abcsf", cb)] = (st[cb], st[cb].n)
        M[("abinvfree",)] = (mm, mm.n)
    ctx.barrier()
    ctx.psum_free_all()

    r.sync.dma(ctx.sb["abrb"][:], ctx.dram["rowbase"], ctx.sems["ld"])
    for e in ("vector",):
        getattr(r, e).wait_ge(ctx.sems["ld"].h, ctx.sems["ld"].n)
    ct = [ctx.sb["abct0"], ctx.sb["abct1"]]
    t8 = ctx.sb["abt8"]; i8 = ctx.sb["abi8"]; w8r = ctx.sb["abw8"]
    off = ctx.sb["aboff"]
    acc = ctx.sb["abacc"]
    gt = [ctx.sb["abg0"], ctx.sb["abg1"]]
    gld = ctx.sems["gld"]
    for tt in range(OT):
        b = tt % 2
        if ("abct", b) in M:
            r.sync.wait(*M[("abct", b)])
        r.sync.dma(ct[b][:], ctx.dram["corrh"][tt * 128:(tt + 1) * 128, :], ld[b])
        if ("abacc",) in M:
            r.sync.wait(*M[("abacc",)])
        r.sync.dma(acc[:], ctx.dram["xr2"][tt * 128:(tt + 1) * 128, :], ld[b])
        r.vector.wait(ld[b])
        if ("abacc",) in M:
            r.vector.wait(*M[("abacc",)])
        r.vector.max(t8[:, 0:8], ct[b][:])
        r.vector.drain()
        r.vector.max_index(i8[:, 0:8], t8[:, 0:8], ct[b][:])
        r.vector.drain()
        r.vector.tensor_scalar_mul(t8[:, 8:9], t8[:, 0:1], -1.0)
        r.vector.drain().then_inc(ve, 1); ve.inc(1)
        M[("abct", b)] = (ve, ve.n)
        r.scalar.wait(ve)
        r.scalar.activation(t8[:, 16:24], t8[:, 0:8], AF.Exp,
                            bias=t8[:, 8:9], scale=1.0)
        r.scalar.drain().then_inc(sc, 1); sc.inc(1)
        r.vector.wait(sc)
        r.vector.reduce_sum(t8[:, 9:10], t8[:, 16:24], axis=AX.X)
        r.vector.drain()
        r.vector.reciprocal(t8[:, 10:11], t8[:, 9:10])
        r.vector.drain()
        r.vector.tensor_scalar(w8r[:, 0:8], t8[:, 16:24], t8[:, 10:11], None,
                               op0=ALU.mult)
        r.vector.drain()
        r.vector.tensor_copy(ctx.sb["abof"][:, 0:8], i8[:, 0:8])
        r.vector.drain()
        r.vector.tensor_scalar(ctx.sb["abof"][:, 0:8], ctx.sb["abof"][:, 0:8],
                               ctx.sb["abrb"][:, tt:tt + 1], None, op0=ALU.add)
        r.vector.drain()
        r.vector.tensor_copy(off[:, 0:8], ctx.sb["abof"][:, 0:8])
        r.vector.drain().then_inc(ve, 1); ve.inc(1)
        for k in range(8):
            gb_ = k % 2
            r.gpsimd.wait(ve)
            if ("abg", gb_) in M:
                r.gpsimd.wait(*M[("abg", gb_)])
            r.gpsimd.indirect_dma_start(
                gt[gb_][:], None, ctx.dram["vdbl"],
                IndirectOffsetOnAxis(off[:, k:k + 1], 1),
            ).then_inc(gld, 16)
            gld.inc(16)
            r.vector.wait(gld)
            r.vector.scalar_tensor_tensor(acc[:], gt[gb_][:], w8r[:, k:k + 1],
                                          acc[:], op0=ALU.mult, op1=ALU.add)
            r.vector.drain().then_inc(ve, 1); ve.inc(1)
            M[("abg", gb_)] = (ve, ve.n)
        M[("abacc",)] = (ve, ve.n)
        r.scalar.wait(ve)
        r.scalar.dma(ctx.dram["xr3"][tt * 128:(tt + 1) * 128, :], acc[:], st[b])
        M[("abacc",)] = (st[b], st[b].n)


def _replay_role(ctx, name, eng):
    needs_role = any(r.meth == "IF_ROLE" for r in ctx.rec.q[name])
    if needs_role:
        with eng.register(f"role_{name}") as rr:
            def loader():
                eng.wait_ge(ctx.sems["ld"].h, 16)
                eng.reg_load(rr, ctx.sb["rolesb"][0:1, 0:1])
                return rr
            replay(ctx.rec, name, eng, loader)
    else:
        replay(ctx.rec, name, eng, None)


def kernel(**inputs):
    cfg = CFG
    nc, ctx = build_program(cfg)
    in_maps = [host_core_inputs(cfg, inputs, c) for c in range(8)]
    res = run_bass_kernel_spmd(nc, in_maps, core_ids=list(range(8)))
    x = np.asarray(inputs["L0_attn_a_hidden_states"])
    out = np.empty_like(x, dtype=np.float32)
    for c in range(8):
        s, role = c // 2, c % 2
        yc = np.asarray(res.results[c]["y"])
        for bi, p in enumerate(cfg.QPOS[role]):
            out[s, p * cfg.QB:(p + 1) * cfg.QB] = yc[bi * cfg.QB:(bi + 1) * cfg.QB]
    return out
